# version 04
# baseline (speedup 1.0000x reference)
import numpy as np
import concourse.bass as bass
import concourse.mybir as mybir
from concourse.bass_utils import run_bass_kernel_spmd

F32 = mybir.dt.float32
BF16 = mybir.dt.bfloat16
AF = mybir.ActivationFunctionType
ALU = mybir.AluOpType

S = 2048
D = 1024
NCH = 8
NQB = 4
QB = 512
NTT = 16
NH = 16
EPS = 1e-5

ENGINES = ("sync", "tensor", "scalar", "vector", "gpsimd")

_AUDIT = False
_AUDIT_ENGINES = ()


class Prog:

    def __init__(self, nc):
        self.nc = nc
        self.q = {e: [] for e in ENGINES}
        self.chans = {}

    def _chan(self, name):
        if name not in self.chans:
            self.chans[name] = [self.nc.alloc_semaphore(f"c_{name}"), 0]
        return self.chans[name]

    ENGCH = {"tensor": "T", "scalar": "S", "vector": "V", "gpsimd": "G"}

    def _mkwaits(self, wait):
        wl = []
        for w in wait:
            if isinstance(w, tuple) and len(w) == 3:
                _, src, v = w
                s = self._chan(src[1:] if src.startswith("#") else self.ENGCH[src])
                if v > 0:
                    wl.append((s[0], v))
                continue
            if isinstance(w, tuple):
                src, lag = w
            else:
                src, lag = w, 0
            name = src[1:] if src.startswith("#") else self.ENGCH[src]
            s = self._chan(name)
            v = s[1] - lag
            if v > 0:
                wl.append((s[0], v))
        return wl

    def op(self, eng, fn, wait=(), sig=None):
        wl = self._mkwaits(wait)
        if eng in ("gpsimd", "vector", "scalar") or (_AUDIT and eng != "sync") or eng in _AUDIT_ENGINES:
            s = self._chan(self.ENGCH[eng])
            if s[1] > 0:
                wl.append((s[0], s[1]))
            s[1] += 1
            sigs = [(s[0], 1)]
        elif sig is not None:
            s = self._chan(self.ENGCH[eng])
            s[1] += 1
            sigs = [(s[0], 1)]
        else:
            sigs = []

        def run(e, fn=fn, wl=wl, sigs=sigs):
            for h, v in wl:
                e.wait_ge(h, v)
            ins = fn(e)
            if sigs:
                ins.then_inc(sigs[0][0], sigs[0][1])

        self.q[eng].append(run)

    def count(self, name):
        return self._chan(name[1:] if name.startswith("#") else self.ENGCH[name])[1]

    def wait(self, eng, srcs):
        wl = self._mkwaits([srcs] if isinstance(srcs, (str, tuple)) else srcs)
        if wl:
            def run(e, wl=wl):
                for h, v in wl:
                    e.wait_ge(h, v)
            self.q[eng].append(run)

    def dma(self, out_ap, in_ap, chan, wait=()):
        s = self._chan(chan)
        s[1] += 16
        wl = self._mkwaits(wait)

        def run(e, wl=wl, h=s[0]):
            for hh, v in wl:
                e.wait_ge(hh, v)
            e.dma_start(out=out_ap, in_=in_ap).then_inc(h, 16)

        self.q["sync"].append(run)

    def barrier(self, engines=("tensor", "scalar", "vector", "gpsimd")):
        fence = {
            "tensor": lambda e: e.ldweights(self.fence_ap),
            "scalar": lambda e: e.copy(self.fence_sc[0:1, 1:2], self.fence_src),
            "vector": lambda e: e.memset(self.fence_sc[0:1, 2:3], 0.0),
            "gpsimd": lambda e: e.memset(self.fence_sc[0:1, 3:4], 0.0),
        }
        for a in engines:
            self.op(a, fence[a], wait=(["#G"] if a in ("tensor", "scalar") else []), sig=True)
        for a in engines:
            self.wait(a, [b for b in engines if b != a])


def ceil32(x):
    return (x + 31) & ~31


def host_prepare(inputs):
    g = {}
    for k, v in inputs.items():
        a = np.asarray(v)
        if a.dtype != np.float32:
            a = a.astype(np.float32)
        g[k] = a
    x = g["L0_attn_x"]
    out = {}
    for t in ("q", "k", "v"):
        w1 = g[f"L0_ffn_{t}_w1"]
        w1f = np.ascontiguousarray(w1.transpose(1, 0, 2).reshape(D, D))
        A = w1f.reshape(NCH, 128, NCH, 128)
        out[f"{t}w1"] = np.ascontiguousarray(A.transpose(2, 1, 0, 3).reshape(NCH, 128, D))
        w2f = g[f"L0_ffn_{t}_w2"].reshape(D, D)
        if t == "v":
            out["vw2"] = np.ascontiguousarray(w2f.reshape(NCH, 128, D))
        else:
            C = w2f.reshape(NCH, 128, NCH, 128)
            out[f"{t}w2"] = np.ascontiguousarray(C.transpose(2, 1, 0, 3).reshape(NCH, 128, D))
        out[f"{t}b1"] = np.ascontiguousarray(g[f"L0_ffn_{t}_b1"].reshape(NCH, 128).T)
        out[f"{t}b2"] = np.ascontiguousarray(g[f"L0_ffn_{t}_b2"])
    npk = np.zeros((128, 408), np.float32)
    for ti, t in enumerate(("q", "k", "v")):
        nwT = g[f"L0_ffn_{t}_node_w"].T.reshape(NCH, 128, 15)
        for c in range(NCH):
            npk[:, c * 45 + ti * 15:c * 45 + ti * 15 + 15] = nwT[c]
        npk[0, 360 + ti * 15:360 + ti * 15 + 15] = g[f"L0_ffn_{t}_node_b"]
    out["npk"] = npk
    qwf = np.ascontiguousarray(g["L0_attn_Q_w"].transpose(1, 0, 2).reshape(D, D))
    kwf = np.ascontiguousarray(g["L0_attn_K_w"].transpose(1, 0, 2).reshape(D, D))
    vwf = np.ascontiguousarray(g["L0_attn_V_w"].transpose(1, 0, 2).reshape(D, D))
    owf = np.ascontiguousarray(g["L0_attn_O_w"].reshape(D, D))
    for nm, wf in (("qw", qwf), ("kw", kwf), ("ow", owf)):
        Cw = wf.reshape(NCH, 128, NCH, 128)
        out[nm] = np.ascontiguousarray(Cw.transpose(2, 1, 0, 3).reshape(NCH, 128, D))
    out["vw"] = np.ascontiguousarray(vwf.reshape(NCH, 128, D))
    out["qb8"] = np.ascontiguousarray(g["L0_attn_Q_b"].reshape(D).reshape(NCH, 128).T)
    out["kb8"] = np.ascontiguousarray(g["L0_attn_K_b"].reshape(D).reshape(NCH, 128).T)
    out["ob8"] = np.ascontiguousarray(g["L0_attn_O_b"].reshape(NCH, 128).T)
    out["vbrow"] = np.ascontiguousarray(g["L0_attn_V_b"].reshape(1, D))
    per_core = []
    for b in range(8):
        m = dict(out)
        m["x"] = np.ascontiguousarray(x[b])
        per_core.append(m)
    return per_core


def build(upto=None, dbg=()):
    nc = bass.Bass("TRN2", target_bir_lowering=False, dynamic_dma_scratch_size=64,
                   detect_race_conditions=_AUDIT)
    P = Prog(nc)

    dr = {}
    dr["x"] = nc.dram_tensor("x", [S, D], F32, kind="ExternalInput").ap()
    for t in ("q", "k", "v"):
        for suf, shape in (("w1", [NCH, 128, D]), ("w2", [NCH, 128, D]),
                           ("b1", [128, 8]), ("b2", [16, D])):
            dr[f"{t}{suf}"] = nc.dram_tensor(f"{t}{suf}", shape, F32, kind="ExternalInput").ap()
    dr["npk"] = nc.dram_tensor("npk", [128, 408], F32, kind="ExternalInput").ap()
    for nm in ("qw", "kw", "vw", "ow"):
        dr[nm] = nc.dram_tensor(nm, [NCH, 128, D], F32, kind="ExternalInput").ap()
    for nm in ("qb8", "kb8", "ob8"):
        dr[nm] = nc.dram_tensor(nm, [128, 8], F32, kind="ExternalInput").ap()
    dr["vbrow"] = nc.dram_tensor("vbrow", [1, D], F32, kind="ExternalInput").ap()
    y = nc.dram_tensor("y", [S, D], F32, kind="ExternalOutput").ap()

    SBYTES = (nc.SBUF_PARTITION_SIZE_BYTES or 196608) - 32
    off = 512

    def at(name, shape, dtype, o=None):
        nonlocal off
        fixed = o is not None
        if o is None:
            o = off
        tb = {F32: 4, BF16: 2}[dtype]
        nbytes = int(np.prod(shape[1:])) * tb
        h = nc.alloc_sbuf_tensor_at(name, shape, dtype, offset=o)
        if not fixed:
            off = ceil32(o + nbytes)
        return h

    X = at("X", [128, NCH * S], F32)
    XLN = at("XLN", [128, NCH * S], BF16)
    KF = at("KF", [128, NCH * S], BF16)
    VT = at("VT", [128, NTT * 1040], BF16)
    SCRO = off
    PBUF = at("PBUF", [128, 8192], BF16, SCRO)
    SB = at("SB", [128, 2048], F32, SCRO)
    SFM = at("SFM", [128, 2048], F32, SCRO)
    ATQ = at("ATQ", [128, 4096], BF16, SCRO)
    VWA = at("VWA", [128, 4608], BF16, SCRO)
    XSTG0 = at("XSTG0", [128, 1024], F32, SCRO)
    XSTG1 = at("XSTG1", [128, 1024], F32, SCRO + 4096)
    MSB = at("MSB", [128, 2048], F32, SCRO + 8192)
    SALL = at("SALL", [128, 720], F32, SCRO + 8192)
    PRING = at("PRING", [128, 2048], BF16, SCRO + 8192)
    MUXT = at("MUXT", [128, 192], F32, SCRO + 12288)
    QSTG = at("QSTG", [128, 4096], BF16, SCRO + 12288)
    RROWM = at("RROWM", [1, 512], BF16, SCRO + 13312)
    RBM = at("RBM", [64, 512], F32, SCRO + 14336)
    LEAFB = at("LEAFB", [128, 1024], BF16, SCRO + 16384)
    B2BF = at("B2BF", [16, 1024], BF16, SCRO + 16384)
    SCRN = at("SCRN", [128, 1024], F32, SCRO + 16384)
    RBF = at("RBF", [128, 512], F32, SCRO + 16384)
    E16H = at("E16H", [16, 1024], BF16, SCRO + 18432)
    MSK = at("MSK", [128, 1024], BF16, SCRO + 18432)
    LEAFT = at("LEAFT", [8, 128], BF16, SCRO + 18432)
    TMPF = at("TMPF", [128, 512], F32, SCRO + 18432)
    RROWF = at("RROWF", [1, 512], BF16, SCRO + 18432)
    off = ceil32(SCRO + 20480)
    WS32 = at("WS32", [128, 1024], F32)
    OSTG = at("OSTG", [128, 256], F32, off - 4096)
    WB16 = at("WB16", [128, 2048], BF16)
    ONES1 = at("ONES1", [1, 512], BF16)
    ONESC = at("ONESC", [128, 128], BF16)
    I128B = at("I128B", [128, 128], BF16)
    I128F = at("I128F", [128, 128], F32)
    TRI = at("TRI", [128, 128], BF16)
    IOTA16 = at("IOTA16", [128, 16], F32)
    C2 = at("C2", [128, 8], F32)
    BO16 = at("BO16", [128, 128], BF16)
    B1T = {t: at(f"B1{t}", [128, 8], F32) for t in ("q", "k", "v")}
    QB8 = at("QB8", [128, 8], F32)
    QB8S = at("QB8S", [128, 8], F32)
    KB8 = at("KB8", [128, 8], F32)
    OB8 = at("OB8", [128, 8], F32)
    LTF = at("LTF", [128, 16], F32)
    LT16 = {t: at(f"LT16{t}", [128, 16], BF16) for t in ("q", "k", "v")}
    LTFS = {t: at(f"LTFS{t}", [128, 16], F32) for t in ("q", "k", "v")}
    STAT = at("STAT", [128, 16], F32)
    BN = at("BN", [128, 1], F32)
    BH = at("BH", [128, 16], F32)
    BH16 = at("BH16", [16, 1], BF16)
    ETOK = at("ETOK", [128, 16], BF16)
    ETOK1 = at("ETOK1", [1, 16], BF16)
    GT1 = at("GT1", [128, 128], F32)
    RR32F = at("RR32F", [1, 512], F32)
    EPSC = at("EPSC", [128, 1], F32)
    FEN = at("FEN", [128, 8], F32)
    P.fence_ap = ONESC[0:1, 0:1]
    P.fence_sc = FEN
    P.fence_src = EPSC[0:1, 0:1]
    assert off <= SBYTES, f"SBUF overflow: {off} > {SBYTES}"
    assert SCRO + 20480 <= SBYTES

    PS = [nc.alloc_psum_tensor(f"ps{i}", [128, 512], F32) for i in range(8)]

    dbg_outs = {}
    named = {"X": X, "XLN": XLN, "KF": KF, "VT": VT, "PBUF": PBUF, "SALL": SALL,
             "SB": SB, "MSB": MSB, "SFM": SFM, "LEAFB": LEAFB, "E16H": E16H,
             "ATQ": ATQ, "BH": BH, "QSTG": QSTG, "STAT": STAT, "BN": BN,
             "LTFq": LTFS["q"], "LTFk": LTFS["k"], "LTFv": LTFS["v"]}

    def dbg_dump(phase):
        if upto == phase or any(p == phase for p, _ in dbg):
            P.barrier()
            P.op("scalar", lambda e: e.copy(P.fence_sc[0:1, 4:5], P.fence_src), wait=["#G"], sig=True)
            for p, name in dbg:
                if p != phase:
                    continue
                t = named[name]
                key = f"dbg_{phase}_{name}"
                shp = list(t.shape)
                d = nc.dram_tensor(key, shp, t.dtype, kind="ExternalOutput").ap()
                P.wait("sync", ["scalar"])
                P.dma(d, t[:], "dbg")
                dbg_outs[key] = tuple(shp)
            P.wait("scalar", ["#dbg"])
        return upto == phase

    cops = [
        lambda e: e.memset(FEN[:], 0.0),
        lambda e: e.memset(VT[:], 0.0),
        lambda e: e.memset(ONES1[:], 1.0),
        lambda e: e.memset(ONESC[:], 1.0),
        lambda e: e.memset(EPSC[:], EPS),
        lambda e: e.iota(I128B[:], [[1, 128]], channel_multiplier=-1,
                         allow_small_or_imprecise_dtypes=True),
        lambda e: e.tensor_single_scalar(I128B[:], I128B[:], 0.0, ALU.is_equal),
        lambda e: e.iota(I128F[:], [[1, 128]], channel_multiplier=-1,
                         allow_small_or_imprecise_dtypes=True),
        lambda e: e.tensor_single_scalar(I128F[:], I128F[:], 0.0, ALU.is_equal),
        lambda e: e.iota(TRI[:], [[1, 128]], channel_multiplier=-1,
                         allow_small_or_imprecise_dtypes=True),
        lambda e: e.tensor_single_scalar(TRI[:], TRI[:], 0.0, ALU.is_ge),
        lambda e: e.iota(IOTA16[:], [[1, 16]], channel_multiplier=0,
                         allow_small_or_imprecise_dtypes=True),
        lambda e: e.iota(GT1[:, 1:2], [[0, 1]], channel_multiplier=1,
                         allow_small_or_imprecise_dtypes=True),
        lambda e: e.tensor_single_scalar(GT1[:, 1:2], GT1[:, 1:2], 64.0, ALU.is_ge),
        lambda e: e.iota(C2[:], [[2, 8]], channel_multiplier=0,
                         allow_small_or_imprecise_dtypes=True),
        lambda e: e.tensor_tensor(C2[:], C2[:], GT1[:, 1:2].broadcast_to([128, 8]), ALU.add),
        lambda e: e.iota(GT1[:, 0:128], [[-2, 8], [1, 16]], channel_multiplier=0,
                         allow_small_or_imprecise_dtypes=True),
        lambda e: e.tensor_tensor(GT1[:, 0:128], GT1[:, 0:128],
                                  GT1[:, 1:2].broadcast_to([128, 128]), ALU.subtract),
        lambda e: e.tensor_single_scalar(BO16[:], GT1[:, 0:128], 0.0, ALU.is_equal),
    ]
    for i, cop in enumerate(cops):
        P.op("gpsimd", cop,
             sig=(["tensor", "scalar", "vector"] if i == len(cops) - 1 else None))
    P.dma(QB8[:], dr["qb8"], "mc")
    P.dma(KB8[:], dr["kb8"], "mc")
    P.dma(OB8[:], dr["ob8"], "mc")
    for t in ("q", "k", "v"):
        P.dma(B1T[t][:], dr[f"{t}b1"], "mc")
    P.op("vector", lambda e: e.tensor_single_scalar(QB8S[:], QB8[:], 0.125, ALU.mult),
         wait=["#mc", "#G"], sig="scalar")

    xst = [XSTG0, XSTG1]
    for tt in range(NTT):
        sl = xst[tt % 2]
        P.dma(sl[:], dr["x"][tt * 128:(tt + 1) * 128, :], f"x{tt % 2}",
              wait=([("scalar", 1)] if tt >= 2 else []))
        for c in range(NCH):
            pb = PS[c % 2]
            P.op("tensor",
                 lambda e, pb=pb, sl=sl, c=c: e.transpose(
                     pb[:, 0:128], sl[:, c * 128:(c + 1) * 128], I128F[:]),
                 wait=(["#G"] if tt == 0 and c == 0 else []) +
                      ([f"#x{tt % 2}"] if c == 0 else []) +
                      ([("scalar", 1)] if (tt, c) > (0, 1) else []),
                 sig="scalar")
            P.op("scalar",
                 lambda e, pb=pb, c=c, tt=tt: e.copy(
                     X[:, c * S + tt * 128:c * S + tt * 128 + 128], pb[:, 0:128]),
                 wait=["tensor"], sig=["tensor"] + (["sync"] if c == 7 else []))
    if dbg_dump("xin"):
        return nc, P, dbg_outs

    wstate = {"n": 0, "gafter": [0, 0]}

    def wpush(dram_ap, ncols=512, dst16=None, extra_cast_wait=(), colw=None):
        u = wstate["n"]
        wstate["n"] += 1
        i32 = u % 2
        pr = dram_ap.shape[0]
        P.dma(WS32[0:pr, i32 * 512:i32 * 512 + ncols], dram_ap, f"w{i32}",
              wait=[("abs", "#V", wstate["gafter"][i32])])
        if dst16 is None:
            cw = colw or 512
            i16 = u % (2048 // cw)
            dst = WB16[0:pr, i16 * cw:i16 * cw + ncols]
            col = i16 * cw
            cw = ["tensor"]
        else:
            tns, col, prx = dst16
            dst = tns[0:pr, col:col + ncols]
        P.op("vector",
             lambda e, dst=dst, i32=i32, pr=pr, ncols=ncols: e.tensor_copy(
                 dst, WS32[0:pr, i32 * 512:i32 * 512 + ncols]),
             wait=[f"#w{i32}", "tensor"] + list(extra_cast_wait), sig=True)
        wstate["gafter"][i32] = P.count("#V")
        return col

    def emit_ln(tag):
        part = globals().get("_LN_PART")
        P.barrier()
        cast_rec = {}
        for c in range(NCH):
            eng = "gpsimd" if c < 4 else "vector"
            P.op(eng, lambda e, c=c: e.tensor_copy(XLN[:, c * S:(c + 1) * S], X[:, c * S:(c + 1) * S]),
                 sig=True)
            cast_rec[c] = ("#G" if c < 4 else "#V", P.count("#G" if c < 4 else "#V"))
        for qb in range(NQB):
            pm = PS[qb % 2]
            for c in range(NCH):
                P.op("tensor",
                     lambda e, pm=pm, c=c, qb=qb: e.matmul(
                         pm[:], ONESC[:], XLN[:, c * S + qb * QB:c * S + qb * QB + QB],
                         start=(c == 0), stop=(c == NCH - 1)),
                     wait=([("abs",) + cast_rec[c]] if qb == 0 else []) +
                          ([("vector", 1)] if qb >= 2 and c == 0 else []),
                     sig=("vector" if c == NCH - 1 else None))
            P.op("vector",
                 lambda e, pm=pm, qb=qb: e.tensor_single_scalar(
                     MSB[:, qb * QB:(qb + 1) * QB], pm[:], 1.0 / D, ALU.mult),
                 wait=["tensor"], sig="tensor")
        if part == "sums":
            return
        sq_rec = {}
        for c in range(NCH):
            if c < 4:
                P.op("scalar",
                     lambda e, c=c: e.activation(XLN[:, c * S:(c + 1) * S],
                                                 X[:, c * S:(c + 1) * S], AF.Square),
                     wait=(["tensor"] if c == 0 else []), sig=True)
                sq_rec[c] = ("#S", P.count("#S"))
            else:
                P.op("vector",
                     lambda e, c=c: e.tensor_mul(XLN[:, c * S:(c + 1) * S],
                                                 X[:, c * S:(c + 1) * S], X[:, c * S:(c + 1) * S]),
                     wait=(["tensor"] if c == 4 else []), sig=True)
                sq_rec[c] = ("#V", P.count("#V"))
        if part == "sq":
            return
        for qb in range(NQB):
            pv = PS[2 + qb % 2]
            q0, q1 = qb * QB, (qb + 1) * QB
            for c in range(NCH):
                P.op("tensor",
                     lambda e, pv=pv, c=c, qb=qb: e.matmul(
                         pv[:], ONESC[:], XLN[:, c * S + qb * QB:c * S + qb * QB + QB],
                         start=(c == 0), stop=(c == NCH - 1)),
                     wait=([("abs",) + sq_rec[c]] if qb == 0 else []) +
                          ([("vector", 2)] if qb >= 2 and c == 0 else []),
                     sig=("vector" if c == NCH - 1 else None))
            P.op("vector", lambda e, q0=q0, q1=q1: e.tensor_mul(
                SCRN[:, 0:512], MSB[:, q0:q1], MSB[:, q0:q1]))
            P.op("vector",
                 lambda e, pv=pv: e.scalar_tensor_tensor(
                     SCRN[:, 512:1024], pv[:], 1.0 / D, SCRN[:, 0:512],
                     ALU.mult, ALU.subtract),
                 wait=["tensor"], sig=["tensor", "scalar"])
            P.op("scalar", lambda e: e.activation(SCRN[:, 0:512], SCRN[:, 512:1024],
                                                  AF.Ln, bias=EPSC[:]),
                 wait=["vector"])
            P.op("scalar", lambda e, q0=q0, q1=q1: e.activation(
                SB[:, q0:q1], SCRN[:, 0:512], AF.Exp, scale=-0.5), sig="vector")
            P.op("vector", lambda e, q0=q0, q1=q1: e.tensor_mul(
                MSB[:, q0:q1], MSB[:, q0:q1], SB[:, q0:q1]),
                wait=["scalar"], sig="gpsimd")
        if part == "stats":
            return
        P.wait("gpsimd", ["vector"])
        pi = 0
        for c in range(NCH):
            for hh in range(4):
                o = c * S + hh * 512
                so = hh * 512 if False else (hh % 4) * 512
                so = (hh * 512) % 2048
                sl = (pi % 2) * 512
                P.op("gpsimd",
                     lambda e, o=o, so=so, sl=sl: e.tensor_mul(
                         SCRN[:, sl:sl + 512], X[:, o:o + 512], SB[:, so:so + 512]),
                     wait=([("vector", 1)] if pi >= 2 else []))
                P.op("vector",
                     lambda e, o=o, so=so, sl=sl: e.tensor_sub(
                         XLN[:, o:o + 512], SCRN[:, sl:sl + 512], MSB[:, so:so + 512]),
                     wait=["#G"],
                     sig=["gpsimd"] + (["tensor", "scalar"] if pi == 31 else []))
                pi += 1

    def emit_routing(tag):
        P.barrier()
        co = wpush(dr["npk"], ncols=408)
        for qb in range(NQB):
            pb = PS[4]
            for c in range(NCH):
                P.op("tensor",
                     lambda e, pb=pb, c=c, qb=qb, co=co: e.matmul(
                         pb[0:45, :], WB16[0:128, co + c * 45:co + c * 45 + 45],
                         XLN[:, c * S + qb * QB:c * S + qb * QB + QB],
                         start=(c == 0), stop=False),
                     wait=(["vector"] if qb == 0 and c == 0 else []) +
                          (["scalar"] if qb > 0 and c == 0 else []))
            P.op("tensor",
                 lambda e, pb=pb, co=co: e.matmul(
                     pb[0:45, :], WB16[0:1, co + 360:co + 405], ONES1[:],
                     start=False, stop=True),
                 sig="scalar")
            P.op("scalar",
                 lambda e, pb=pb, qb=qb: e.copy(SFM[0:45, qb * QB:(qb + 1) * QB], pb[0:45, :]),
                 wait=["tensor"], sig=["tensor"] + (["gpsimd"] if qb == NQB - 1 else []))
        for tt in range(NTT):
            pb = PS[5]
            P.op("tensor",
                 lambda e, pb=pb, tt=tt: e.transpose(
                     pb[0:128, 0:45], SFM[0:45, tt * 128:(tt + 1) * 128],
                     I128F[0:45, 0:45]),
                 wait=["scalar"], sig="scalar")
            P.op("scalar",
                 lambda e, pb=pb, tt=tt: e.copy(SALL[:, tt * 45:tt * 45 + 45], pb[0:128, 0:45]),
                 wait=["tensor"], sig=["tensor"] + (["vector"] if tt == NTT - 1 else []))

        def scol(col):
            return SALL[:, col:col + 45 * 15 + 1:45]

        def mcol(i):
            return MUXT[:, i * 16:(i + 1) * 16]

        for ti, t in enumerate(("q", "k", "v")):
            b0, b1, b2, b3 = mcol(0), mcol(1), mcol(2), mcol(3)
            t0, t1, t2, t3 = mcol(4), mcol(5), mcol(6), mcol(7)
            u0, u1 = mcol(8), mcol(9)
            base = ti * 15

            def V(fn, **kw):
                P.op("vector", fn, **kw)

            def mux(dst, a, b_, sel):
                V(lambda e, dst=dst, a=a, b_=b_: e.tensor_sub(dst, b_, a))
                V(lambda e, dst=dst, sel=sel: e.tensor_mul(dst, dst, sel))
                V(lambda e, dst=dst, a=a: e.tensor_add(dst, dst, a))

            V(lambda e, base=base, b0=b0: e.tensor_single_scalar(b0, scol(base + 0), 0.0, ALU.is_ge),
              wait=(["scalar"] if ti == 0 else []))
            mux(t0, scol(base + 1), scol(base + 2), b0)
            V(lambda e, b1=b1, t0=t0: e.tensor_single_scalar(b1, t0, 0.0, ALU.is_ge))
            mux(t0, scol(base + 3), scol(base + 4), b1)
            mux(t1, scol(base + 5), scol(base + 6), b1)
            mux(t2, t0, t1, b0)
            V(lambda e, b2=b2, t2=t2: e.tensor_single_scalar(b2, t2, 0.0, ALU.is_ge))
            mux(t0, scol(base + 7), scol(base + 8), b2)
            mux(t1, scol(base + 9), scol(base + 10), b2)
            mux(t2, scol(base + 11), scol(base + 12), b2)
            mux(t3, scol(base + 13), scol(base + 14), b2)
            mux(u0, t0, t1, b1)
            mux(u1, t2, t3, b1)
            mux(t0, u0, u1, b0)
            V(lambda e, b3=b3, t0=t0: e.tensor_single_scalar(b3, t0, 0.0, ALU.is_ge))
            V(lambda e, b0=b0, b1=b1: e.scalar_tensor_tensor(LTF[:], b0, 2.0, b1, ALU.mult, ALU.add))
            V(lambda e, b2=b2: e.scalar_tensor_tensor(LTF[:], LTF[:], 2.0, b2, ALU.mult, ALU.add))
            V(lambda e, b3=b3, t=t: e.scalar_tensor_tensor(LTFS[t][:], LTF[:], 2.0, b3, ALU.mult, ALU.add))
            V(lambda e, t=t: e.tensor_copy(LT16[t][:], LTFS[t][:]),
              sig=(["tensor", "scalar", "gpsimd"] if t == "v" else None))

    def emit_tree(tree, dst, tag, scale=1.0):
        P.barrier()
        for th in range(2):
            P.op("tensor",
                 lambda e, th=th, tree=tree: e.matmul(
                     PS[6][0:8, 0:128], LT16[tree][:, th * 8:th * 8 + 8], I128B[:],
                     start=True, stop=True),
                 wait=(["vector", "scalar", "#G"] if th == 1 else []),
                 sig="scalar")
            P.op("scalar", lambda e: e.copy(LEAFT[:], PS[6][0:8, 0:128]),
                 wait=["tensor"], sig="tensor")
            for g in range(2):
                for j in range(4):
                    tt8 = g * 4 + j
                    P.op("tensor",
                         lambda e, tt8=tt8, j=j: e.matmul(
                             PS[5][:, j * 128:(j + 1) * 128],
                             I128B[0:8, tt8:tt8 + 1].broadcast_to([8, 128]), LEAFT[:],
                             start=True, stop=True),
                         wait=(["scalar"] if tt8 == 0 else []) +
                              (["scalar"] if g == 1 and j == 0 else []),
                         sig=("scalar" if j == 3 else None))
                P.op("scalar",
                     lambda e, g=g: e.copy(LEAFB[:, g * 512:(g + 1) * 512], PS[5][:]),
                     wait=["tensor"], sig=["tensor", "vector"])
            wcols = {}
            wa = wpush(dr[f"{tree}w1"][0, :, 0:512])
            wb = wpush(dr[f"{tree}w1"][0, :, 512:1024])
            wcols[0] = (wa, wb, P.count("#V"))
            for l in range(NCH):
                if l + 1 < NCH:
                    wa = wpush(dr[f"{tree}w1"][l + 1, :, 0:512])
                    wb = wpush(dr[f"{tree}w1"][l + 1, :, 512:1024])
                    wcols[l + 1] = (wa, wb, P.count("#V"))
                wa, wb, vw1 = wcols[l]
                for qh in range(2):
                    qb = th * 2 + qh
                    gi = (2 * l + qh) % 4
                    pb = PS[gi]
                    for c in range(NCH):
                        colof = (wa if c < 4 else wb) + (c % 4) * 128
                        P.op("tensor",
                             lambda e, pb=pb, c=c, qb=qb, colof=colof: e.matmul(
                                 pb[:], WB16[:, colof:colof + 128],
                                 XLN[:, c * S + qb * QB:c * S + qb * QB + QB],
                                 start=(c == 0), stop=(c == NCH - 1)),
                             wait=([("abs", "#V", vw1)] if c == 0 else []) +
                                  ([("scalar", 3)] if c == 0 else []),
                             sig=("scalar" if c == NCH - 1 else None))
                    P.op("scalar",
                         lambda e, pb=pb, l=l, qh=qh: e.activation(
                             PBUF[:, l * 1024 + qh * 512:l * 1024 + qh * 512 + 512],
                             pb[:], AF.Relu, bias=B1T[tree][:, l:l + 1]),
                         wait=["tensor"] + (["#mc"] if l == 0 and qh == 0 and th == 0 else []),
                         sig=["tensor"] + (["vector"] if qh == 1 else []))
                P.op("vector",
                     lambda e, l=l: e.tensor_single_scalar(MSK[:], LEAFB[:], C2[:, l:l + 1], ALU.is_equal),
                     wait=["scalar"])
                P.op("vector",
                     lambda e, l=l: e.tensor_mul(PBUF[:, l * 1024:(l + 1) * 1024],
                                                 PBUF[:, l * 1024:(l + 1) * 1024], MSK[:]),
                     sig=["tensor"])
            for g in range(2):
                for j in range(4):
                    tt8 = g * 4 + j
                    P.op("vector",
                         lambda e, tt8=tt8, th=th, tree=tree: e.tensor_single_scalar(
                             ETOK[:], IOTA16[:],
                             LTFS[tree][:, th * 8 + tt8:th * 8 + tt8 + 1], ALU.is_equal),
                         wait=([("tensor", 0)] if tt8 > 0 else []), sig="tensor")
                    P.op("tensor",
                         lambda e, j=j: e.matmul(PS[6][0:16, j * 128:(j + 1) * 128],
                                                 ETOK[:], I128B[:], start=True, stop=True),
                         wait=["vector"] + (["scalar"] if j == 0 else []),
                         sig=["vector"] + (["scalar"] if j == 3 else []))
                P.op("scalar",
                     lambda e, g=g: e.copy(E16H[:, g * 512:(g + 1) * 512], PS[6][0:16, :]),
                     wait=["tensor"], sig="tensor")
            for half in range(2):
                wpush(dr[f"{tree}b2"][:, half * 512:half * 512 + 512],
                      dst16=(B2BF, half * 512, 16), extra_cast_wait=["vector", "scalar"])
            if dst == "VT":
                for g in range(4):
                    slots = []
                    for lcg in range(2):
                        for lcq in range(4):
                            lc = lcg * 4 + lcq
                            u = wpush(dr["vw2"][lc, :, g * 256:(g + 1) * 256], ncols=256, colw=256)
                            slots.append(u)
                    for lcg in range(2):
                        for tt8 in range(8):
                            tt = th * 8 + tt8
                            pb = PS[tt8]
                            for lcq in range(4):
                                lc = lcg * 4 + lcq
                                P.op("tensor",
                                     lambda e, pb=pb, lc=lc, tt8=tt8, g=g,
                                            cu=slots[lc]: e.matmul(
                                         pb[:, 0:256],
                                         PBUF[:, lc * 1024 + tt8 * 128:lc * 1024 + tt8 * 128 + 128],
                                         WB16[:, cu:cu + 256],
                                         start=(lc == 0), stop=False),
                                     wait=(["vector"] if lc == 0 and tt8 == 0 and lcg == 0 else []) +
                                          (["scalar"] if lc == 0 and lcg == 0 and tt8 == 0 and g > 0 else []))
                            if lcg == 1:
                                P.op("tensor",
                                     lambda e, pb=pb, tt8=tt8, g=g: e.matmul(
                                         pb[:, 0:256],
                                         E16H[:, tt8 * 128:tt8 * 128 + 128],
                                         B2BF[:, g * 256:(g + 1) * 256],
                                         start=False, stop=True),
                                     sig="scalar")
                                P.op("scalar",
                                     lambda e, pb=pb, tt=tt, g=g: e.activation(
                                         VT[:, tt * 1040 + g * 256:tt * 1040 + g * 256 + 256],
                                         pb[:, 0:256], AF.Copy, scale=scale),
                                     wait=["tensor"],
                                     sig=["tensor"] + (["gpsimd"] if tt8 >= 6 else []))
            else:
                dtens = KF if dst == "KF" else XLN
                w2cols = {}
                wa = wpush(dr[f"{tree}w2"][0, :, 0:512])
                wb = wpush(dr[f"{tree}w2"][0, :, 512:1024])
                w2cols[0] = (wa, wb, P.count("#V"))
                for oc in range(NCH):
                    if oc + 1 < NCH:
                        wa = wpush(dr[f"{tree}w2"][oc + 1, :, 0:512])
                        wb = wpush(dr[f"{tree}w2"][oc + 1, :, 512:1024])
                        w2cols[oc + 1] = (wa, wb, P.count("#V"))
                    wa, wb, vw2 = w2cols[oc]
                    for qh in range(2):
                        qb = th * 2 + qh
                        pb = PS[(2 * oc + qh) % 4]
                        for lc in range(NCH):
                            colof = (wa if lc < 4 else wb) + (lc % 4) * 128
                            P.op("tensor",
                                 lambda e, pb=pb, lc=lc, qh=qh, colof=colof: e.matmul(
                                     pb[:], WB16[:, colof:colof + 128],
                                     PBUF[:, lc * 1024 + qh * 512:lc * 1024 + qh * 512 + 512],
                                     start=(lc == 0), stop=False),
                                 wait=([("abs", "#V", vw2)] if lc == 0 else []) +
                                      ([("scalar", 3)] if lc == 0 else []))
                        P.op("tensor",
                             lambda e, pb=pb, oc=oc, qh=qh: e.matmul(
                                 pb[:], B2BF[:, oc * 128:(oc + 1) * 128],
                                 E16H[:, qh * 512:qh * 512 + 512],
                                 start=False, stop=True),
                             sig="scalar")
                        P.op("scalar",
                             lambda e, pb=pb, oc=oc, qb=qb, dtens=dtens: e.activation(
                                 dtens[:, oc * S + qb * QB:oc * S + qb * QB + QB],
                                 pb[:], AF.Copy, scale=scale),
                             wait=["tensor"],
                             sig=["tensor"] + (["gpsimd"] if qh == 1 else []))

    def emit_fff_attn(tag, emit_output=False):
        P.barrier()
        for wi, TS in ((0, XLN), (1, KF)):
            for qb in range(NQB):
                for c in range(NCH):
                    P.op("vector",
                         lambda e, TS=TS, c=c, qb=qb: e.tensor_mul(
                             PBUF[:, (c % 4) * 512:(c % 4) * 512 + 512],
                             TS[:, c * S + qb * QB:c * S + qb * QB + QB],
                             TS[:, c * S + qb * QB:c * S + qb * QB + QB]),
                         wait=([("tensor", 3)] if (c >= 4 or qb > 0 or wi > 0) else []),
                         sig="tensor")
                    P.op("tensor",
                         lambda e, c=c: e.matmul(
                             PS[4][0:1, :], ONESC[:, 0:1],
                             PBUF[:, (c % 4) * 512:(c % 4) * 512 + 512],
                             start=(c == 0), stop=(c == NCH - 1)),
                         wait=["vector"], sig=True)
                P.op("vector",
                     lambda e: e.reduce_max(STAT[0:1, 1:2], PS[4][0:1, :], mybir.AxisListType.X),
                     wait=["tensor"], sig="tensor")
                if qb == 0:
                    P.op("vector", lambda e, wi=wi: e.tensor_copy(
                        STAT[0:1, 3 + wi:4 + wi], STAT[0:1, 1:2]))
                else:
                    P.op("vector", lambda e, wi=wi: e.tensor_max(
                        STAT[0:1, 3 + wi:4 + wi], STAT[0:1, 3 + wi:4 + wi], STAT[0:1, 1:2]))
        P.op("vector", lambda e: e.tensor_mul(STAT[0:1, 5:6], STAT[0:1, 3:4], STAT[0:1, 4:5]),
             sig="scalar")
        P.op("scalar", lambda e: e.activation(STAT[0:1, 6:7], STAT[0:1, 5:6], AF.Ln),
             wait=["vector"])
        P.op("scalar", lambda e: e.activation(STAT[0:1, 7:8], STAT[0:1, 6:7], AF.Exp, scale=0.5),
             sig="vector")
        P.op("vector", lambda e: e.tensor_single_scalar(STAT[0:1, 8:9], STAT[0:1, 7:8], -1.0, ALU.mult),
             wait=["scalar"])
        P.op("vector", lambda e: e.tensor_copy(ETOK1[0:1, 0:1], STAT[0:1, 8:9]), sig="tensor")
        P.op("tensor", lambda e: e.matmul(PS[5][:, 0:1], ONESC[0:1, :], ETOK1[0:1, 0:1],
                                          start=True, stop=True),
             wait=["vector"], sig="scalar")
        P.op("scalar", lambda e: e.copy(BN[:], PS[5][:, 0:1]), wait=["tensor"],
             sig=["tensor", "vector"])

        for qb in range(NQB):
            for kt in range(NTT):
                pb = PS[kt % 3]
                for c in range(NCH):
                    P.op("tensor",
                         lambda e, pb=pb, c=c, kt=kt, qb=qb: e.matmul(
                             pb[:], KF[:, c * S + kt * 128:c * S + kt * 128 + 128],
                             XLN[:, c * S + qb * QB:c * S + qb * QB + QB],
                             start=(c == 0), stop=(c == NCH - 1)),
                         wait=([("scalar", 2)] if c == 0 and kt >= 3 else []) +
                              (["vector", "scalar"] if c == 0 and kt == 0 and qb == 0 else []) +
                              (["vector"] if c == 0 and kt == 0 and qb > 0 else []),
                         sig=("scalar" if c == NCH - 1 else None))
                if kt > 0:
                    P.op("tensor",
                         lambda e, kt=kt: e.matmul(
                             PS[4][0:1, :], ONESC[:, 0:1],
                             PBUF[:, (kt - 1) * 512:kt * 512],
                             start=(kt - 1 == 0), stop=False),
                         wait=["scalar"])
                P.op("scalar",
                     lambda e, pb=pb, kt=kt: e.activation(
                         PBUF[:, kt * 512:(kt + 1) * 512], pb[:], AF.Exp, bias=BN[:]),
                     wait=["tensor"], sig="tensor")
            P.op("tensor",
                 lambda e: e.matmul(PS[4][0:1, :], ONESC[:, 0:1],
                                    PBUF[:, (NTT - 1) * 512:NTT * 512],
                                    start=False, stop=True),
                 wait=["scalar"], sig=True)
            P.op("vector", lambda e: e.reciprocal(RR32F[:], PS[4][0:1, :]),
                 wait=[("tensor", 0), "#G"], sig="tensor")
            P.op("vector", lambda e: e.tensor_copy(RROWF[:], RR32F[:]), sig="tensor")
            P.op("tensor", lambda e: e.matmul(PS[5][:], ONES1[0:1, 0:128], RROWF[:],
                                              start=True, stop=True),
                 wait=["vector", ("scalar", 0)], sig="scalar")
            P.op("scalar", lambda e: e.copy(RBF[:], PS[5][:]), wait=["tensor"],
                 sig=["tensor", "vector"])
            for dc in range(NCH):
                pb = PS[3 if dc % 2 == 0 else 7]
                for kt in range(NTT):
                    P.op("tensor",
                         lambda e, pb=pb, dc=dc, kt=kt: e.matmul(
                             pb[:], VT[:, kt * 1040 + dc * 128:kt * 1040 + dc * 128 + 128],
                             PBUF[:, kt * 512:(kt + 1) * 512],
                             start=(kt == 0), stop=(kt == NTT - 1)),
                         wait=([("vector", 2)] if kt == 0 and dc >= 2 else []),
                         sig=("vector" if kt == NTT - 1 else None))
                o = dc * S + qb * QB
                P.op("vector",
                     lambda e, pb=pb: e.tensor_mul(TMPF[:], pb[:], RBF[:]),
                     wait=["tensor", "#G"] + (["scalar"] if dc == 0 else []),
                     sig=["tensor", "gpsimd"])
                P.op("gpsimd",
                     lambda e, o=o: e.tensor_add(X[:, o:o + 512], X[:, o:o + 512], TMPF[:]),
                     wait=["vector"],
                     sig=["vector"] + (["tensor"] if emit_output else []))
                if emit_output:
                    for j in range(4):
                        P.op("tensor",
                             lambda e, o=o, j=j: e.transpose(
                                 PS[6][:, 0:128], X[:, o + j * 128:o + j * 128 + 128], I128F[:]),
                             wait=["#G"] + (["scalar"] if (dc, j) > (0, 0) or qb > 0 else []),
                             sig="scalar")
                        P.op("scalar",
                             lambda e, j=j: e.copy(OSTG[:, (j % 2) * 128:(j % 2) * 128 + 128],
                                                   PS[6][:, 0:128]),
                             wait=["tensor", f"#o{j % 2}"], sig=["tensor", "sync"])
                        P.dma(y[qb * QB + j * 128:qb * QB + j * 128 + 128,
                                dc * 128:dc * 128 + 128],
                              OSTG[:, (j % 2) * 128:(j % 2) * 128 + 128], f"o{j % 2}",
                              wait=["scalar"])

    def emit_mha(tag):
        P.barrier()
        kcols = {}
        wa = wpush(dr["kw"][0, :, 0:512])
        wb = wpush(dr["kw"][0, :, 512:1024])
        kcols[0] = (wa, wb, P.count("#V"))
        for oc in range(NCH):
            if oc + 1 < NCH:
                wa = wpush(dr["kw"][oc + 1, :, 0:512])
                wb = wpush(dr["kw"][oc + 1, :, 512:1024])
                kcols[oc + 1] = (wa, wb, P.count("#V"))
            wa, wb, vwk = kcols[oc]
            for qb in range(NQB):
                pb = PS[(4 * oc + qb) % 4]
                for mc in range(NCH):
                    colof = (wa if mc < 4 else wb) + (mc % 4) * 128
                    P.op("tensor",
                         lambda e, pb=pb, mc=mc, qb=qb, colof=colof: e.matmul(
                             pb[:], WB16[:, colof:colof + 128],
                             XLN[:, mc * S + qb * QB:mc * S + qb * QB + QB],
                             start=(mc == 0), stop=(mc == NCH - 1)),
                         wait=([("abs", "#V", vwk)] if mc == 0 and qb == 0 else []) +
                              ([("scalar", 3)] if mc == 0 else []),
                         sig=("scalar" if mc == NCH - 1 else None))
                P.op("scalar",
                     lambda e, pb=pb, oc=oc, qb=qb: e.add(
                         KF[:, oc * S + qb * QB:oc * S + qb * QB + QB], pb[:],
                         KB8[:, oc:oc + 1]),
                     wait=["tensor"],
                     sig=["tensor"] + (["gpsimd"] if qb == NQB - 1 else []))
        P.op("gpsimd", lambda e: e.memset(
            VT[:].rearrange("p (t h) -> p t h", h=1040)[:, :, 64:1040:65], 1.0),
            wait=["tensor", "scalar"], sig="tensor")
        for half in range(2):
            for mc in range(NCH):
                wpush(dr["vw"][mc, :, half * 512:half * 512 + 512],
                      dst16=(VWA, mc * 512, 128),
                      extra_cast_wait=(["tensor"] if half == 1 else []))
            wpush(dr["vbrow"][0:1, half * 512:half * 512 + 512],
                  dst16=(VWA, 4096, 1))
            vwv = P.count("#V")
            for tt in range(NTT):
                pb = PS[tt % 4]
                for mc in range(NCH):
                    P.op("tensor",
                         lambda e, pb=pb, mc=mc, tt=tt: e.matmul(
                             pb[:], XLN[:, mc * S + tt * 128:mc * S + tt * 128 + 128],
                             VWA[:, mc * 512:(mc + 1) * 512],
                             start=(mc == 0), stop=False),
                         wait=([("abs", "#V", vwv)] if mc == 0 and tt == 0 else []) +
                              ([("scalar", 3)] if mc == 0 and tt > 0 else []))
                P.op("tensor",
                     lambda e, pb=pb: e.matmul(pb[:], ONES1[0:1, 0:128], VWA[0:1, 4096:4608],
                                               start=False, stop=True),
                     sig="scalar")
                P.op("scalar",
                     lambda e, pb=pb, tt=tt, half=half: e.copy(
                         VT[:].rearrange("p (t h) -> p t h", h=1040)
                         [:, tt, half * 520:half * 520 + 520]
                         .rearrange("p (g c) -> p g c", c=65)[:, :, 0:64],
                         pb[:]),
                     wait=["tensor"],
                     sig=["tensor"] + (["gpsimd"] if tt >= NTT - 2 else []))
        for qb in range(NQB):
            qev = []
            qcols = {}
            wa = wpush(dr["qw"][0, :, 0:512])
            wb = wpush(dr["qw"][0, :, 512:1024])
            qcols[0] = (wa, wb, P.count("#V"))
            for oc in range(NCH):
                if oc + 1 < NCH:
                    wa = wpush(dr["qw"][oc + 1, :, 0:512])
                    wb = wpush(dr["qw"][oc + 1, :, 512:1024])
                    qcols[oc + 1] = (wa, wb, P.count("#V"))
                wa, wb, vwq = qcols[oc]
                pb = PS[oc % 4]
                for mc in range(NCH):
                    colof = (wa if mc < 4 else wb) + (mc % 4) * 128
                    P.op("tensor",
                         lambda e, pb=pb, mc=mc, qb=qb, colof=colof: e.matmul(
                             pb[:], WB16[:, colof:colof + 128],
                             XLN[:, mc * S + qb * QB:mc * S + qb * QB + QB],
                             start=(mc == 0), stop=(mc == NCH - 1)),
                         wait=([("abs", "#V", vwq)] if mc == 0 else []) +
                              ([("abs", "#V", qev[oc - 4])] if mc == 0 and oc >= 4 else []) +
                              (["vector"] if mc == 0 and oc == 0 and qb > 0 else []),
                         sig=("vector" if mc == NCH - 1 else None))
                P.op("vector",
                     lambda e, pb=pb, oc=oc: e.tensor_scalar(
                         QSTG[:, oc * 512:(oc + 1) * 512], pb[:],
                         0.125, QB8S[:, oc:oc + 1], ALU.mult, ALU.add),
                     wait=["tensor"], sig=True)
                qev.append(P.count("#V"))
            for oc in range(NCH):
                P.op("vector",
                     lambda e, oc=oc, qb=qb: e.tensor_copy(
                         XLN[:, oc * S + qb * QB:oc * S + qb * QB + QB],
                         QSTG[:, oc * 512:(oc + 1) * 512]),
                     sig=(["tensor"] if oc == NCH - 1 else None))
        for wi, TS in ((0, XLN), (1, KF)):
            for qb in range(NQB):
                for c in range(NCH):
                    P.op("vector",
                         lambda e, TS=TS, c=c, qb=qb: e.tensor_mul(
                             PBUF[:, (c % 4) * 512:(c % 4) * 512 + 512],
                             TS[:, c * S + qb * QB:c * S + qb * QB + QB],
                             TS[:, c * S + qb * QB:c * S + qb * QB + QB]),
                         wait=([("tensor", 3)] if (c >= 4 or qb > 0 or wi > 0) else []) +
                              (["scalar"] if wi == 0 and qb == 0 and c == 0 else []),
                         sig="tensor")
                    P.op("tensor",
                         lambda e, c=c: e.matmul(
                             PS[4][0:16, :], BO16[:, c * 16:c * 16 + 16],
                             PBUF[:, (c % 4) * 512:(c % 4) * 512 + 512],
                             start=(c == 0), stop=(c == NCH - 1)),
                         wait=["vector"], sig=True)
                P.op("vector",
                     lambda e: e.reduce_max(STAT[0:16, 1:2], PS[4][0:16, :], mybir.AxisListType.X),
                     wait=["tensor"], sig="tensor")
                if qb == 0:
                    P.op("vector", lambda e, wi=wi: e.tensor_copy(
                        STAT[0:16, 4 + wi:5 + wi], STAT[0:16, 1:2]))
                else:
                    P.op("vector", lambda e, wi=wi: e.tensor_max(
                        STAT[0:16, 4 + wi:5 + wi], STAT[0:16, 4 + wi:5 + wi], STAT[0:16, 1:2]))
        P.op("vector", lambda e: e.tensor_mul(STAT[0:16, 6:7], STAT[0:16, 4:5], STAT[0:16, 5:6]),
             sig="scalar")
        P.op("scalar", lambda e: e.activation(STAT[0:16, 7:8], STAT[0:16, 6:7], AF.Ln),
             wait=["vector"])
        P.op("scalar", lambda e: e.activation(STAT[0:16, 8:9], STAT[0:16, 7:8], AF.Exp, scale=0.5),
             sig="vector")
        P.op("vector", lambda e: e.tensor_single_scalar(STAT[0:16, 9:10], STAT[0:16, 8:9],
                                                        -1.0, ALU.mult),
             wait=["scalar"])
        P.op("vector", lambda e: e.tensor_copy(BH16[:], STAT[0:16, 9:10]), sig="tensor")
        for h in range(NH):
            P.op("tensor",
                 lambda e, h=h: e.matmul(PS[5][:, h:h + 1],
                                         I128B[0:16, h:h + 1].broadcast_to([16, 128]),
                                         BH16[:], start=True, stop=True),
                 wait=(["vector"] if h == 0 else []),
                 sig=("scalar" if h == NH - 1 else None))
        P.op("scalar", lambda e: e.copy(BH[:], PS[5][:, 0:16]), wait=["tensor"],
             sig=["tensor", "vector"])

        for qb in range(NQB):
            for h in range(NH):
                hc, hp = h // 2, (h % 2) * 64
                pv = PS[2 + h % 2]
                nkt = 4 * qb + 4
                for kt in range(nkt):
                    j = kt - 4 * qb
                    diag = j >= 0
                    c0 = 128 * j if diag else 0
                    ps_ = PS[kt % 2]
                    sslot = (kt % 4) * 512
                    P.op("tensor",
                         lambda e, ps_=ps_, hc=hc, hp=hp, kt=kt, qb=qb, c0=c0: e.matmul(
                             ps_[:, c0:512],
                             KF[hp:hp + 64, hc * S + kt * 128:hc * S + kt * 128 + 128],
                             XLN[hp:hp + 64, hc * S + qb * QB + c0:hc * S + qb * QB + QB],
                             start=True, stop=True),
                         wait=([("scalar", 1)] if (kt >= 2 or h > 0 or qb > 0) else []) +
                              (["vector", "#G"] if kt == 0 and h == 0 and qb == 0 else []) +
                              (["vector"] if kt == 0 and h == 0 and qb > 0 else []),
                         sig="scalar")
                    P.op("scalar",
                         lambda e, ps_=ps_, sslot=sslot, c0=c0, h=h: e.activation(
                             PRING[:, sslot + c0:sslot + 512], ps_[:, c0:512],
                             AF.Exp, bias=BH[:, h:h + 1]),
                         wait=["tensor"],
                         sig=["tensor"] + (["vector"] if diag else []))
                    if diag:
                        P.op("vector",
                             lambda e, sslot=sslot, c0=c0: e.tensor_mul(
                                 PRING[:, sslot + c0:sslot + c0 + 128],
                                 PRING[:, sslot + c0:sslot + c0 + 128], TRI[:]),
                             wait=["scalar"], sig="tensor")
                    P.op("tensor",
                         lambda e, pv=pv, h=h, kt=kt, c0=c0, nkt=nkt, sslot=sslot: e.matmul(
                             pv[0:65, c0:512],
                             VT[:, kt * 1040 + h * 65:kt * 1040 + h * 65 + 65],
                             PRING[:, sslot + c0:sslot + 512],
                             start=(kt == 0), stop=(kt == nkt - 1)),
                         wait=(["vector"] if diag else ["scalar"]),
                         sig=("vector" if kt == nkt - 1 else None))
                P.op("vector", lambda e, pv=pv: e.reciprocal(RR32F[:], pv[64:65, :]),
                     wait=["tensor"], sig="tensor")
                P.op("vector", lambda e: e.tensor_copy(RROWM[:], RR32F[:]), sig="tensor")
                P.op("tensor",
                     lambda e: e.matmul(PS[5][0:64, :], ONES1[0:1, 0:64], RROWM[:],
                                        start=True, stop=True),
                     wait=["vector"], sig="scalar")
                P.op("scalar",
                     lambda e: e.copy(RBM[:], PS[5][0:64, :]),
                     wait=["tensor"], sig=["tensor", "vector"])
                P.op("vector",
                     lambda e, pv=pv, hc=hc, hp=hp: e.tensor_mul(
                         ATQ[hp:hp + 64, hc * 512:(hc + 1) * 512], pv[0:64, :], RBM[:]),
                     wait=["scalar"], sig=["tensor"])
            oev = []
            ocols = {}
            wa = wpush(dr["ow"][0, :, 0:512])
            wb = wpush(dr["ow"][0, :, 512:1024])
            ocols[0] = (wa, wb, P.count("#V"))
            for mc in range(NCH):
                if mc + 1 < NCH:
                    wa = wpush(dr["ow"][mc + 1, :, 0:512])
                    wb = wpush(dr["ow"][mc + 1, :, 512:1024])
                    ocols[mc + 1] = (wa, wb, P.count("#V"))
                wa, wb, vwo = ocols[mc]
                pb = PS[mc % 4]
                for hcc in range(NCH):
                    colof = (wa if hcc < 4 else wb) + (hcc % 4) * 128
                    P.op("tensor",
                         lambda e, pb=pb, hcc=hcc, colof=colof: e.matmul(
                             pb[:], WB16[:, colof:colof + 128], ATQ[:, hcc * 512:(hcc + 1) * 512],
                             start=(hcc == 0), stop=(hcc == NCH - 1)),
                         wait=([("abs", "#V", vwo)] if hcc == 0 else []) +
                              ([("abs", "#V", oev[mc - 4])] if hcc == 0 and mc >= 4 else []),
                         sig=("vector" if hcc == NCH - 1 else None))
                o = mc * S + qb * QB
                P.op("vector",
                     lambda e, pb=pb, o=o, mc=mc: e.scalar_tensor_tensor(
                         X[:, o:o + 512], pb[:], OB8[:, mc:mc + 1], X[:, o:o + 512],
                         ALU.add, ALU.add),
                     wait=["tensor"], sig=["tensor"])
                oev.append(P.count("#V"))
            P.wait("tensor", ["vector"])

    emit_ln("ln1")
    if dbg_dump("ln1"):
        return nc, P, dbg_outs
    emit_routing("r1")
    if dbg_dump("route1"):
        return nc, P, dbg_outs
    emit_tree("k", "KF", "t1k")
    if dbg_dump("ktree1"):
        return nc, P, dbg_outs
    emit_tree("v", "VT", "t1v", scale=0.5)
    if dbg_dump("vtree1"):
        return nc, P, dbg_outs
    emit_tree("q", "XLN", "t1q", scale=1.0 / 32.0)
    if dbg_dump("qtree1"):
        return nc, P, dbg_outs
    emit_fff_attn("a1")
    if dbg_dump("attn1"):
        return nc, P, dbg_outs
    emit_ln("ln2")
    if dbg_dump("ln2"):
        return nc, P, dbg_outs
    emit_mha("mha")
    if dbg_dump("mha"):
        return nc, P, dbg_outs
    emit_ln("ln3")
    if dbg_dump("ln3"):
        return nc, P, dbg_outs
    emit_routing("r2")
    if dbg_dump("route2"):
        return nc, P, dbg_outs
    emit_tree("k", "KF", "t2k")
    emit_tree("v", "VT", "t2v", scale=0.5)
    emit_tree("q", "XLN", "t2q", scale=1.0 / 32.0)
    if dbg_dump("trees2"):
        return nc, P, dbg_outs
    emit_fff_attn("a2", emit_output=True)
    for cn in ("o0", "o1"):
        s = P._chan(cn)
        P.q["sync"].append(lambda e, h=s[0], v=s[1]: e.wait_ge(h, v))
    dbg_dump("out")
    return nc, P, dbg_outs


def finalize(nc, P):
    with nc.Block() as block:
        @block.sync
        def _(e):
            for fn in P.q["sync"]:
                fn(e)

        @block.tensor
        def _(e):
            for fn in P.q["tensor"]:
                fn(e)

        @block.scalar
        def _(e):
            for fn in P.q["scalar"]:
                fn(e)

        @block.vector
        def _(e):
            for fn in P.q["vector"]:
                fn(e)

        @block.gpsimd
        def _(e):
            for fn in P.q["gpsimd"]:
                fn(e)
    return nc


_BUILD_CACHE = {}


def _get_program(upto=None, dbg=()):
    key = (upto, tuple(dbg))
    if key not in _BUILD_CACHE:
        nc, P, dbg_outs = build(upto=upto, dbg=dbg)
        finalize(nc, P)
        _BUILD_CACHE[key] = (nc, dbg_outs)
    return _BUILD_CACHE[key]


def run(inputs, upto=None, dbg=()):
    per_core = host_prepare(inputs)
    nc, dbg_outs = _get_program(upto=upto, dbg=dbg)
    res = run_bass_kernel_spmd(nc, per_core, core_ids=list(range(8)))
    return res, dbg_outs


def kernel(**inputs):
    res, _ = run(inputs)
    return np.stack([np.asarray(r["y"], np.float32) for r in res.results], axis=0)


# version 06
# speedup vs baseline: 1.0826x; 1.0826x over previous
import numpy as np
import concourse.bass as bass
import concourse.mybir as mybir
from concourse.bass_utils import run_bass_kernel_spmd

F32 = mybir.dt.float32
BF16 = mybir.dt.bfloat16
AF = mybir.ActivationFunctionType
ALU = mybir.AluOpType

S = 2048
D = 1024
NCH = 8
NQB = 4
QB = 512
NTT = 16
NH = 16
EPS = 1e-5

ENGINES = ("sync", "tensor", "scalar", "vector", "gpsimd")

_AUDIT = False
_AUDIT_ENGINES = ()


class Prog:

    def __init__(self, nc):
        self.nc = nc
        self.q = {e: [] for e in ENGINES}
        self.chans = {}

    def _chan(self, name):
        if name not in self.chans:
            self.chans[name] = [self.nc.alloc_semaphore(f"c_{name}"), 0]
        return self.chans[name]

    ENGCH = {"tensor": "T", "scalar": "S", "vector": "V", "gpsimd": "G"}

    def _mkwaits(self, wait):
        wl = []
        for w in wait:
            if isinstance(w, tuple) and len(w) == 3:
                _, src, v = w
                s = self._chan(src[1:] if src.startswith("#") else self.ENGCH[src])
                if v > 0:
                    wl.append((s[0], v))
                continue
            if isinstance(w, tuple):
                src, lag = w
            else:
                src, lag = w, 0
            name = src[1:] if src.startswith("#") else self.ENGCH[src]
            s = self._chan(name)
            v = s[1] - lag
            if v > 0:
                wl.append((s[0], v))
        return wl

    def op(self, eng, fn, wait=(), sig=None):
        wl = self._mkwaits(wait)
        if eng in ("gpsimd", "vector", "scalar") or (_AUDIT and eng != "sync") or eng in _AUDIT_ENGINES:
            s = self._chan(self.ENGCH[eng])
            if s[1] > 0:
                wl.append((s[0], s[1]))
            s[1] += 1
            sigs = [(s[0], 1)]
        elif sig is not None:
            s = self._chan(self.ENGCH[eng])
            s[1] += 1
            sigs = [(s[0], 1)]
        else:
            sigs = []

        def run(e, fn=fn, wl=wl, sigs=sigs):
            for h, v in wl:
                e.wait_ge(h, v)
            ins = fn(e)
            if sigs:
                ins.then_inc(sigs[0][0], sigs[0][1])

        self.q[eng].append(run)

    def count(self, name):
        return self._chan(name[1:] if name.startswith("#") else self.ENGCH[name])[1]

    def wait(self, eng, srcs):
        wl = self._mkwaits([srcs] if isinstance(srcs, (str, tuple)) else srcs)
        if wl:
            def run(e, wl=wl):
                for h, v in wl:
                    e.wait_ge(h, v)
            self.q[eng].append(run)

    def dma(self, out_ap, in_ap, chan, wait=()):
        s = self._chan(chan)
        s[1] += 16
        wl = self._mkwaits(wait)

        def run(e, wl=wl, h=s[0]):
            for hh, v in wl:
                e.wait_ge(hh, v)
            e.dma_start(out=out_ap, in_=in_ap).then_inc(h, 16)

        self.q["sync"].append(run)

    def barrier(self, engines=("tensor", "scalar", "vector", "gpsimd")):
        fence = {
            "tensor": lambda e: e.ldweights(self.fence_ap),
            "scalar": lambda e: e.copy(self.fence_sc[0:1, 1:2], self.fence_src),
            "vector": lambda e: e.memset(self.fence_sc[0:1, 2:3], 0.0),
            "gpsimd": lambda e: e.memset(self.fence_sc[0:1, 3:4], 0.0),
        }
        for a in engines:
            self.op(a, fence[a], wait=(["#G"] if a in ("tensor", "scalar") else []), sig=True)
        for a in engines:
            self.wait(a, [b for b in engines if b != a])


def ceil32(x):
    return (x + 31) & ~31


def host_prepare(inputs):
    g = {}
    for k, v in inputs.items():
        a = np.asarray(v)
        if a.dtype != np.float32:
            a = a.astype(np.float32)
        g[k] = a
    x = g["L0_attn_x"]
    out = {}
    for t in ("q", "k", "v"):
        w1 = g[f"L0_ffn_{t}_w1"]
        w1f = np.ascontiguousarray(w1.transpose(1, 0, 2).reshape(D, D))
        A = w1f.reshape(NCH, 128, NCH, 128)
        out[f"{t}w1"] = np.ascontiguousarray(A.transpose(2, 1, 0, 3).reshape(NCH, 128, D))
        w2f = g[f"L0_ffn_{t}_w2"].reshape(D, D)
        if t == "v":
            out["vw2"] = np.ascontiguousarray(w2f.reshape(NCH, 128, D))
        else:
            C = w2f.reshape(NCH, 128, NCH, 128)
            out[f"{t}w2"] = np.ascontiguousarray(C.transpose(2, 1, 0, 3).reshape(NCH, 128, D))
        out[f"{t}b1"] = np.ascontiguousarray(g[f"L0_ffn_{t}_b1"].reshape(NCH, 128).T)
        out[f"{t}b2"] = np.ascontiguousarray(g[f"L0_ffn_{t}_b2"])
    npk = np.zeros((128, 408), np.float32)
    for ti, t in enumerate(("q", "k", "v")):
        nwT = g[f"L0_ffn_{t}_node_w"].T.reshape(NCH, 128, 15)
        for c in range(NCH):
            npk[:, c * 45 + ti * 15:c * 45 + ti * 15 + 15] = nwT[c]
        npk[0, 360 + ti * 15:360 + ti * 15 + 15] = g[f"L0_ffn_{t}_node_b"]
    out["npk"] = npk
    qwf = np.ascontiguousarray(g["L0_attn_Q_w"].transpose(1, 0, 2).reshape(D, D))
    kwf = np.ascontiguousarray(g["L0_attn_K_w"].transpose(1, 0, 2).reshape(D, D))
    vwf = np.ascontiguousarray(g["L0_attn_V_w"].transpose(1, 0, 2).reshape(D, D))
    owf = np.ascontiguousarray(g["L0_attn_O_w"].reshape(D, D))
    for nm, wf in (("qw", qwf), ("kw", kwf), ("ow", owf)):
        Cw = wf.reshape(NCH, 128, NCH, 128)
        out[nm] = np.ascontiguousarray(Cw.transpose(2, 1, 0, 3).reshape(NCH, 128, D))
    out["vw"] = np.ascontiguousarray(vwf.reshape(NCH, 128, D))
    out["qb8"] = np.ascontiguousarray(g["L0_attn_Q_b"].reshape(D).reshape(NCH, 128).T)
    out["kb8"] = np.ascontiguousarray(g["L0_attn_K_b"].reshape(D).reshape(NCH, 128).T)
    out["ob8"] = np.ascontiguousarray(g["L0_attn_O_b"].reshape(NCH, 128).T)
    out["vbrow"] = np.ascontiguousarray(g["L0_attn_V_b"].reshape(1, D))
    per_core = []
    for b in range(8):
        m = dict(out)
        m["x"] = np.ascontiguousarray(x[b])
        per_core.append(m)
    return per_core


def build(upto=None, dbg=()):
    nc = bass.Bass("TRN2", target_bir_lowering=False, dynamic_dma_scratch_size=64,
                   detect_race_conditions=_AUDIT)
    P = Prog(nc)

    dr = {}
    dr["x"] = nc.dram_tensor("x", [S, D], F32, kind="ExternalInput").ap()
    for t in ("q", "k", "v"):
        for suf, shape in (("w1", [NCH, 128, D]), ("w2", [NCH, 128, D]),
                           ("b1", [128, 8]), ("b2", [16, D])):
            dr[f"{t}{suf}"] = nc.dram_tensor(f"{t}{suf}", shape, F32, kind="ExternalInput").ap()
    dr["npk"] = nc.dram_tensor("npk", [128, 408], F32, kind="ExternalInput").ap()
    for nm in ("qw", "kw", "vw", "ow"):
        dr[nm] = nc.dram_tensor(nm, [NCH, 128, D], F32, kind="ExternalInput").ap()
    for nm in ("qb8", "kb8", "ob8"):
        dr[nm] = nc.dram_tensor(nm, [128, 8], F32, kind="ExternalInput").ap()
    dr["vbrow"] = nc.dram_tensor("vbrow", [1, D], F32, kind="ExternalInput").ap()
    y = nc.dram_tensor("y", [S, D], F32, kind="ExternalOutput").ap()

    SBYTES = (nc.SBUF_PARTITION_SIZE_BYTES or 196608) - 32
    off = 512

    def at(name, shape, dtype, o=None):
        nonlocal off
        fixed = o is not None
        if o is None:
            o = off
        tb = {F32: 4, BF16: 2}[dtype]
        nbytes = int(np.prod(shape[1:])) * tb
        h = nc.alloc_sbuf_tensor_at(name, shape, dtype, offset=o)
        if not fixed:
            off = ceil32(o + nbytes)
        return h

    X = at("X", [128, NCH * S], F32)
    XLN = at("XLN", [128, NCH * S], BF16)
    KF = at("KF", [128, NCH * S], BF16)
    VT = at("VT", [128, NTT * 1040], BF16)
    SCRO = off
    PBUF = at("PBUF", [128, 8192], BF16, SCRO)
    SB = at("SB", [128, 2048], F32, SCRO)
    SFM = at("SFM", [128, 2048], F32, SCRO)
    ATQ = at("ATQ", [128, 4096], BF16, SCRO)
    VWA = at("VWA", [128, 4608], BF16, SCRO)
    XSTG0 = at("XSTG0", [128, 1024], F32, SCRO)
    XSTG1 = at("XSTG1", [128, 1024], F32, SCRO + 4096)
    MSB = at("MSB", [128, 2048], F32, SCRO + 8192)
    SALL = at("SALL", [128, 720], F32, SCRO + 8192)
    PRING = at("PRING", [128, 2048], BF16, SCRO + 8192)
    MUXT = at("MUXT", [128, 192], F32, SCRO + 12288)
    QSTG = at("QSTG", [128, 4096], BF16, SCRO + 12288)
    RROWM = at("RROWM", [1, 512], BF16, SCRO + 13312)
    RBM = at("RBM", [64, 512], F32, SCRO + 14336)
    RROWN = at("RROWN", [1, 512], BF16, SCRO + 12288)
    LEAFB = at("LEAFB", [128, 1024], BF16, SCRO + 16384)
    B2BF = at("B2BF", [16, 1024], BF16, SCRO + 16384)
    SCRN = at("SCRN", [128, 1024], F32, SCRO + 16384)
    RBF = at("RBF", [128, 512], F32, SCRO + 16384)
    E16H = at("E16H", [16, 1024], BF16, SCRO + 18432)
    MSK = at("MSK", [128, 1024], BF16, SCRO + 18432)
    LEAFT = at("LEAFT", [8, 128], BF16, SCRO + 18432)
    TMPF = at("TMPF", [128, 512], F32, SCRO + 18432)
    RROWF = at("RROWF", [1, 512], BF16, SCRO + 18432)
    off = ceil32(SCRO + 20480)
    WS32 = at("WS32", [128, 1024], F32)
    OSTG = at("OSTG", [128, 256], F32, off - 4096)
    WB16 = at("WB16", [128, 2048], BF16)
    ONES1 = at("ONES1", [1, 512], BF16)
    ONESC = at("ONESC", [128, 128], BF16)
    I128B = at("I128B", [128, 128], BF16)
    I128F = at("I128F", [128, 128], F32)
    TRI = at("TRI", [128, 128], BF16)
    IOTA16 = at("IOTA16", [128, 16], F32)
    C2 = at("C2", [128, 8], F32)
    BO16 = at("BO16", [128, 128], BF16)
    B1T = {t: at(f"B1{t}", [128, 8], F32) for t in ("q", "k", "v")}
    QB8 = at("QB8", [128, 8], F32)
    QB8S = at("QB8S", [128, 8], F32)
    KB8 = at("KB8", [128, 8], F32)
    OB8 = at("OB8", [128, 8], F32)
    LTF = at("LTF", [128, 16], F32)
    LT16 = {t: at(f"LT16{t}", [128, 16], BF16) for t in ("q", "k", "v")}
    LTFS = {t: at(f"LTFS{t}", [128, 16], F32) for t in ("q", "k", "v")}
    STAT = at("STAT", [128, 16], F32)
    STAIR = at("STAIR", [128, 32], BF16)
    BN = at("BN", [128, 1], F32)
    BH = at("BH", [128, 16], F32)
    BH16 = at("BH16", [16, 1], BF16)
    ETOK = at("ETOK", [128, 16], BF16)
    ETOK1 = at("ETOK1", [1, 16], BF16)
    GT1 = at("GT1", [128, 128], F32)
    RR32F = at("RR32F", [1, 512], F32)
    EPSC = at("EPSC", [128, 1], F32)
    FEN = at("FEN", [128, 8], F32)
    P.fence_ap = ONESC[0:1, 0:1]
    P.fence_sc = FEN
    P.fence_src = EPSC[0:1, 0:1]
    assert off <= SBYTES, f"SBUF overflow: {off} > {SBYTES}"
    assert SCRO + 20480 <= SBYTES

    PSALL = nc.alloc_psum_tensor("psall", [128, 4096], F32)
    PS = [PSALL[:, i * 512:(i + 1) * 512] for i in range(8)]

    dbg_outs = {}
    named = {"X": X, "XLN": XLN, "KF": KF, "VT": VT, "PBUF": PBUF, "SALL": SALL,
             "SB": SB, "MSB": MSB, "SFM": SFM, "LEAFB": LEAFB, "E16H": E16H,
             "ATQ": ATQ, "BH": BH, "QSTG": QSTG, "STAT": STAT, "BN": BN,
             "LTFq": LTFS["q"], "LTFk": LTFS["k"], "LTFv": LTFS["v"]}

    def dbg_dump(phase):
        if upto == phase or any(p == phase for p, _ in dbg):
            P.barrier()
            P.op("scalar", lambda e: e.copy(P.fence_sc[0:1, 4:5], P.fence_src), wait=["#G"], sig=True)
            for p, name in dbg:
                if p != phase:
                    continue
                t = named[name]
                key = f"dbg_{phase}_{name}"
                shp = list(t.shape)
                d = nc.dram_tensor(key, shp, t.dtype, kind="ExternalOutput").ap()
                P.wait("sync", ["scalar"])
                P.dma(d, t[:], "dbg")
                dbg_outs[key] = tuple(shp)
            P.wait("scalar", ["#dbg"])
        return upto == phase

    cops = [
        lambda e: e.memset(FEN[:], 0.0),
        lambda e: e.memset(VT[:], 0.0),
        lambda e: e.memset(ONES1[:], 1.0),
        lambda e: e.memset(ONESC[:], 1.0),
        lambda e: e.memset(EPSC[:], EPS),
        lambda e: e.iota(I128B[:], [[1, 128]], channel_multiplier=-1,
                         allow_small_or_imprecise_dtypes=True),
        lambda e: e.tensor_single_scalar(I128B[:], I128B[:], 0.0, ALU.is_equal),
        lambda e: e.iota(I128F[:], [[1, 128]], channel_multiplier=-1,
                         allow_small_or_imprecise_dtypes=True),
        lambda e: e.tensor_single_scalar(I128F[:], I128F[:], 0.0, ALU.is_equal),
        lambda e: e.iota(TRI[:], [[1, 128]], channel_multiplier=-1,
                         allow_small_or_imprecise_dtypes=True),
        lambda e: e.tensor_single_scalar(TRI[:], TRI[:], 0.0, ALU.is_ge),
        lambda e: e.iota(IOTA16[:], [[1, 16]], channel_multiplier=0,
                         allow_small_or_imprecise_dtypes=True),
        lambda e: e.iota(GT1[:, 1:2], [[0, 1]], channel_multiplier=1,
                         allow_small_or_imprecise_dtypes=True),
        lambda e: e.tensor_single_scalar(GT1[:, 1:2], GT1[:, 1:2], 64.0, ALU.is_ge),
        lambda e: e.iota(C2[:], [[2, 8]], channel_multiplier=0,
                         allow_small_or_imprecise_dtypes=True),
        lambda e: e.tensor_tensor(C2[:], C2[:], GT1[:, 1:2].broadcast_to([128, 8]), ALU.add),
        lambda e: e.iota(GT1[:, 0:128], [[-2, 8], [1, 16]], channel_multiplier=0,
                         allow_small_or_imprecise_dtypes=True),
        lambda e: e.tensor_tensor(GT1[:, 0:128], GT1[:, 0:128],
                                  GT1[:, 1:2].broadcast_to([128, 128]), ALU.subtract),
        lambda e: e.tensor_single_scalar(BO16[:], GT1[:, 0:128], 0.0, ALU.is_equal),
        lambda e: e.iota(STAIR[:], [[1, 32]], channel_multiplier=0,
                         allow_small_or_imprecise_dtypes=True),
        lambda e: e.tensor_single_scalar(STAIR[:], STAIR[:], 15.0, ALU.is_equal),
    ]
    for i, cop in enumerate(cops):
        P.op("gpsimd", cop,
             sig=(["tensor", "scalar", "vector"] if i == len(cops) - 1 else None))
    P.dma(QB8[:], dr["qb8"], "mc")
    P.dma(KB8[:], dr["kb8"], "mc")
    P.dma(OB8[:], dr["ob8"], "mc")
    for t in ("q", "k", "v"):
        P.dma(B1T[t][:], dr[f"{t}b1"], "mc")
    P.op("vector", lambda e: e.tensor_single_scalar(QB8S[:], QB8[:], 0.125, ALU.mult),
         wait=["#mc", "#G"], sig="scalar")

    xst = [XSTG0, XSTG1]
    for tt in range(NTT):
        sl = xst[tt % 2]
        P.dma(sl[:], dr["x"][tt * 128:(tt + 1) * 128, :], f"x{tt % 2}",
              wait=([("scalar", 1)] if tt >= 2 else []))
        for c in range(NCH):
            pb = PS[c % 2]
            P.op("tensor",
                 lambda e, pb=pb, sl=sl, c=c: e.transpose(
                     pb[:, 0:128], sl[:, c * 128:(c + 1) * 128], I128F[:]),
                 wait=(["#G"] if tt == 0 and c == 0 else []) +
                      ([f"#x{tt % 2}"] if c == 0 else []) +
                      ([("scalar", 1)] if (tt, c) > (0, 1) else []),
                 sig="scalar")
            P.op("scalar",
                 lambda e, pb=pb, c=c, tt=tt: e.copy(
                     X[:, c * S + tt * 128:c * S + tt * 128 + 128], pb[:, 0:128]),
                 wait=["tensor"], sig=["tensor"] + (["sync"] if c == 7 else []))
    if dbg_dump("xin"):
        return nc, P, dbg_outs

    wstate = {"n": 0, "gafter": [0, 0]}

    def wpush(dram_ap, ncols=512, dst16=None, extra_cast_wait=(), colw=None):
        u = wstate["n"]
        wstate["n"] += 1
        i32 = u % 2
        pr = dram_ap.shape[0]
        P.dma(WS32[0:pr, i32 * 512:i32 * 512 + ncols], dram_ap, f"w{i32}",
              wait=[("abs", "#V", wstate["gafter"][i32])])
        if dst16 is None:
            cw = colw or 512
            i16 = u % (2048 // cw)
            dst = WB16[0:pr, i16 * cw:i16 * cw + ncols]
            col = i16 * cw
            cw = ["tensor"]
        else:
            tns, col, prx = dst16
            dst = tns[0:pr, col:col + ncols]
        P.op("vector",
             lambda e, dst=dst, i32=i32, pr=pr, ncols=ncols: e.tensor_copy(
                 dst, WS32[0:pr, i32 * 512:i32 * 512 + ncols]),
             wait=[f"#w{i32}", "tensor"] + list(extra_cast_wait), sig=True)
        wstate["gafter"][i32] = P.count("#V")
        return col

    def emit_ln(tag):
        part = globals().get("_LN_PART")
        P.barrier()
        cast_rec = {}
        for c in range(NCH):
            eng = "gpsimd" if c < 4 else "vector"
            P.op(eng, lambda e, c=c: e.tensor_copy(XLN[:, c * S:(c + 1) * S], X[:, c * S:(c + 1) * S]),
                 sig=True)
            cast_rec[c] = ("#G" if c < 4 else "#V", P.count("#G" if c < 4 else "#V"))
        for qb in range(NQB):
            pm = PS[qb % 2]
            for c in range(NCH):
                P.op("tensor",
                     lambda e, pm=pm, c=c, qb=qb: e.matmul(
                         pm[:], ONESC[:], XLN[:, c * S + qb * QB:c * S + qb * QB + QB],
                         start=(c == 0), stop=(c == NCH - 1)),
                     wait=([("abs",) + cast_rec[c]] if qb == 0 else []) +
                          ([("vector", 1)] if qb >= 2 and c == 0 else []),
                     sig=("vector" if c == NCH - 1 else None))
            P.op("vector",
                 lambda e, pm=pm, qb=qb: e.tensor_single_scalar(
                     MSB[:, qb * QB:(qb + 1) * QB], pm[:], 1.0 / D, ALU.mult),
                 wait=["tensor"], sig="tensor")
        if part == "sums":
            return
        sq_rec = {}
        for c in range(NCH):
            if c < 4:
                P.op("scalar",
                     lambda e, c=c: e.activation(XLN[:, c * S:(c + 1) * S],
                                                 X[:, c * S:(c + 1) * S], AF.Square),
                     wait=(["tensor"] if c == 0 else []), sig=True)
                sq_rec[c] = ("#S", P.count("#S"))
            else:
                P.op("vector",
                     lambda e, c=c: e.tensor_mul(XLN[:, c * S:(c + 1) * S],
                                                 X[:, c * S:(c + 1) * S], X[:, c * S:(c + 1) * S]),
                     wait=(["tensor"] if c == 4 else []), sig=True)
                sq_rec[c] = ("#V", P.count("#V"))
        if part == "sq":
            return
        for qb in range(NQB):
            pv = PS[2 + qb % 2]
            q0, q1 = qb * QB, (qb + 1) * QB
            for c in range(NCH):
                P.op("tensor",
                     lambda e, pv=pv, c=c, qb=qb: e.matmul(
                         pv[:], ONESC[:], XLN[:, c * S + qb * QB:c * S + qb * QB + QB],
                         start=(c == 0), stop=(c == NCH - 1)),
                     wait=([("abs",) + sq_rec[c]] if qb == 0 else []) +
                          ([("vector", 2)] if qb >= 2 and c == 0 else []),
                     sig=("vector" if c == NCH - 1 else None))
            P.op("vector", lambda e, q0=q0, q1=q1: e.tensor_mul(
                SCRN[:, 0:512], MSB[:, q0:q1], MSB[:, q0:q1]))
            P.op("vector",
                 lambda e, pv=pv: e.scalar_tensor_tensor(
                     SCRN[:, 512:1024], pv[:], 1.0 / D, SCRN[:, 0:512],
                     ALU.mult, ALU.subtract),
                 wait=["tensor"], sig=["tensor", "scalar"])
            P.op("scalar", lambda e: e.activation(SCRN[:, 0:512], SCRN[:, 512:1024],
                                                  AF.Ln, bias=EPSC[:]),
                 wait=["vector"])
            P.op("scalar", lambda e, q0=q0, q1=q1: e.activation(
                SB[:, q0:q1], SCRN[:, 0:512], AF.Exp, scale=-0.5), sig="vector")
            P.op("vector", lambda e, q0=q0, q1=q1: e.tensor_mul(
                MSB[:, q0:q1], MSB[:, q0:q1], SB[:, q0:q1]),
                wait=["scalar"], sig="gpsimd")
        if part == "stats":
            return
        P.wait("gpsimd", ["vector"])
        pi = 0
        for c in range(NCH):
            for hh in range(4):
                o = c * S + hh * 512
                so = hh * 512 if False else (hh % 4) * 512
                so = (hh * 512) % 2048
                sl = (pi % 2) * 512
                P.op("gpsimd",
                     lambda e, o=o, so=so, sl=sl: e.tensor_mul(
                         SCRN[:, sl:sl + 512], X[:, o:o + 512], SB[:, so:so + 512]),
                     wait=([("vector", 1)] if pi >= 2 else []))
                P.op("vector",
                     lambda e, o=o, so=so, sl=sl: e.tensor_sub(
                         XLN[:, o:o + 512], SCRN[:, sl:sl + 512], MSB[:, so:so + 512]),
                     wait=["#G"],
                     sig=["gpsimd"] + (["tensor", "scalar"] if pi == 31 else []))
                pi += 1

    def emit_routing(tag):
        P.barrier()
        co = wpush(dr["npk"], ncols=408)
        for qb in range(NQB):
            pb = PS[4]
            for c in range(NCH):
                P.op("tensor",
                     lambda e, pb=pb, c=c, qb=qb, co=co: e.matmul(
                         pb[0:45, :], WB16[0:128, co + c * 45:co + c * 45 + 45],
                         XLN[:, c * S + qb * QB:c * S + qb * QB + QB],
                         start=(c == 0), stop=False),
                     wait=(["vector"] if qb == 0 and c == 0 else []) +
                          (["scalar"] if qb > 0 and c == 0 else []))
            P.op("tensor",
                 lambda e, pb=pb, co=co: e.matmul(
                     pb[0:45, :], WB16[0:1, co + 360:co + 405], ONES1[:],
                     start=False, stop=True),
                 sig="scalar")
            P.op("scalar",
                 lambda e, pb=pb, qb=qb: e.copy(SFM[0:45, qb * QB:(qb + 1) * QB], pb[0:45, :]),
                 wait=["tensor"], sig=["tensor"] + (["gpsimd"] if qb == NQB - 1 else []))
        for tt in range(NTT):
            pb = PS[5]
            P.op("tensor",
                 lambda e, pb=pb, tt=tt: e.transpose(
                     pb[0:128, 0:45], SFM[0:45, tt * 128:(tt + 1) * 128],
                     I128F[0:45, 0:45]),
                 wait=["scalar"], sig="scalar")
            P.op("scalar",
                 lambda e, pb=pb, tt=tt: e.copy(SALL[:, tt * 45:tt * 45 + 45], pb[0:128, 0:45]),
                 wait=["tensor"], sig=["tensor"] + (["vector"] if tt == NTT - 1 else []))

        def scol(col):
            return SALL[:, col:col + 45 * 15 + 1:45]

        def mcol(i):
            return MUXT[:, i * 16:(i + 1) * 16]

        for ti, t in enumerate(("q", "k", "v")):
            b0, b1, b2, b3 = mcol(0), mcol(1), mcol(2), mcol(3)
            t0, t1, t2, t3 = mcol(4), mcol(5), mcol(6), mcol(7)
            u0, u1 = mcol(8), mcol(9)
            base = ti * 15

            def V(fn, **kw):
                P.op("vector", fn, **kw)

            def mux(dst, a, b_, sel):
                V(lambda e, dst=dst, a=a, b_=b_: e.tensor_sub(dst, b_, a))
                V(lambda e, dst=dst, sel=sel: e.tensor_mul(dst, dst, sel))
                V(lambda e, dst=dst, a=a: e.tensor_add(dst, dst, a))

            V(lambda e, base=base, b0=b0: e.tensor_single_scalar(b0, scol(base + 0), 0.0, ALU.is_ge),
              wait=(["scalar"] if ti == 0 else []))
            mux(t0, scol(base + 1), scol(base + 2), b0)
            V(lambda e, b1=b1, t0=t0: e.tensor_single_scalar(b1, t0, 0.0, ALU.is_ge))
            mux(t0, scol(base + 3), scol(base + 4), b1)
            mux(t1, scol(base + 5), scol(base + 6), b1)
            mux(t2, t0, t1, b0)
            V(lambda e, b2=b2, t2=t2: e.tensor_single_scalar(b2, t2, 0.0, ALU.is_ge))
            mux(t0, scol(base + 7), scol(base + 8), b2)
            mux(t1, scol(base + 9), scol(base + 10), b2)
            mux(t2, scol(base + 11), scol(base + 12), b2)
            mux(t3, scol(base + 13), scol(base + 14), b2)
            mux(u0, t0, t1, b1)
            mux(u1, t2, t3, b1)
            mux(t0, u0, u1, b0)
            V(lambda e, b3=b3, t0=t0: e.tensor_single_scalar(b3, t0, 0.0, ALU.is_ge))
            V(lambda e, b0=b0, b1=b1: e.scalar_tensor_tensor(LTF[:], b0, 2.0, b1, ALU.mult, ALU.add))
            V(lambda e, b2=b2: e.scalar_tensor_tensor(LTF[:], LTF[:], 2.0, b2, ALU.mult, ALU.add))
            V(lambda e, b3=b3, t=t: e.scalar_tensor_tensor(LTFS[t][:], LTF[:], 2.0, b3, ALU.mult, ALU.add))
            V(lambda e, t=t: e.tensor_copy(LT16[t][:], LTFS[t][:]),
              sig=(["tensor", "scalar", "gpsimd"] if t == "v" else None))

    def emit_tree(tree, dst, tag, scale=1.0):
        P.barrier()
        for th in range(2):
            P.op("tensor",
                 lambda e, th=th, tree=tree: e.matmul(
                     PS[6][0:8, 0:128], LT16[tree][:, th * 8:th * 8 + 8], I128B[:],
                     start=True, stop=True),
                 wait=(["vector", "scalar", "#G"] if th == 1 else []),
                 sig="scalar")
            P.op("scalar", lambda e: e.copy(LEAFT[:], PS[6][0:8, 0:128]),
                 wait=["tensor"], sig="tensor")
            for g in range(2):
                for j in range(4):
                    tt8 = g * 4 + j
                    P.op("tensor",
                         lambda e, tt8=tt8, j=j: e.matmul(
                             PS[5][:, j * 128:(j + 1) * 128],
                             I128B[0:8, tt8:tt8 + 1].broadcast_to([8, 128]), LEAFT[:],
                             start=True, stop=True),
                         wait=(["scalar"] if tt8 == 0 else []) +
                              (["scalar"] if g == 1 and j == 0 else []),
                         sig=("scalar" if j == 3 else None))
                P.op("scalar",
                     lambda e, g=g: e.copy(LEAFB[:, g * 512:(g + 1) * 512], PS[5][:]),
                     wait=["tensor"], sig=["tensor", "vector"])
            wcols = {}
            wa = wpush(dr[f"{tree}w1"][0, :, 0:512])
            wb = wpush(dr[f"{tree}w1"][0, :, 512:1024])
            wcols[0] = (wa, wb, P.count("#V"))
            for l in range(NCH):
                if l + 1 < NCH:
                    wa = wpush(dr[f"{tree}w1"][l + 1, :, 0:512])
                    wb = wpush(dr[f"{tree}w1"][l + 1, :, 512:1024])
                    wcols[l + 1] = (wa, wb, P.count("#V"))
                wa, wb, vw1 = wcols[l]
                for qh in range(2):
                    qb = th * 2 + qh
                    gi = (2 * l + qh) % 4
                    pb = PS[gi]
                    for c in range(NCH):
                        colof = (wa if c < 4 else wb) + (c % 4) * 128
                        P.op("tensor",
                             lambda e, pb=pb, c=c, qb=qb, colof=colof: e.matmul(
                                 pb[:], WB16[:, colof:colof + 128],
                                 XLN[:, c * S + qb * QB:c * S + qb * QB + QB],
                                 start=(c == 0), stop=(c == NCH - 1)),
                             wait=([("abs", "#V", vw1)] if c == 0 else []) +
                                  ([("scalar", 3)] if c == 0 else []),
                             sig=("scalar" if c == NCH - 1 else None))
                    P.op("scalar",
                         lambda e, pb=pb, l=l, qh=qh: e.activation(
                             PBUF[:, l * 1024 + qh * 512:l * 1024 + qh * 512 + 512],
                             pb[:], AF.Relu, bias=B1T[tree][:, l:l + 1]),
                         wait=["tensor"] + (["#mc"] if l == 0 and qh == 0 and th == 0 else []),
                         sig=["tensor"] + (["vector"] if qh == 1 else []))
                P.op("vector",
                     lambda e, l=l: e.tensor_single_scalar(MSK[:], LEAFB[:], C2[:, l:l + 1], ALU.is_equal),
                     wait=["scalar"])
                P.op("vector",
                     lambda e, l=l: e.tensor_mul(PBUF[:, l * 1024:(l + 1) * 1024],
                                                 PBUF[:, l * 1024:(l + 1) * 1024], MSK[:]),
                     sig=["tensor"])
            for g in range(2):
                for j in range(4):
                    tt8 = g * 4 + j
                    P.op("vector",
                         lambda e, tt8=tt8, th=th, tree=tree: e.tensor_single_scalar(
                             ETOK[:], IOTA16[:],
                             LTFS[tree][:, th * 8 + tt8:th * 8 + tt8 + 1], ALU.is_equal),
                         wait=([("tensor", 0)] if tt8 > 0 else []), sig="tensor")
                    P.op("tensor",
                         lambda e, j=j: e.matmul(PS[6][0:16, j * 128:(j + 1) * 128],
                                                 ETOK[:], I128B[:], start=True, stop=True),
                         wait=["vector"] + (["scalar"] if j == 0 else []),
                         sig=["vector"] + (["scalar"] if j == 3 else []))
                P.op("scalar",
                     lambda e, g=g: e.copy(E16H[:, g * 512:(g + 1) * 512], PS[6][0:16, :]),
                     wait=["tensor"], sig="tensor")
            for half in range(2):
                wpush(dr[f"{tree}b2"][:, half * 512:half * 512 + 512],
                      dst16=(B2BF, half * 512, 16), extra_cast_wait=["vector", "scalar"])
            if dst == "VT":
                for g in range(4):
                    slots = []
                    for lcg in range(2):
                        for lcq in range(4):
                            lc = lcg * 4 + lcq
                            u = wpush(dr["vw2"][lc, :, g * 256:(g + 1) * 256], ncols=256, colw=256)
                            slots.append(u)
                    for lcg in range(2):
                        for tt8 in range(8):
                            tt = th * 8 + tt8
                            pb = PS[tt8]
                            for lcq in range(4):
                                lc = lcg * 4 + lcq
                                P.op("tensor",
                                     lambda e, pb=pb, lc=lc, tt8=tt8, g=g,
                                            cu=slots[lc]: e.matmul(
                                         pb[:, 0:256],
                                         PBUF[:, lc * 1024 + tt8 * 128:lc * 1024 + tt8 * 128 + 128],
                                         WB16[:, cu:cu + 256],
                                         start=(lc == 0), stop=False),
                                     wait=(["vector"] if lc == 0 and tt8 == 0 and lcg == 0 else []) +
                                          (["scalar"] if lc == 0 and lcg == 0 and tt8 == 0 and g > 0 else []))
                            if lcg == 1:
                                P.op("tensor",
                                     lambda e, pb=pb, tt8=tt8, g=g: e.matmul(
                                         pb[:, 0:256],
                                         E16H[:, tt8 * 128:tt8 * 128 + 128],
                                         B2BF[:, g * 256:(g + 1) * 256],
                                         start=False, stop=True),
                                     sig="scalar")
                                P.op("scalar",
                                     lambda e, pb=pb, tt=tt, g=g: e.activation(
                                         VT[:, tt * 1040 + g * 256:tt * 1040 + g * 256 + 256],
                                         pb[:, 0:256], AF.Copy, scale=scale),
                                     wait=["tensor"],
                                     sig=["tensor"] + (["gpsimd"] if tt8 >= 6 else []))
            else:
                dtens = KF if dst == "KF" else XLN
                w2cols = {}
                wa = wpush(dr[f"{tree}w2"][0, :, 0:512])
                wb = wpush(dr[f"{tree}w2"][0, :, 512:1024])
                w2cols[0] = (wa, wb, P.count("#V"))
                for oc in range(NCH):
                    if oc + 1 < NCH:
                        wa = wpush(dr[f"{tree}w2"][oc + 1, :, 0:512])
                        wb = wpush(dr[f"{tree}w2"][oc + 1, :, 512:1024])
                        w2cols[oc + 1] = (wa, wb, P.count("#V"))
                    wa, wb, vw2 = w2cols[oc]
                    for qh in range(2):
                        qb = th * 2 + qh
                        pb = PS[(2 * oc + qh) % 4]
                        for lc in range(NCH):
                            colof = (wa if lc < 4 else wb) + (lc % 4) * 128
                            P.op("tensor",
                                 lambda e, pb=pb, lc=lc, qh=qh, colof=colof: e.matmul(
                                     pb[:], WB16[:, colof:colof + 128],
                                     PBUF[:, lc * 1024 + qh * 512:lc * 1024 + qh * 512 + 512],
                                     start=(lc == 0), stop=False),
                                 wait=([("abs", "#V", vw2)] if lc == 0 else []) +
                                      ([("scalar", 3)] if lc == 0 else []))
                        P.op("tensor",
                             lambda e, pb=pb, oc=oc, qh=qh: e.matmul(
                                 pb[:], B2BF[:, oc * 128:(oc + 1) * 128],
                                 E16H[:, qh * 512:qh * 512 + 512],
                                 start=False, stop=True),
                             sig="scalar")
                        P.op("scalar",
                             lambda e, pb=pb, oc=oc, qb=qb, dtens=dtens: e.activation(
                                 dtens[:, oc * S + qb * QB:oc * S + qb * QB + QB],
                                 pb[:], AF.Copy, scale=scale),
                             wait=["tensor"],
                             sig=["tensor"] + (["gpsimd"] if qh == 1 else []))

    def emit_fff_attn(tag, emit_output=False):
        P.barrier()
        for wi, TS in ((0, XLN), (1, KF)):
            for qb in range(NQB):
                for c in range(NCH):
                    P.op("vector",
                         lambda e, TS=TS, c=c, qb=qb: e.tensor_mul(
                             PBUF[:, (c % 4) * 512:(c % 4) * 512 + 512],
                             TS[:, c * S + qb * QB:c * S + qb * QB + QB],
                             TS[:, c * S + qb * QB:c * S + qb * QB + QB]),
                         wait=([("tensor", 3)] if (c >= 4 or qb > 0 or wi > 0) else []),
                         sig="tensor")
                    P.op("tensor",
                         lambda e, c=c: e.matmul(
                             PS[4][0:1, :], ONESC[:, 0:1],
                             PBUF[:, (c % 4) * 512:(c % 4) * 512 + 512],
                             start=(c == 0), stop=(c == NCH - 1)),
                         wait=["vector"], sig=True)
                P.op("vector",
                     lambda e: e.reduce_max(STAT[0:1, 1:2], PS[4][0:1, :], mybir.AxisListType.X),
                     wait=["tensor"], sig="tensor")
                if qb == 0:
                    P.op("vector", lambda e, wi=wi: e.tensor_copy(
                        STAT[0:1, 3 + wi:4 + wi], STAT[0:1, 1:2]))
                else:
                    P.op("vector", lambda e, wi=wi: e.tensor_max(
                        STAT[0:1, 3 + wi:4 + wi], STAT[0:1, 3 + wi:4 + wi], STAT[0:1, 1:2]))
        P.op("vector", lambda e: e.tensor_mul(STAT[0:1, 5:6], STAT[0:1, 3:4], STAT[0:1, 4:5]),
             sig="scalar")
        P.op("scalar", lambda e: e.activation(STAT[0:1, 6:7], STAT[0:1, 5:6], AF.Ln),
             wait=["vector"])
        P.op("scalar", lambda e: e.activation(STAT[0:1, 7:8], STAT[0:1, 6:7], AF.Exp, scale=0.5),
             sig="vector")
        P.op("vector", lambda e: e.tensor_single_scalar(STAT[0:1, 8:9], STAT[0:1, 7:8], -1.0, ALU.mult),
             wait=["scalar"])
        P.op("vector", lambda e: e.tensor_copy(ETOK1[0:1, 0:1], STAT[0:1, 8:9]), sig="tensor")
        P.op("tensor", lambda e: e.matmul(PS[5][:, 0:1], ONESC[0:1, :], ETOK1[0:1, 0:1],
                                          start=True, stop=True),
             wait=["vector"], sig="scalar")
        P.op("scalar", lambda e: e.copy(BN[:], PS[5][:, 0:1]), wait=["tensor"],
             sig=["tensor", "vector"])

        for qb in range(NQB):
            for kt in range(NTT):
                pb = PS[kt % 3]
                for c in range(NCH):
                    P.op("tensor",
                         lambda e, pb=pb, c=c, kt=kt, qb=qb: e.matmul(
                             pb[:], KF[:, c * S + kt * 128:c * S + kt * 128 + 128],
                             XLN[:, c * S + qb * QB:c * S + qb * QB + QB],
                             start=(c == 0), stop=(c == NCH - 1)),
                         wait=([("scalar", 2)] if c == 0 and kt >= 3 else []) +
                              (["vector", "scalar"] if c == 0 and kt == 0 and qb == 0 else []) +
                              (["vector"] if c == 0 and kt == 0 and qb > 0 else []),
                         sig=("scalar" if c == NCH - 1 else None))
                if kt > 0:
                    P.op("tensor",
                         lambda e, kt=kt: e.matmul(
                             PS[4][0:1, :], ONESC[:, 0:1],
                             PBUF[:, (kt - 1) * 512:kt * 512],
                             start=(kt - 1 == 0), stop=False),
                         wait=["scalar"])
                P.op("scalar",
                     lambda e, pb=pb, kt=kt: e.activation(
                         PBUF[:, kt * 512:(kt + 1) * 512], pb[:], AF.Exp, bias=BN[:]),
                     wait=["tensor"], sig="tensor")
            P.op("tensor",
                 lambda e: e.matmul(PS[4][0:1, :], ONESC[:, 0:1],
                                    PBUF[:, (NTT - 1) * 512:NTT * 512],
                                    start=False, stop=True),
                 wait=["scalar"], sig=True)
            P.op("vector", lambda e: e.reciprocal(RR32F[:], PS[4][0:1, :]),
                 wait=[("tensor", 0), "#G"], sig="tensor")
            P.op("vector", lambda e: e.tensor_copy(RROWF[:], RR32F[:]), sig="tensor")
            P.op("tensor", lambda e: e.matmul(PS[5][:], ONES1[0:1, 0:128], RROWF[:],
                                              start=True, stop=True),
                 wait=["vector", ("scalar", 0)], sig="scalar")
            P.op("scalar", lambda e: e.copy(RBF[:], PS[5][:]), wait=["tensor"],
                 sig=["tensor", "vector"])
            for dc in range(NCH):
                pb = PS[3 if dc % 2 == 0 else 7]
                for kt in range(NTT):
                    P.op("tensor",
                         lambda e, pb=pb, dc=dc, kt=kt: e.matmul(
                             pb[:], VT[:, kt * 1040 + dc * 128:kt * 1040 + dc * 128 + 128],
                             PBUF[:, kt * 512:(kt + 1) * 512],
                             start=(kt == 0), stop=(kt == NTT - 1)),
                         wait=([("vector", 2)] if kt == 0 and dc >= 2 else []),
                         sig=("vector" if kt == NTT - 1 else None))
                o = dc * S + qb * QB
                P.op("vector",
                     lambda e, pb=pb: e.tensor_mul(TMPF[:], pb[:], RBF[:]),
                     wait=["tensor", "#G"] + (["scalar"] if dc == 0 else []),
                     sig=["tensor", "gpsimd"])
                P.op("gpsimd",
                     lambda e, o=o: e.tensor_add(X[:, o:o + 512], X[:, o:o + 512], TMPF[:]),
                     wait=["vector"],
                     sig=["vector"] + (["tensor"] if emit_output else []))
                if emit_output:
                    for j in range(4):
                        P.op("tensor",
                             lambda e, o=o, j=j: e.transpose(
                                 PS[6][:, 0:128], X[:, o + j * 128:o + j * 128 + 128], I128F[:]),
                             wait=["#G"] + (["scalar"] if (dc, j) > (0, 0) or qb > 0 else []),
                             sig="scalar")
                        P.op("scalar",
                             lambda e, j=j: e.copy(OSTG[:, (j % 2) * 128:(j % 2) * 128 + 128],
                                                   PS[6][:, 0:128]),
                             wait=["tensor", f"#o{j % 2}"], sig=["tensor", "sync"])
                        P.dma(y[qb * QB + j * 128:qb * QB + j * 128 + 128,
                                dc * 128:dc * 128 + 128],
                              OSTG[:, (j % 2) * 128:(j % 2) * 128 + 128], f"o{j % 2}",
                              wait=["scalar"])

    def emit_mha(tag):
        P.barrier()
        kcols = {}
        wa = wpush(dr["kw"][0, :, 0:512])
        wb = wpush(dr["kw"][0, :, 512:1024])
        kcols[0] = (wa, wb, P.count("#V"))
        for oc in range(NCH):
            if oc + 1 < NCH:
                wa = wpush(dr["kw"][oc + 1, :, 0:512])
                wb = wpush(dr["kw"][oc + 1, :, 512:1024])
                kcols[oc + 1] = (wa, wb, P.count("#V"))
            wa, wb, vwk = kcols[oc]
            for qb in range(NQB):
                pb = PS[(4 * oc + qb) % 4]
                for mc in range(NCH):
                    colof = (wa if mc < 4 else wb) + (mc % 4) * 128
                    P.op("tensor",
                         lambda e, pb=pb, mc=mc, qb=qb, colof=colof: e.matmul(
                             pb[:], WB16[:, colof:colof + 128],
                             XLN[:, mc * S + qb * QB:mc * S + qb * QB + QB],
                             start=(mc == 0), stop=(mc == NCH - 1)),
                         wait=([("abs", "#V", vwk)] if mc == 0 and qb == 0 else []) +
                              ([("scalar", 3)] if mc == 0 else []),
                         sig=("scalar" if mc == NCH - 1 else None))
                P.op("scalar",
                     lambda e, pb=pb, oc=oc, qb=qb: e.add(
                         KF[:, oc * S + qb * QB:oc * S + qb * QB + QB], pb[:],
                         KB8[:, oc:oc + 1]),
                     wait=["tensor"],
                     sig=["tensor"] + (["gpsimd"] if qb == NQB - 1 else []))
        P.op("gpsimd", lambda e: e.memset(
            VT[:].rearrange("p (t h) -> p t h", h=1040)[:, :, 64:1040:65], 1.0),
            wait=["tensor", "scalar"], sig="tensor")
        for half in range(2):
            for mc in range(NCH):
                wpush(dr["vw"][mc, :, half * 512:half * 512 + 512],
                      dst16=(VWA, mc * 512, 128),
                      extra_cast_wait=(["tensor"] if half == 1 else []))
            wpush(dr["vbrow"][0:1, half * 512:half * 512 + 512],
                  dst16=(VWA, 4096, 1))
            vwv = P.count("#V")
            for tt in range(NTT):
                pb = PS[tt % 4]
                for mc in range(NCH):
                    P.op("tensor",
                         lambda e, pb=pb, mc=mc, tt=tt: e.matmul(
                             pb[:], XLN[:, mc * S + tt * 128:mc * S + tt * 128 + 128],
                             VWA[:, mc * 512:(mc + 1) * 512],
                             start=(mc == 0), stop=False),
                         wait=([("abs", "#V", vwv)] if mc == 0 and tt == 0 else []) +
                              ([("scalar", 3)] if mc == 0 and tt > 0 else []))
                P.op("tensor",
                     lambda e, pb=pb: e.matmul(pb[:], ONES1[0:1, 0:128], VWA[0:1, 4096:4608],
                                               start=False, stop=True),
                     sig="scalar")
                P.op("scalar",
                     lambda e, pb=pb, tt=tt, half=half: e.copy(
                         VT[:].rearrange("p (t h) -> p t h", h=1040)
                         [:, tt, half * 520:half * 520 + 520]
                         .rearrange("p (g c) -> p g c", c=65)[:, :, 0:64],
                         pb[:]),
                     wait=["tensor"],
                     sig=["tensor"] + (["gpsimd"] if tt >= NTT - 2 else []))
        for qb in range(NQB):
            qev = []
            qcols = {}
            wa = wpush(dr["qw"][0, :, 0:512])
            wb = wpush(dr["qw"][0, :, 512:1024])
            qcols[0] = (wa, wb, P.count("#V"))
            for oc in range(NCH):
                if oc + 1 < NCH:
                    wa = wpush(dr["qw"][oc + 1, :, 0:512])
                    wb = wpush(dr["qw"][oc + 1, :, 512:1024])
                    qcols[oc + 1] = (wa, wb, P.count("#V"))
                wa, wb, vwq = qcols[oc]
                pb = PS[oc % 4]
                for mc in range(NCH):
                    colof = (wa if mc < 4 else wb) + (mc % 4) * 128
                    P.op("tensor",
                         lambda e, pb=pb, mc=mc, qb=qb, colof=colof: e.matmul(
                             pb[:], WB16[:, colof:colof + 128],
                             XLN[:, mc * S + qb * QB:mc * S + qb * QB + QB],
                             start=(mc == 0), stop=(mc == NCH - 1)),
                         wait=([("abs", "#V", vwq)] if mc == 0 else []) +
                              ([("abs", "#V", qev[oc - 4])] if mc == 0 and oc >= 4 else []) +
                              (["vector"] if mc == 0 and oc == 0 and qb > 0 else []),
                         sig=("vector" if mc == NCH - 1 else None))
                P.op("vector",
                     lambda e, pb=pb, oc=oc: e.tensor_scalar(
                         QSTG[:, oc * 512:(oc + 1) * 512], pb[:],
                         0.125, QB8S[:, oc:oc + 1], ALU.mult, ALU.add),
                     wait=["tensor"], sig=True)
                qev.append(P.count("#V"))
            for oc in range(NCH):
                P.op("vector",
                     lambda e, oc=oc, qb=qb: e.tensor_copy(
                         XLN[:, oc * S + qb * QB:oc * S + qb * QB + QB],
                         QSTG[:, oc * 512:(oc + 1) * 512]),
                     sig=(["tensor"] if oc == NCH - 1 else None))
        for wi, TS in ((0, XLN), (1, KF)):
            for qb in range(NQB):
                for c in range(NCH):
                    P.op("vector",
                         lambda e, TS=TS, c=c, qb=qb: e.tensor_mul(
                             PBUF[:, (c % 4) * 512:(c % 4) * 512 + 512],
                             TS[:, c * S + qb * QB:c * S + qb * QB + QB],
                             TS[:, c * S + qb * QB:c * S + qb * QB + QB]),
                         wait=([("tensor", 3)] if (c >= 4 or qb > 0 or wi > 0) else []) +
                              (["scalar"] if wi == 0 and qb == 0 and c == 0 else []),
                         sig="tensor")
                    P.op("tensor",
                         lambda e, c=c: e.matmul(
                             PS[4][0:16, :], BO16[:, c * 16:c * 16 + 16],
                             PBUF[:, (c % 4) * 512:(c % 4) * 512 + 512],
                             start=(c == 0), stop=(c == NCH - 1)),
                         wait=["vector"], sig=True)
                P.op("vector",
                     lambda e: e.reduce_max(STAT[0:16, 1:2], PS[4][0:16, :], mybir.AxisListType.X),
                     wait=["tensor"], sig="tensor")
                if qb == 0:
                    P.op("vector", lambda e, wi=wi: e.tensor_copy(
                        STAT[0:16, 4 + wi:5 + wi], STAT[0:16, 1:2]))
                else:
                    P.op("vector", lambda e, wi=wi: e.tensor_max(
                        STAT[0:16, 4 + wi:5 + wi], STAT[0:16, 4 + wi:5 + wi], STAT[0:16, 1:2]))
        P.op("vector", lambda e: e.tensor_mul(STAT[0:16, 6:7], STAT[0:16, 4:5], STAT[0:16, 5:6]),
             sig="scalar")
        P.op("scalar", lambda e: e.activation(STAT[0:16, 7:8], STAT[0:16, 6:7], AF.Ln),
             wait=["vector"])
        P.op("scalar", lambda e: e.activation(STAT[0:16, 8:9], STAT[0:16, 7:8], AF.Exp, scale=0.5),
             sig="vector")
        P.op("vector", lambda e: e.tensor_single_scalar(STAT[0:16, 9:10], STAT[0:16, 8:9],
                                                        -1.0, ALU.mult),
             wait=["scalar"])
        P.op("vector", lambda e: e.tensor_copy(BH16[:], STAT[0:16, 9:10]), sig="tensor")
        for h in range(NH):
            P.op("tensor",
                 lambda e, h=h: e.matmul(PS[5][:, h:h + 1],
                                         I128B[0:16, h:h + 1].broadcast_to([16, 128]),
                                         BH16[:], start=True, stop=True),
                 wait=(["vector"] if h == 0 else []),
                 sig=("scalar" if h == NH - 1 else None))
        P.op("scalar", lambda e: e.copy(BH[:], PS[5][:, 0:16]), wait=["tensor"],
             sig=["tensor", "vector"])

        P.wait("tensor", ["vector", "scalar", "#G"])
        for qb in range(NQB):
            nkt = 4 * qb + 4
            tile_ctr = [0]
            bank_rec = {0: 0, 1: 0, 2: 0, 3: 0}
            pair_tog = [0]

            def stile(h, kts, c0s):
                hc, hp = h // 2, (h % 2) * 64
                if len(kts) == 2:
                    b0 = pair_tog[0] * 2
                    pair_tog[0] ^= 1
                    banks = [b0, b0 + 1]
                    if tile_ctr[0] % 2:
                        tile_ctr[0] += 1
                else:
                    banks = [tile_ctr[0] % 4]
                slots = []
                for kt, c0, b in zip(kts, c0s, [banks[0], banks[-1]]):
                    sl = tile_ctr[0] % 4
                    tile_ctr[0] += 1
                    slots.append(sl)
                    w = []
                    if bank_rec[b] > 0:
                        w.append(("abs", "#S", bank_rec[b]))
                    P.op("tensor",
                         lambda e, b=b, hc=hc, hp=hp, kt=kt, qb=qb, c0=c0: e.matmul(
                             PS[b][:, c0:512],
                             KF[hp:hp + 64, hc * S + kt * 128:hc * S + kt * 128 + 128],
                             XLN[hp:hp + 64, hc * S + qb * QB + c0:hc * S + qb * QB + QB],
                             start=True, stop=True),
                         wait=w, sig=(None if (len(kts) == 2 and kt == kts[0]) else "scalar"))
                if len(kts) == 2:
                    b0, sl0 = banks[0], slots[0]
                    assert sl0 % 2 == 0
                    P.op("scalar",
                         lambda e, b0=b0, sl0=sl0, h=h: e.activation(
                             PRING[:, sl0 * 512:sl0 * 512 + 1024],
                             PSALL[:, b0 * 512:b0 * 512 + 1024],
                             AF.Exp, bias=BH[:, h:h + 1]),
                         wait=["tensor"], sig=["tensor"])
                    bank_rec[banks[0]] = bank_rec[banks[1]] = P.count("#S")
                else:
                    b, sl, c0 = banks[0], slots[0], c0s[0]
                    P.op("scalar",
                         lambda e, b=b, sl=sl, c0=c0, h=h: e.activation(
                             PRING[:, sl * 512 + c0:sl * 512 + 512], PS[b][:, c0:512],
                             AF.Exp, bias=BH[:, h:h + 1]),
                         wait=["tensor"], sig=["tensor"])
                    bank_rec[b] = P.count("#S")
                return slots

            def tri_mask(slots, c0s):
                sl, c0 = slots[0], c0s[0]
                P.op("vector",
                     lambda e, sl=sl, c0=c0: e.tensor_mul(
                         PRING[:, sl * 512 + c0:sl * 512 + c0 + 128],
                         PRING[:, sl * 512 + c0:sl * 512 + c0 + 128], TRI[:]),
                     wait=["scalar"], sig="tensor")

            def pvgrp(h, kts, c0s, slots, first, last):
                pvb = PS[6] if h % 2 == 0 else PS[7]
                diag = c0s[0] > 0 or kts[0] >= 4 * qb
                for kt, c0, sl in zip(kts, c0s, slots):
                    P.op("tensor",
                         lambda e, pvb=pvb, h=h, kt=kt, c0=c0, sl=sl,
                                st=(first and kt == kts[0]),
                                sp=(last and kt == kts[-1]): e.matmul(
                             pvb[0:65, c0:512],
                             VT[:, kt * 1040 + h * 65:kt * 1040 + h * 65 + 65],
                             PRING[:, sl * 512 + c0:sl * 512 + 512],
                             start=st, stop=sp),
                         wait=((["vector"] if diag else ["scalar"]) if kt == kts[0] else []) +
                              (["vector"] if first and kt == kts[0] else []),
                         sig=("scalar" if last and kt == kts[-1] else None))

            pend = []

            def flush_norm():
                for h, pvb, rt in pend:
                    hc, hp = h // 2, (h % 2) * 64
                    P.op("tensor",
                         lambda e, rt=rt: e.matmul(
                             PS[5][0:64, :], ONES1[0:1, 0:64], rt[:],
                             start=True, stop=True),
                         wait=["vector"], sig="scalar")
                    P.op("scalar",
                         lambda e: e.copy(RBM[:], PS[5][0:64, :]),
                         wait=["tensor", "vector"], sig=["tensor", "vector"])
                    P.op("vector",
                         lambda e, pvb=pvb, hc=hc, hp=hp: e.tensor_mul(
                             ATQ[hp:hp + 64, hc * 512:(hc + 1) * 512], pvb[0:64, :], RBM[:]),
                         wait=["scalar"], sig=["tensor"])
                del pend[:]

            for hp2 in range(NH // 2):
                h0, h1 = 2 * hp2, 2 * hp2 + 1
                flush_norm()
                chunks = []
                for k2 in range(0, 4 * qb, 2):
                    chunks.append(([k2, k2 + 1], [0, 0]))
                for j in range(4):
                    chunks.append(([4 * qb + j], [128 * j]))
                nch_ = len(chunks)
                for ci, (kts, c0s) in enumerate(chunks):
                    sl_h0 = stile(h0, kts, c0s)
                    if len(kts) == 1:
                        tri_mask(sl_h0, c0s)
                    sl_h1 = stile(h1, kts, c0s)
                    if len(kts) == 1:
                        tri_mask(sl_h1, c0s)
                    pvgrp(h0, kts, c0s, sl_h0, ci == 0, ci == nch_ - 1)
                    pvgrp(h1, kts, c0s, sl_h1, ci == 0, ci == nch_ - 1)
                for h in (h0, h1):
                    pvb = PS[6] if h % 2 == 0 else PS[7]
                    rt = RROWM if h % 2 == 0 else RROWN
                    P.op("vector", lambda e, pvb=pvb: e.reciprocal(RR32F[:], pvb[64:65, :]),
                         wait=["tensor"], sig="tensor")
                    P.op("vector", lambda e, rt=rt: e.tensor_copy(rt[:], RR32F[:]), sig="tensor")
                    pend.append((h, pvb, rt))
            flush_norm()
            oev = []
            ocols = {}
            wa = wpush(dr["ow"][0, :, 0:512])
            wb = wpush(dr["ow"][0, :, 512:1024])
            ocols[0] = (wa, wb, P.count("#V"))
            for mc in range(NCH):
                if mc + 1 < NCH:
                    wa = wpush(dr["ow"][mc + 1, :, 0:512])
                    wb = wpush(dr["ow"][mc + 1, :, 512:1024])
                    ocols[mc + 1] = (wa, wb, P.count("#V"))
                wa, wb, vwo = ocols[mc]
                pb = PS[mc % 4]
                for hcc in range(NCH):
                    colof = (wa if hcc < 4 else wb) + (hcc % 4) * 128
                    P.op("tensor",
                         lambda e, pb=pb, hcc=hcc, colof=colof: e.matmul(
                             pb[:], WB16[:, colof:colof + 128], ATQ[:, hcc * 512:(hcc + 1) * 512],
                             start=(hcc == 0), stop=(hcc == NCH - 1)),
                         wait=([("abs", "#V", vwo)] if hcc == 0 else []) +
                              ([("abs", "#V", oev[mc - 4])] if hcc == 0 and mc >= 4 else []),
                         sig=("vector" if hcc == NCH - 1 else None))
                o = mc * S + qb * QB
                P.op("vector",
                     lambda e, pb=pb, o=o, mc=mc: e.scalar_tensor_tensor(
                         X[:, o:o + 512], pb[:], OB8[:, mc:mc + 1], X[:, o:o + 512],
                         ALU.add, ALU.add),
                     wait=["tensor"], sig=["tensor"])
                oev.append(P.count("#V"))
            P.wait("tensor", ["vector"])

    emit_ln("ln1")
    if dbg_dump("ln1"):
        return nc, P, dbg_outs
    emit_routing("r1")
    if dbg_dump("route1"):
        return nc, P, dbg_outs
    emit_tree("k", "KF", "t1k")
    if dbg_dump("ktree1"):
        return nc, P, dbg_outs
    emit_tree("v", "VT", "t1v", scale=0.5)
    if dbg_dump("vtree1"):
        return nc, P, dbg_outs
    emit_tree("q", "XLN", "t1q", scale=1.0 / 32.0)
    if dbg_dump("qtree1"):
        return nc, P, dbg_outs
    emit_fff_attn("a1")
    if dbg_dump("attn1"):
        return nc, P, dbg_outs
    emit_ln("ln2")
    if dbg_dump("ln2"):
        return nc, P, dbg_outs
    emit_mha("mha")
    if dbg_dump("mha"):
        return nc, P, dbg_outs
    emit_ln("ln3")
    if dbg_dump("ln3"):
        return nc, P, dbg_outs
    emit_routing("r2")
    if dbg_dump("route2"):
        return nc, P, dbg_outs
    emit_tree("k", "KF", "t2k")
    emit_tree("v", "VT", "t2v", scale=0.5)
    emit_tree("q", "XLN", "t2q", scale=1.0 / 32.0)
    if dbg_dump("trees2"):
        return nc, P, dbg_outs
    emit_fff_attn("a2", emit_output=True)
    for cn in ("o0", "o1"):
        s = P._chan(cn)
        P.q["sync"].append(lambda e, h=s[0], v=s[1]: e.wait_ge(h, v))
    dbg_dump("out")
    return nc, P, dbg_outs


def finalize(nc, P):
    with nc.Block() as block:
        @block.sync
        def _(e):
            for fn in P.q["sync"]:
                fn(e)

        @block.tensor
        def _(e):
            for fn in P.q["tensor"]:
                fn(e)

        @block.scalar
        def _(e):
            for fn in P.q["scalar"]:
                fn(e)

        @block.vector
        def _(e):
            for fn in P.q["vector"]:
                fn(e)

        @block.gpsimd
        def _(e):
            for fn in P.q["gpsimd"]:
                fn(e)
    return nc


_BUILD_CACHE = {}


def _get_program(upto=None, dbg=()):
    key = (upto, tuple(dbg))
    if key not in _BUILD_CACHE:
        nc, P, dbg_outs = build(upto=upto, dbg=dbg)
        finalize(nc, P)
        _BUILD_CACHE[key] = (nc, dbg_outs)
    return _BUILD_CACHE[key]


def run(inputs, upto=None, dbg=()):
    per_core = host_prepare(inputs)
    nc, dbg_outs = _get_program(upto=upto, dbg=dbg)
    res = run_bass_kernel_spmd(nc, per_core, core_ids=list(range(8)))
    return res, dbg_outs


def kernel(**inputs):
    res, _ = run(inputs)
    return np.stack([np.asarray(r["y"], np.float32) for r in res.results], axis=0)


# version 07
# speedup vs baseline: 1.0991x; 1.0152x over previous
import numpy as np
import concourse.bass as bass
import concourse.mybir as mybir
from concourse.bass_utils import run_bass_kernel_spmd

F32 = mybir.dt.float32
BF16 = mybir.dt.bfloat16
AF = mybir.ActivationFunctionType
ALU = mybir.AluOpType

S = 2048
D = 1024
NCH = 8
NQB = 4
QB = 512
NTT = 16
NH = 16
EPS = 1e-5

ENGINES = ("sync", "tensor", "scalar", "vector", "gpsimd")

_AUDIT = False
_AUDIT_ENGINES = ()


class Prog:

    def __init__(self, nc):
        self.nc = nc
        self.q = {e: [] for e in ENGINES}
        self.chans = {}

    def _chan(self, name):
        if name not in self.chans:
            self.chans[name] = [self.nc.alloc_semaphore(f"c_{name}"), 0]
        return self.chans[name]

    ENGCH = {"tensor": "T", "scalar": "S", "vector": "V", "gpsimd": "G"}

    def _mkwaits(self, wait):
        wl = []
        for w in wait:
            if isinstance(w, tuple) and len(w) == 3:
                _, src, v = w
                s = self._chan(src[1:] if src.startswith("#") else self.ENGCH[src])
                if v > 0:
                    wl.append((s[0], v))
                continue
            if isinstance(w, tuple):
                src, lag = w
            else:
                src, lag = w, 0
            name = src[1:] if src.startswith("#") else self.ENGCH[src]
            s = self._chan(name)
            v = s[1] - lag
            if v > 0:
                wl.append((s[0], v))
        return wl

    def op(self, eng, fn, wait=(), sig=None):
        wl = self._mkwaits(wait)
        if eng in ("gpsimd", "vector", "scalar") or (_AUDIT and eng != "sync") or eng in _AUDIT_ENGINES:
            s = self._chan(self.ENGCH[eng])
            if s[1] > 0:
                wl.append((s[0], s[1]))
            s[1] += 1
            sigs = [(s[0], 1)]
        elif sig is not None:
            s = self._chan(self.ENGCH[eng])
            s[1] += 1
            sigs = [(s[0], 1)]
        else:
            sigs = []

        def run(e, fn=fn, wl=wl, sigs=sigs):
            for h, v in wl:
                e.wait_ge(h, v)
            ins = fn(e)
            if sigs:
                ins.then_inc(sigs[0][0], sigs[0][1])

        self.q[eng].append(run)

    def count(self, name):
        return self._chan(name[1:] if name.startswith("#") else self.ENGCH[name])[1]

    def wait(self, eng, srcs):
        wl = self._mkwaits([srcs] if isinstance(srcs, (str, tuple)) else srcs)
        if wl:
            def run(e, wl=wl):
                for h, v in wl:
                    e.wait_ge(h, v)
            self.q[eng].append(run)

    def dma(self, out_ap, in_ap, chan, wait=()):
        s = self._chan(chan)
        s[1] += 16
        wl = self._mkwaits(wait)

        def run(e, wl=wl, h=s[0]):
            for hh, v in wl:
                e.wait_ge(hh, v)
            e.dma_start(out=out_ap, in_=in_ap).then_inc(h, 16)

        self.q["sync"].append(run)

    def barrier(self, engines=("tensor", "scalar", "vector", "gpsimd")):
        fence = {
            "tensor": lambda e: e.ldweights(self.fence_ap),
            "scalar": lambda e: e.copy(self.fence_sc[0:1, 1:2], self.fence_src),
            "vector": lambda e: e.memset(self.fence_sc[0:1, 2:3], 0.0),
            "gpsimd": lambda e: e.memset(self.fence_sc[0:1, 3:4], 0.0),
        }
        for a in engines:
            self.op(a, fence[a], wait=(["#G"] if a in ("tensor", "scalar") else []), sig=True)
        for a in engines:
            self.wait(a, [b for b in engines if b != a])


def ceil32(x):
    return (x + 31) & ~31


def host_prepare(inputs):
    g = {}
    for k, v in inputs.items():
        a = np.asarray(v)
        if a.dtype != np.float32:
            a = a.astype(np.float32)
        g[k] = a
    x = g["L0_attn_x"]
    out = {}
    for t in ("q", "k", "v"):
        w1 = g[f"L0_ffn_{t}_w1"]
        w1f = np.ascontiguousarray(w1.transpose(1, 0, 2).reshape(D, D))
        A = w1f.reshape(NCH, 128, NCH, 128)
        out[f"{t}w1"] = np.ascontiguousarray(A.transpose(2, 1, 0, 3).reshape(NCH, 128, D))
        w2f = g[f"L0_ffn_{t}_w2"].reshape(D, D)
        if t == "v":
            out["vw2"] = np.ascontiguousarray(w2f.reshape(NCH, 128, D))
        else:
            C = w2f.reshape(NCH, 128, NCH, 128)
            out[f"{t}w2"] = np.ascontiguousarray(C.transpose(2, 1, 0, 3).reshape(NCH, 128, D))
        out[f"{t}b1"] = np.ascontiguousarray(g[f"L0_ffn_{t}_b1"].reshape(NCH, 128).T)
        out[f"{t}b2"] = np.ascontiguousarray(g[f"L0_ffn_{t}_b2"])
    npk = np.zeros((128, 408), np.float32)
    for ti, t in enumerate(("q", "k", "v")):
        nwT = g[f"L0_ffn_{t}_node_w"].T.reshape(NCH, 128, 15)
        for c in range(NCH):
            npk[:, c * 45 + ti * 15:c * 45 + ti * 15 + 15] = nwT[c]
        npk[0, 360 + ti * 15:360 + ti * 15 + 15] = g[f"L0_ffn_{t}_node_b"]
    out["npk"] = npk
    qwf = np.ascontiguousarray(g["L0_attn_Q_w"].transpose(1, 0, 2).reshape(D, D))
    kwf = np.ascontiguousarray(g["L0_attn_K_w"].transpose(1, 0, 2).reshape(D, D))
    vwf = np.ascontiguousarray(g["L0_attn_V_w"].transpose(1, 0, 2).reshape(D, D))
    owf = np.ascontiguousarray(g["L0_attn_O_w"].reshape(D, D))
    for nm, wf in (("qw", qwf), ("kw", kwf), ("ow", owf)):
        Cw = wf.reshape(NCH, 128, NCH, 128)
        out[nm] = np.ascontiguousarray(Cw.transpose(2, 1, 0, 3).reshape(NCH, 128, D))
    out["vw"] = np.ascontiguousarray(vwf.reshape(NCH, 128, D))
    out["qb8"] = np.ascontiguousarray(g["L0_attn_Q_b"].reshape(D).reshape(NCH, 128).T)
    out["kb8"] = np.ascontiguousarray(g["L0_attn_K_b"].reshape(D).reshape(NCH, 128).T)
    out["ob8"] = np.ascontiguousarray(g["L0_attn_O_b"].reshape(NCH, 128).T)
    out["vbrow"] = np.ascontiguousarray(g["L0_attn_V_b"].reshape(1, D))
    per_core = []
    for b in range(8):
        m = dict(out)
        m["x"] = np.ascontiguousarray(x[b])
        per_core.append(m)
    return per_core


def build(upto=None, dbg=()):
    nc = bass.Bass("TRN2", target_bir_lowering=False, dynamic_dma_scratch_size=64,
                   detect_race_conditions=_AUDIT)
    P = Prog(nc)

    dr = {}
    dr["x"] = nc.dram_tensor("x", [S, D], F32, kind="ExternalInput").ap()
    for t in ("q", "k", "v"):
        for suf, shape in (("w1", [NCH, 128, D]), ("w2", [NCH, 128, D]),
                           ("b1", [128, 8]), ("b2", [16, D])):
            dr[f"{t}{suf}"] = nc.dram_tensor(f"{t}{suf}", shape, F32, kind="ExternalInput").ap()
    dr["npk"] = nc.dram_tensor("npk", [128, 408], F32, kind="ExternalInput").ap()
    for nm in ("qw", "kw", "vw", "ow"):
        dr[nm] = nc.dram_tensor(nm, [NCH, 128, D], F32, kind="ExternalInput").ap()
    for nm in ("qb8", "kb8", "ob8"):
        dr[nm] = nc.dram_tensor(nm, [128, 8], F32, kind="ExternalInput").ap()
    dr["vbrow"] = nc.dram_tensor("vbrow", [1, D], F32, kind="ExternalInput").ap()
    y = nc.dram_tensor("y", [S, D], F32, kind="ExternalOutput").ap()

    SBYTES = (nc.SBUF_PARTITION_SIZE_BYTES or 196608) - 32
    off = 512

    def at(name, shape, dtype, o=None):
        nonlocal off
        fixed = o is not None
        if o is None:
            o = off
        tb = {F32: 4, BF16: 2}[dtype]
        nbytes = int(np.prod(shape[1:])) * tb
        h = nc.alloc_sbuf_tensor_at(name, shape, dtype, offset=o)
        if not fixed:
            off = ceil32(o + nbytes)
        return h

    X = at("X", [128, NCH * S], F32)
    XLN = at("XLN", [128, NCH * S], BF16)
    KF = at("KF", [128, NCH * S], BF16)
    VT = at("VT", [128, NTT * 1040], BF16)
    SCRO = off
    PBUF = at("PBUF", [128, 8192], BF16, SCRO)
    SB = at("SB", [128, 2048], F32, SCRO)
    SFM = at("SFM", [128, 2048], F32, SCRO)
    ATQ = at("ATQ", [128, 4096], BF16, SCRO)
    VWA = at("VWA", [128, 4608], BF16, SCRO)
    XSTG0 = at("XSTG0", [128, 1024], F32, SCRO)
    XSTG1 = at("XSTG1", [128, 1024], F32, SCRO + 4096)
    MSB = at("MSB", [128, 2048], F32, SCRO + 8192)
    SALL = at("SALL", [128, 720], F32, SCRO + 8192)
    PRING = at("PRING", [128, 2048], BF16, SCRO + 8192)
    MUXT = at("MUXT", [128, 192], F32, SCRO + 12288)
    QSTG = at("QSTG", [128, 4096], BF16, SCRO + 12288)
    RROWM = at("RROWM", [1, 512], BF16, SCRO + 13312)
    RBM = at("RBM", [64, 512], F32, SCRO + 14336)
    RROWN = at("RROWN", [1, 512], BF16, SCRO + 12288)
    LEAFB = at("LEAFB", [128, 1024], BF16, SCRO + 16384)
    B2BF = at("B2BF", [16, 1024], BF16, SCRO + 16384)
    SCRN = at("SCRN", [128, 1024], F32, SCRO + 16384)
    RBF = at("RBF", [128, 512], F32, SCRO + 16384)
    E16H = at("E16H", [16, 1024], BF16, SCRO + 18432)
    MSK = at("MSK", [128, 1024], BF16, SCRO + 18432)
    LEAFT = at("LEAFT", [8, 128], BF16, SCRO + 18432)
    TMPF = at("TMPF", [128, 512], F32, SCRO + 18432)
    RROWF = at("RROWF", [1, 512], BF16, SCRO + 18432)
    off = ceil32(SCRO + 20480)
    WS32 = at("WS32", [128, 1024], F32)
    OSTG = at("OSTG", [128, 256], F32, off - 4096)
    WB16 = at("WB16", [128, 2048], BF16)
    ONES1 = at("ONES1", [1, 512], BF16)
    ONESC = at("ONESC", [128, 128], BF16)
    I128B = at("I128B", [128, 128], BF16)
    I128F = at("I128F", [128, 128], F32)
    TRI = at("TRI", [128, 128], BF16)
    IOTA16 = at("IOTA16", [128, 16], F32)
    C2 = at("C2", [128, 8], F32)
    BO16 = at("BO16", [128, 128], BF16)
    B1T = {t: at(f"B1{t}", [128, 8], F32) for t in ("q", "k", "v")}
    QB8 = at("QB8", [128, 8], F32)
    QB8S = at("QB8S", [128, 8], F32)
    KB8 = at("KB8", [128, 8], F32)
    OB8 = at("OB8", [128, 8], F32)
    LTF = at("LTF", [128, 16], F32)
    LT16 = {t: at(f"LT16{t}", [128, 16], BF16) for t in ("q", "k", "v")}
    LTFS = {t: at(f"LTFS{t}", [128, 16], F32) for t in ("q", "k", "v")}
    STAT = at("STAT", [128, 16], F32)
    STAIR = at("STAIR", [128, 32], BF16)
    PIDX = at("PIDX", [128, 1], F32)
    BN = at("BN", [128, 1], F32)
    BH = at("BH", [128, 16], F32)
    BH16 = at("BH16", [16, 1], BF16)
    ETOK = at("ETOK", [128, 16], BF16)
    ETOK1 = at("ETOK1", [1, 16], BF16)
    GT1 = at("GT1", [128, 128], F32)
    RR32F = at("RR32F", [1, 512], F32)
    EPSC = at("EPSC", [128, 1], F32)
    FEN = at("FEN", [128, 8], F32)
    P.fence_ap = ONESC[0:1, 0:1]
    P.fence_sc = FEN
    P.fence_src = EPSC[0:1, 0:1]
    assert off <= SBYTES, f"SBUF overflow: {off} > {SBYTES}"
    assert SCRO + 20480 <= SBYTES

    PSALL = nc.alloc_psum_tensor("psall", [128, 4096], F32)
    PS = [PSALL[:, i * 512:(i + 1) * 512] for i in range(8)]

    dbg_outs = {}
    named = {"X": X, "XLN": XLN, "KF": KF, "VT": VT, "PBUF": PBUF, "SALL": SALL,
             "SB": SB, "MSB": MSB, "SFM": SFM, "LEAFB": LEAFB, "E16H": E16H,
             "ATQ": ATQ, "BH": BH, "QSTG": QSTG, "STAT": STAT, "BN": BN,
             "LTFq": LTFS["q"], "LTFk": LTFS["k"], "LTFv": LTFS["v"]}

    def dbg_dump(phase):
        if upto == phase or any(p == phase for p, _ in dbg):
            P.barrier()
            P.op("scalar", lambda e: e.copy(P.fence_sc[0:1, 4:5], P.fence_src), wait=["#G"], sig=True)
            for p, name in dbg:
                if p != phase:
                    continue
                t = named[name]
                key = f"dbg_{phase}_{name}"
                shp = list(t.shape)
                d = nc.dram_tensor(key, shp, t.dtype, kind="ExternalOutput").ap()
                P.wait("sync", ["scalar"])
                P.dma(d, t[:], "dbg")
                dbg_outs[key] = tuple(shp)
            P.wait("scalar", ["#dbg"])
        return upto == phase

    cops = [
        lambda e: e.memset(FEN[:], 0.0),
        lambda e: e.memset(VT[:], 0.0),
        lambda e: e.memset(ONES1[:], 1.0),
        lambda e: e.memset(ONESC[:], 1.0),
        lambda e: e.memset(EPSC[:], EPS),
        lambda e: e.iota(I128B[:], [[1, 128]], channel_multiplier=-1,
                         allow_small_or_imprecise_dtypes=True),
        lambda e: e.tensor_single_scalar(I128B[:], I128B[:], 0.0, ALU.is_equal),
        lambda e: e.iota(I128F[:], [[1, 128]], channel_multiplier=-1,
                         allow_small_or_imprecise_dtypes=True),
        lambda e: e.tensor_single_scalar(I128F[:], I128F[:], 0.0, ALU.is_equal),
        lambda e: e.iota(TRI[:], [[1, 128]], channel_multiplier=-1,
                         allow_small_or_imprecise_dtypes=True),
        lambda e: e.tensor_single_scalar(TRI[:], TRI[:], 0.0, ALU.is_ge),
        lambda e: e.iota(IOTA16[:], [[1, 16]], channel_multiplier=0,
                         allow_small_or_imprecise_dtypes=True),
        lambda e: e.iota(GT1[:, 1:2], [[0, 1]], channel_multiplier=1,
                         allow_small_or_imprecise_dtypes=True),
        lambda e: e.tensor_single_scalar(GT1[:, 1:2], GT1[:, 1:2], 64.0, ALU.is_ge),
        lambda e: e.iota(C2[:], [[2, 8]], channel_multiplier=0,
                         allow_small_or_imprecise_dtypes=True),
        lambda e: e.tensor_tensor(C2[:], C2[:], GT1[:, 1:2].broadcast_to([128, 8]), ALU.add),
        lambda e: e.iota(GT1[:, 0:128], [[-2, 8], [1, 16]], channel_multiplier=0,
                         allow_small_or_imprecise_dtypes=True),
        lambda e: e.tensor_tensor(GT1[:, 0:128], GT1[:, 0:128],
                                  GT1[:, 1:2].broadcast_to([128, 128]), ALU.subtract),
        lambda e: e.tensor_single_scalar(BO16[:], GT1[:, 0:128], 0.0, ALU.is_equal),
        lambda e: e.iota(STAIR[:], [[1, 32]], channel_multiplier=0,
                         allow_small_or_imprecise_dtypes=True),
        lambda e: e.iota(PIDX[:], [[0, 1]], channel_multiplier=1,
                         allow_small_or_imprecise_dtypes=True),
        lambda e: e.tensor_single_scalar(STAIR[:], STAIR[:], 15.0, ALU.is_equal),
    ]
    for i, cop in enumerate(cops):
        P.op("gpsimd", cop,
             sig=(["tensor", "scalar", "vector"] if i == len(cops) - 1 else None))
    P.dma(QB8[:], dr["qb8"], "mc")
    P.dma(KB8[:], dr["kb8"], "mc")
    P.dma(OB8[:], dr["ob8"], "mc")
    for t in ("q", "k", "v"):
        P.dma(B1T[t][:], dr[f"{t}b1"], "mc")
    P.op("vector", lambda e: e.tensor_single_scalar(QB8S[:], QB8[:], 0.125, ALU.mult),
         wait=["#mc", "#G"], sig="scalar")

    xst = [XSTG0, XSTG1]
    for tt in range(NTT):
        sl = xst[tt % 2]
        P.dma(sl[:], dr["x"][tt * 128:(tt + 1) * 128, :], f"x{tt % 2}",
              wait=([("scalar", 1)] if tt >= 2 else []))
        for c in range(NCH):
            pb = PS[c % 2]
            P.op("tensor",
                 lambda e, pb=pb, sl=sl, c=c: e.transpose(
                     pb[:, 0:128], sl[:, c * 128:(c + 1) * 128], I128F[:]),
                 wait=(["#G"] if tt == 0 and c == 0 else []) +
                      ([f"#x{tt % 2}"] if c == 0 else []) +
                      ([("scalar", 1)] if (tt, c) > (0, 1) else []),
                 sig="scalar")
            P.op("scalar",
                 lambda e, pb=pb, c=c, tt=tt: e.copy(
                     X[:, c * S + tt * 128:c * S + tt * 128 + 128], pb[:, 0:128]),
                 wait=["tensor"], sig=["tensor"] + (["sync"] if c == 7 else []))
    if dbg_dump("xin"):
        return nc, P, dbg_outs

    wstate = {"n": 0, "gafter": [0, 0]}

    def wpush(dram_ap, ncols=512, dst16=None, extra_cast_wait=(), colw=None):
        u = wstate["n"]
        wstate["n"] += 1
        i32 = u % 2
        pr = dram_ap.shape[0]
        P.dma(WS32[0:pr, i32 * 512:i32 * 512 + ncols], dram_ap, f"w{i32}",
              wait=[("abs", "#V", wstate["gafter"][i32])])
        if dst16 is None:
            cw = colw or 512
            i16 = u % (2048 // cw)
            dst = WB16[0:pr, i16 * cw:i16 * cw + ncols]
            col = i16 * cw
            cw = ["tensor"]
        else:
            tns, col, prx = dst16
            dst = tns[0:pr, col:col + ncols]
        P.op("vector",
             lambda e, dst=dst, i32=i32, pr=pr, ncols=ncols: e.tensor_copy(
                 dst, WS32[0:pr, i32 * 512:i32 * 512 + ncols]),
             wait=[f"#w{i32}", "tensor"] + list(extra_cast_wait), sig=True)
        wstate["gafter"][i32] = P.count("#V")
        return col

    def emit_ln(tag):
        part = globals().get("_LN_PART")
        P.barrier()
        cast_rec = {}
        for c in range(NCH):
            eng = "gpsimd" if c < 4 else "vector"
            P.op(eng, lambda e, c=c: e.tensor_copy(XLN[:, c * S:(c + 1) * S], X[:, c * S:(c + 1) * S]),
                 sig=True)
            cast_rec[c] = ("#G" if c < 4 else "#V", P.count("#G" if c < 4 else "#V"))
        for qb in range(NQB):
            pm = PS[qb % 2]
            for c in range(NCH):
                P.op("tensor",
                     lambda e, pm=pm, c=c, qb=qb: e.matmul(
                         pm[:], ONESC[:], XLN[:, c * S + qb * QB:c * S + qb * QB + QB],
                         start=(c == 0), stop=(c == NCH - 1)),
                     wait=([("abs",) + cast_rec[c]] if qb == 0 else []) +
                          ([("vector", 1)] if qb >= 2 and c == 0 else []),
                     sig=("vector" if c == NCH - 1 else None))
            P.op("vector",
                 lambda e, pm=pm, qb=qb: e.tensor_single_scalar(
                     MSB[:, qb * QB:(qb + 1) * QB], pm[:], 1.0 / D, ALU.mult),
                 wait=["tensor"], sig="tensor")
        if part == "sums":
            return
        sq_rec = {}
        for c in range(NCH):
            if c < 4:
                P.op("scalar",
                     lambda e, c=c: e.activation(XLN[:, c * S:(c + 1) * S],
                                                 X[:, c * S:(c + 1) * S], AF.Square),
                     wait=(["tensor"] if c == 0 else []), sig=True)
                sq_rec[c] = ("#S", P.count("#S"))
            else:
                P.op("vector",
                     lambda e, c=c: e.tensor_mul(XLN[:, c * S:(c + 1) * S],
                                                 X[:, c * S:(c + 1) * S], X[:, c * S:(c + 1) * S]),
                     wait=(["tensor"] if c == 4 else []), sig=True)
                sq_rec[c] = ("#V", P.count("#V"))
        if part == "sq":
            return
        for qb in range(NQB):
            pv = PS[2 + qb % 2]
            q0, q1 = qb * QB, (qb + 1) * QB
            for c in range(NCH):
                P.op("tensor",
                     lambda e, pv=pv, c=c, qb=qb: e.matmul(
                         pv[:], ONESC[:], XLN[:, c * S + qb * QB:c * S + qb * QB + QB],
                         start=(c == 0), stop=(c == NCH - 1)),
                     wait=([("abs",) + sq_rec[c]] if qb == 0 else []) +
                          ([("vector", 2)] if qb >= 2 and c == 0 else []),
                     sig=("vector" if c == NCH - 1 else None))
            P.op("vector", lambda e, q0=q0, q1=q1: e.tensor_mul(
                SCRN[:, 0:512], MSB[:, q0:q1], MSB[:, q0:q1]))
            P.op("vector",
                 lambda e, pv=pv: e.scalar_tensor_tensor(
                     SCRN[:, 512:1024], pv[:], 1.0 / D, SCRN[:, 0:512],
                     ALU.mult, ALU.subtract),
                 wait=["tensor"], sig=["tensor", "scalar"])
            P.op("scalar", lambda e: e.activation(SCRN[:, 0:512], SCRN[:, 512:1024],
                                                  AF.Ln, bias=EPSC[:]),
                 wait=["vector"])
            P.op("scalar", lambda e, q0=q0, q1=q1: e.activation(
                SB[:, q0:q1], SCRN[:, 0:512], AF.Exp, scale=-0.5), sig="vector")
            P.op("vector", lambda e, q0=q0, q1=q1: e.tensor_mul(
                MSB[:, q0:q1], MSB[:, q0:q1], SB[:, q0:q1]),
                wait=["scalar"], sig="gpsimd")
        if part == "stats":
            return
        P.wait("gpsimd", ["vector"])
        pi = 0
        for c in range(NCH):
            for hh in range(4):
                o = c * S + hh * 512
                so = hh * 512 if False else (hh % 4) * 512
                so = (hh * 512) % 2048
                sl = (pi % 2) * 512
                P.op("gpsimd",
                     lambda e, o=o, so=so, sl=sl: e.tensor_mul(
                         SCRN[:, sl:sl + 512], X[:, o:o + 512], SB[:, so:so + 512]),
                     wait=([("vector", 1)] if pi >= 2 else []))
                P.op("vector",
                     lambda e, o=o, so=so, sl=sl: e.tensor_sub(
                         XLN[:, o:o + 512], SCRN[:, sl:sl + 512], MSB[:, so:so + 512]),
                     wait=["#G"],
                     sig=["gpsimd"] + (["tensor", "scalar"] if pi == 31 else []))
                pi += 1

    def emit_routing(tag):
        P.barrier()
        co = wpush(dr["npk"], ncols=408)
        for qb in range(NQB):
            pb = PS[4]
            for c in range(NCH):
                P.op("tensor",
                     lambda e, pb=pb, c=c, qb=qb, co=co: e.matmul(
                         pb[0:45, :], WB16[0:128, co + c * 45:co + c * 45 + 45],
                         XLN[:, c * S + qb * QB:c * S + qb * QB + QB],
                         start=(c == 0), stop=False),
                     wait=(["vector"] if qb == 0 and c == 0 else []) +
                          (["scalar"] if qb > 0 and c == 0 else []))
            P.op("tensor",
                 lambda e, pb=pb, co=co: e.matmul(
                     pb[0:45, :], WB16[0:1, co + 360:co + 405], ONES1[:],
                     start=False, stop=True),
                 sig="scalar")
            P.op("scalar",
                 lambda e, pb=pb, qb=qb: e.copy(SFM[0:45, qb * QB:(qb + 1) * QB], pb[0:45, :]),
                 wait=["tensor"], sig=["tensor"] + (["gpsimd"] if qb == NQB - 1 else []))
        for tt in range(NTT):
            pb = PS[5]
            P.op("tensor",
                 lambda e, pb=pb, tt=tt: e.transpose(
                     pb[0:128, 0:45], SFM[0:45, tt * 128:(tt + 1) * 128],
                     I128F[0:45, 0:45]),
                 wait=["scalar"], sig="scalar")
            P.op("scalar",
                 lambda e, pb=pb, tt=tt: e.copy(SALL[:, tt * 45:tt * 45 + 45], pb[0:128, 0:45]),
                 wait=["tensor"], sig=["tensor"] + (["vector"] if tt == NTT - 1 else []))

        def scol(col):
            return SALL[:, col:col + 45 * 15 + 1:45]

        def mcol(i):
            return MUXT[:, i * 16:(i + 1) * 16]

        for ti, t in enumerate(("q", "k", "v")):
            b0, b1, b2, b3 = mcol(0), mcol(1), mcol(2), mcol(3)
            t0, t1, t2, t3 = mcol(4), mcol(5), mcol(6), mcol(7)
            u0, u1 = mcol(8), mcol(9)
            base = ti * 15

            def V(fn, **kw):
                P.op("vector", fn, **kw)

            def mux(dst, a, b_, sel):
                V(lambda e, dst=dst, a=a, b_=b_: e.tensor_sub(dst, b_, a))
                V(lambda e, dst=dst, sel=sel: e.tensor_mul(dst, dst, sel))
                V(lambda e, dst=dst, a=a: e.tensor_add(dst, dst, a))

            V(lambda e, base=base, b0=b0: e.tensor_single_scalar(b0, scol(base + 0), 0.0, ALU.is_ge),
              wait=(["scalar"] if ti == 0 else []))
            mux(t0, scol(base + 1), scol(base + 2), b0)
            V(lambda e, b1=b1, t0=t0: e.tensor_single_scalar(b1, t0, 0.0, ALU.is_ge))
            mux(t0, scol(base + 3), scol(base + 4), b1)
            mux(t1, scol(base + 5), scol(base + 6), b1)
            mux(t2, t0, t1, b0)
            V(lambda e, b2=b2, t2=t2: e.tensor_single_scalar(b2, t2, 0.0, ALU.is_ge))
            mux(t0, scol(base + 7), scol(base + 8), b2)
            mux(t1, scol(base + 9), scol(base + 10), b2)
            mux(t2, scol(base + 11), scol(base + 12), b2)
            mux(t3, scol(base + 13), scol(base + 14), b2)
            mux(u0, t0, t1, b1)
            mux(u1, t2, t3, b1)
            mux(t0, u0, u1, b0)
            V(lambda e, b3=b3, t0=t0: e.tensor_single_scalar(b3, t0, 0.0, ALU.is_ge))
            V(lambda e, b0=b0, b1=b1: e.scalar_tensor_tensor(LTF[:], b0, 2.0, b1, ALU.mult, ALU.add))
            V(lambda e, b2=b2: e.scalar_tensor_tensor(LTF[:], LTF[:], 2.0, b2, ALU.mult, ALU.add))
            V(lambda e, b3=b3, t=t: e.scalar_tensor_tensor(LTFS[t][:], LTF[:], 2.0, b3, ALU.mult, ALU.add))
            V(lambda e, t=t: e.tensor_copy(LT16[t][:], LTFS[t][:]),
              sig=(["tensor", "scalar", "gpsimd"] if t == "v" else None))

    def emit_tree(tree, dst, tag, scale=1.0):
        P.barrier()
        for th in range(2):
            P.op("tensor",
                 lambda e, th=th, tree=tree: e.matmul(
                     PS[6][0:8, 0:128], LT16[tree][:, th * 8:th * 8 + 8], I128B[:],
                     start=True, stop=True),
                 wait=(["vector", "scalar", "#G"] if th == 1 else []),
                 sig="scalar")
            P.op("scalar", lambda e: e.copy(LEAFT[:], PS[6][0:8, 0:128]),
                 wait=["tensor"], sig="tensor")
            for g in range(2):
                for j in range(4):
                    tt8 = g * 4 + j
                    P.op("tensor",
                         lambda e, tt8=tt8, j=j: e.matmul(
                             PS[5][:, j * 128:(j + 1) * 128],
                             I128B[0:8, tt8:tt8 + 1].broadcast_to([8, 128]), LEAFT[:],
                             start=True, stop=True),
                         wait=(["scalar"] if tt8 == 0 else []) +
                              (["scalar"] if g == 1 and j == 0 else []),
                         sig=("scalar" if j == 3 else None))
                P.op("scalar",
                     lambda e, g=g: e.copy(LEAFB[:, g * 512:(g + 1) * 512], PS[5][:]),
                     wait=["tensor"], sig=["tensor", "vector"])
            wcols = {}
            wa = wpush(dr[f"{tree}w1"][0, :, 0:512])
            wb = wpush(dr[f"{tree}w1"][0, :, 512:1024])
            wcols[0] = (wa, wb, P.count("#V"))
            for l in range(NCH):
                if l + 1 < NCH:
                    wa = wpush(dr[f"{tree}w1"][l + 1, :, 0:512])
                    wb = wpush(dr[f"{tree}w1"][l + 1, :, 512:1024])
                    wcols[l + 1] = (wa, wb, P.count("#V"))
                wa, wb, vw1 = wcols[l]
                for qh in range(2):
                    qb = th * 2 + qh
                    gi = (2 * l + qh) % 4
                    pb = PS[gi]
                    for c in range(NCH):
                        colof = (wa if c < 4 else wb) + (c % 4) * 128
                        P.op("tensor",
                             lambda e, pb=pb, c=c, qb=qb, colof=colof: e.matmul(
                                 pb[:], WB16[:, colof:colof + 128],
                                 XLN[:, c * S + qb * QB:c * S + qb * QB + QB],
                                 start=(c == 0), stop=(c == NCH - 1)),
                             wait=([("abs", "#V", vw1)] if c == 0 else []) +
                                  ([("scalar", 3)] if c == 0 else []),
                             sig=("scalar" if c == NCH - 1 else None))
                    P.op("scalar",
                         lambda e, pb=pb, l=l, qh=qh: e.activation(
                             PBUF[:, l * 1024 + qh * 512:l * 1024 + qh * 512 + 512],
                             pb[:], AF.Relu, bias=B1T[tree][:, l:l + 1]),
                         wait=["tensor"] + (["#mc"] if l == 0 and qh == 0 and th == 0 else []),
                         sig=["tensor"] + (["vector"] if qh == 1 else []))
                P.op("vector",
                     lambda e, l=l: e.tensor_single_scalar(MSK[:], LEAFB[:], C2[:, l:l + 1], ALU.is_equal),
                     wait=["scalar"])
                P.op("vector",
                     lambda e, l=l: e.tensor_mul(PBUF[:, l * 1024:(l + 1) * 1024],
                                                 PBUF[:, l * 1024:(l + 1) * 1024], MSK[:]),
                     sig=["tensor"])
            for g in range(2):
                for j in range(4):
                    tt8 = g * 4 + j
                    P.op("vector",
                         lambda e, tt8=tt8, th=th, tree=tree: e.tensor_single_scalar(
                             ETOK[:], IOTA16[:],
                             LTFS[tree][:, th * 8 + tt8:th * 8 + tt8 + 1], ALU.is_equal),
                         wait=([("tensor", 0)] if tt8 > 0 else []), sig="tensor")
                    P.op("tensor",
                         lambda e, j=j: e.matmul(PS[6][0:16, j * 128:(j + 1) * 128],
                                                 ETOK[:], I128B[:], start=True, stop=True),
                         wait=["vector"] + (["scalar"] if j == 0 else []),
                         sig=["vector"] + (["scalar"] if j == 3 else []))
                P.op("scalar",
                     lambda e, g=g: e.copy(E16H[:, g * 512:(g + 1) * 512], PS[6][0:16, :]),
                     wait=["tensor"], sig="tensor")
            for half in range(2):
                wpush(dr[f"{tree}b2"][:, half * 512:half * 512 + 512],
                      dst16=(B2BF, half * 512, 16), extra_cast_wait=["vector", "scalar"])
            if dst == "VT":
                for g in range(4):
                    slots = []
                    for lcg in range(2):
                        for lcq in range(4):
                            lc = lcg * 4 + lcq
                            u = wpush(dr["vw2"][lc, :, g * 256:(g + 1) * 256], ncols=256, colw=256)
                            slots.append(u)
                    for lcg in range(2):
                        for tt8 in range(8):
                            tt = th * 8 + tt8
                            pb = PS[tt8]
                            for lcq in range(4):
                                lc = lcg * 4 + lcq
                                P.op("tensor",
                                     lambda e, pb=pb, lc=lc, tt8=tt8, g=g,
                                            cu=slots[lc]: e.matmul(
                                         pb[:, 0:256],
                                         PBUF[:, lc * 1024 + tt8 * 128:lc * 1024 + tt8 * 128 + 128],
                                         WB16[:, cu:cu + 256],
                                         start=(lc == 0), stop=False),
                                     wait=(["vector"] if lc == 0 and tt8 == 0 and lcg == 0 else []) +
                                          (["scalar"] if lc == 0 and lcg == 0 and tt8 == 0 and g > 0 else []))
                            if lcg == 1:
                                P.op("tensor",
                                     lambda e, pb=pb, tt8=tt8, g=g: e.matmul(
                                         pb[:, 0:256],
                                         E16H[:, tt8 * 128:tt8 * 128 + 128],
                                         B2BF[:, g * 256:(g + 1) * 256],
                                         start=False, stop=True),
                                     sig="scalar")
                                P.op("scalar",
                                     lambda e, pb=pb, tt=tt, g=g: e.activation(
                                         VT[:, tt * 1040 + g * 256:tt * 1040 + g * 256 + 256],
                                         pb[:, 0:256], AF.Copy, scale=scale),
                                     wait=["tensor"],
                                     sig=["tensor"] + (["gpsimd"] if tt8 >= 6 else []))
            else:
                dtens = KF if dst == "KF" else XLN
                w2cols = {}
                wa = wpush(dr[f"{tree}w2"][0, :, 0:512])
                wb = wpush(dr[f"{tree}w2"][0, :, 512:1024])
                w2cols[0] = (wa, wb, P.count("#V"))
                for oc in range(NCH):
                    if oc + 1 < NCH:
                        wa = wpush(dr[f"{tree}w2"][oc + 1, :, 0:512])
                        wb = wpush(dr[f"{tree}w2"][oc + 1, :, 512:1024])
                        w2cols[oc + 1] = (wa, wb, P.count("#V"))
                    wa, wb, vw2 = w2cols[oc]
                    for qh in range(2):
                        qb = th * 2 + qh
                        pb = PS[(2 * oc + qh) % 4]
                        for lc in range(NCH):
                            colof = (wa if lc < 4 else wb) + (lc % 4) * 128
                            P.op("tensor",
                                 lambda e, pb=pb, lc=lc, qh=qh, colof=colof: e.matmul(
                                     pb[:], WB16[:, colof:colof + 128],
                                     PBUF[:, lc * 1024 + qh * 512:lc * 1024 + qh * 512 + 512],
                                     start=(lc == 0), stop=False),
                                 wait=([("abs", "#V", vw2)] if lc == 0 else []) +
                                      ([("scalar", 3)] if lc == 0 else []))
                        P.op("tensor",
                             lambda e, pb=pb, oc=oc, qh=qh: e.matmul(
                                 pb[:], B2BF[:, oc * 128:(oc + 1) * 128],
                                 E16H[:, qh * 512:qh * 512 + 512],
                                 start=False, stop=True),
                             sig="scalar")
                        P.op("scalar",
                             lambda e, pb=pb, oc=oc, qb=qb, dtens=dtens: e.activation(
                                 dtens[:, oc * S + qb * QB:oc * S + qb * QB + QB],
                                 pb[:], AF.Copy, scale=scale),
                             wait=["tensor"],
                             sig=["tensor"] + (["gpsimd"] if qh == 1 else []))

    def emit_fff_attn(tag, emit_output=False):
        P.barrier()
        for wi, TS in ((0, XLN), (1, KF)):
            for qb in range(NQB):
                for c in range(NCH):
                    P.op("vector",
                         lambda e, TS=TS, c=c, qb=qb: e.tensor_mul(
                             PBUF[:, (c % 4) * 512:(c % 4) * 512 + 512],
                             TS[:, c * S + qb * QB:c * S + qb * QB + QB],
                             TS[:, c * S + qb * QB:c * S + qb * QB + QB]),
                         wait=([("tensor", 3)] if (c >= 4 or qb > 0 or wi > 0) else []),
                         sig="tensor")
                    P.op("tensor",
                         lambda e, c=c: e.matmul(
                             PS[4][0:1, :], ONESC[:, 0:1],
                             PBUF[:, (c % 4) * 512:(c % 4) * 512 + 512],
                             start=(c == 0), stop=(c == NCH - 1)),
                         wait=["vector"], sig=True)
                P.op("vector",
                     lambda e: e.reduce_max(STAT[0:1, 1:2], PS[4][0:1, :], mybir.AxisListType.X),
                     wait=["tensor"], sig="tensor")
                if qb == 0:
                    P.op("vector", lambda e, wi=wi: e.tensor_copy(
                        STAT[0:1, 3 + wi:4 + wi], STAT[0:1, 1:2]))
                else:
                    P.op("vector", lambda e, wi=wi: e.tensor_max(
                        STAT[0:1, 3 + wi:4 + wi], STAT[0:1, 3 + wi:4 + wi], STAT[0:1, 1:2]))
        P.op("vector", lambda e: e.tensor_mul(STAT[0:1, 5:6], STAT[0:1, 3:4], STAT[0:1, 4:5]),
             sig="scalar")
        P.op("scalar", lambda e: e.activation(STAT[0:1, 6:7], STAT[0:1, 5:6], AF.Ln),
             wait=["vector"])
        P.op("scalar", lambda e: e.activation(STAT[0:1, 7:8], STAT[0:1, 6:7], AF.Exp, scale=0.5),
             sig="vector")
        P.op("vector", lambda e: e.tensor_single_scalar(STAT[0:1, 8:9], STAT[0:1, 7:8], -1.0, ALU.mult),
             wait=["scalar"])
        P.op("vector", lambda e: e.tensor_copy(ETOK1[0:1, 0:1], STAT[0:1, 8:9]), sig="tensor")
        P.op("tensor", lambda e: e.matmul(PS[5][:, 0:1], ONESC[0:1, :], ETOK1[0:1, 0:1],
                                          start=True, stop=True),
             wait=["vector"], sig="scalar")
        P.op("scalar", lambda e: e.copy(BN[:], PS[5][:, 0:1]), wait=["tensor"],
             sig=["tensor", "vector"])

        for qb in range(NQB):
            for kt in range(NTT):
                pb = PS[kt % 3]
                for c in range(NCH):
                    P.op("tensor",
                         lambda e, pb=pb, c=c, kt=kt, qb=qb: e.matmul(
                             pb[:], KF[:, c * S + kt * 128:c * S + kt * 128 + 128],
                             XLN[:, c * S + qb * QB:c * S + qb * QB + QB],
                             start=(c == 0), stop=(c == NCH - 1)),
                         wait=([("scalar", 2)] if c == 0 and kt >= 3 else []) +
                              (["vector", "scalar"] if c == 0 and kt == 0 and qb == 0 else []) +
                              (["vector"] if c == 0 and kt == 0 and qb > 0 else []),
                         sig=("scalar" if c == NCH - 1 else None))
                if kt > 0:
                    P.op("tensor",
                         lambda e, kt=kt: e.matmul(
                             PS[4][0:1, :], ONESC[:, 0:1],
                             PBUF[:, (kt - 1) * 512:kt * 512],
                             start=(kt - 1 == 0), stop=False),
                         wait=["scalar"])
                P.op("scalar",
                     lambda e, pb=pb, kt=kt: e.activation(
                         PBUF[:, kt * 512:(kt + 1) * 512], pb[:], AF.Exp, bias=BN[:]),
                     wait=["tensor"], sig="tensor")
            P.op("tensor",
                 lambda e: e.matmul(PS[4][0:1, :], ONESC[:, 0:1],
                                    PBUF[:, (NTT - 1) * 512:NTT * 512],
                                    start=False, stop=True),
                 wait=["scalar"], sig=True)
            P.op("vector", lambda e: e.reciprocal(RR32F[:], PS[4][0:1, :]),
                 wait=[("tensor", 0), "#G"], sig="tensor")
            P.op("vector", lambda e: e.tensor_copy(RROWF[:], RR32F[:]), sig="tensor")
            P.op("tensor", lambda e: e.matmul(PS[5][:], ONES1[0:1, 0:128], RROWF[:],
                                              start=True, stop=True),
                 wait=["vector", ("scalar", 0)], sig="scalar")
            P.op("scalar", lambda e: e.copy(RBF[:], PS[5][:]), wait=["tensor"],
                 sig=["tensor", "vector"])
            for dc in range(NCH):
                pb = PS[3 if dc % 2 == 0 else 7]
                for kt in range(NTT):
                    P.op("tensor",
                         lambda e, pb=pb, dc=dc, kt=kt: e.matmul(
                             pb[:], VT[:, kt * 1040 + dc * 128:kt * 1040 + dc * 128 + 128],
                             PBUF[:, kt * 512:(kt + 1) * 512],
                             start=(kt == 0), stop=(kt == NTT - 1)),
                         wait=([("vector", 2)] if kt == 0 and dc >= 2 else []),
                         sig=("vector" if kt == NTT - 1 else None))
                o = dc * S + qb * QB
                P.op("vector",
                     lambda e, pb=pb: e.tensor_mul(TMPF[:], pb[:], RBF[:]),
                     wait=["tensor", "#G"] + (["scalar"] if dc == 0 else []),
                     sig=["tensor", "gpsimd"])
                P.op("gpsimd",
                     lambda e, o=o: e.tensor_add(X[:, o:o + 512], X[:, o:o + 512], TMPF[:]),
                     wait=["vector"],
                     sig=["vector"] + (["tensor"] if emit_output else []))
                if emit_output:
                    for j in range(4):
                        P.op("tensor",
                             lambda e, o=o, j=j: e.transpose(
                                 PS[6][:, 0:128], X[:, o + j * 128:o + j * 128 + 128], I128F[:]),
                             wait=["#G"] + (["scalar"] if (dc, j) > (0, 0) or qb > 0 else []),
                             sig="scalar")
                        P.op("scalar",
                             lambda e, j=j: e.copy(OSTG[:, (j % 2) * 128:(j % 2) * 128 + 128],
                                                   PS[6][:, 0:128]),
                             wait=["tensor", f"#o{j % 2}"], sig=["tensor", "sync"])
                        P.dma(y[qb * QB + j * 128:qb * QB + j * 128 + 128,
                                dc * 128:dc * 128 + 128],
                              OSTG[:, (j % 2) * 128:(j % 2) * 128 + 128], f"o{j % 2}",
                              wait=["scalar"])

    def emit_mha(tag):
        P.barrier()
        kcols = {}
        wa = wpush(dr["kw"][0, :, 0:512])
        wb = wpush(dr["kw"][0, :, 512:1024])
        kcols[0] = (wa, wb, P.count("#V"))
        for oc in range(NCH):
            if oc + 1 < NCH:
                wa = wpush(dr["kw"][oc + 1, :, 0:512])
                wb = wpush(dr["kw"][oc + 1, :, 512:1024])
                kcols[oc + 1] = (wa, wb, P.count("#V"))
            wa, wb, vwk = kcols[oc]
            for qb in range(NQB):
                pb = PS[(4 * oc + qb) % 4]
                for mc in range(NCH):
                    colof = (wa if mc < 4 else wb) + (mc % 4) * 128
                    P.op("tensor",
                         lambda e, pb=pb, mc=mc, qb=qb, colof=colof: e.matmul(
                             pb[:], WB16[:, colof:colof + 128],
                             XLN[:, mc * S + qb * QB:mc * S + qb * QB + QB],
                             start=(mc == 0), stop=(mc == NCH - 1)),
                         wait=([("abs", "#V", vwk)] if mc == 0 and qb == 0 else []) +
                              ([("scalar", 3)] if mc == 0 else []),
                         sig=("scalar" if mc == NCH - 1 else None))
                P.op("scalar",
                     lambda e, pb=pb, oc=oc, qb=qb: e.add(
                         KF[:, oc * S + qb * QB:oc * S + qb * QB + QB], pb[:],
                         KB8[:, oc:oc + 1]),
                     wait=["tensor"],
                     sig=["tensor"] + (["gpsimd"] if qb == NQB - 1 else []))
        P.op("gpsimd", lambda e: e.memset(
            VT[:].rearrange("p (t h) -> p t h", h=1040)[:, :, 64:1040:65], 1.0),
            wait=["tensor", "scalar"], sig="tensor")
        for half in range(2):
            for mc in range(NCH):
                wpush(dr["vw"][mc, :, half * 512:half * 512 + 512],
                      dst16=(VWA, mc * 512, 128),
                      extra_cast_wait=(["tensor"] if half == 1 else []))
            wpush(dr["vbrow"][0:1, half * 512:half * 512 + 512],
                  dst16=(VWA, 4096, 1))
            vwv = P.count("#V")
            for tt in range(NTT):
                pb = PS[tt % 4]
                for mc in range(NCH):
                    P.op("tensor",
                         lambda e, pb=pb, mc=mc, tt=tt: e.matmul(
                             pb[:], XLN[:, mc * S + tt * 128:mc * S + tt * 128 + 128],
                             VWA[:, mc * 512:(mc + 1) * 512],
                             start=(mc == 0), stop=False),
                         wait=([("abs", "#V", vwv)] if mc == 0 and tt == 0 else []) +
                              ([("scalar", 3)] if mc == 0 and tt > 0 else []))
                P.op("tensor",
                     lambda e, pb=pb: e.matmul(pb[:], ONES1[0:1, 0:128], VWA[0:1, 4096:4608],
                                               start=False, stop=True),
                     sig="scalar")
                P.op("scalar",
                     lambda e, pb=pb, tt=tt, half=half: e.copy(
                         VT[:].rearrange("p (t h) -> p t h", h=1040)
                         [:, tt, half * 520:half * 520 + 520]
                         .rearrange("p (g c) -> p g c", c=65)[:, :, 0:64],
                         pb[:]),
                     wait=["tensor"],
                     sig=["tensor"] + (["gpsimd"] if tt >= NTT - 2 else []))
        for qb in range(NQB):
            qev = []
            qcols = {}
            wa = wpush(dr["qw"][0, :, 0:512])
            wb = wpush(dr["qw"][0, :, 512:1024])
            qcols[0] = (wa, wb, P.count("#V"))
            for oc in range(NCH):
                if oc + 1 < NCH:
                    wa = wpush(dr["qw"][oc + 1, :, 0:512])
                    wb = wpush(dr["qw"][oc + 1, :, 512:1024])
                    qcols[oc + 1] = (wa, wb, P.count("#V"))
                wa, wb, vwq = qcols[oc]
                pb = PS[oc % 4]
                for mc in range(NCH):
                    colof = (wa if mc < 4 else wb) + (mc % 4) * 128
                    P.op("tensor",
                         lambda e, pb=pb, mc=mc, qb=qb, colof=colof: e.matmul(
                             pb[:], WB16[:, colof:colof + 128],
                             XLN[:, mc * S + qb * QB:mc * S + qb * QB + QB],
                             start=(mc == 0), stop=(mc == NCH - 1)),
                         wait=([("abs", "#V", vwq)] if mc == 0 else []) +
                              ([("abs", "#V", qev[oc - 4])] if mc == 0 and oc >= 4 else []) +
                              (["vector"] if mc == 0 and oc == 0 and qb > 0 else []),
                         sig=("vector" if mc == NCH - 1 else None))
                P.op("vector",
                     lambda e, pb=pb, oc=oc: e.tensor_scalar(
                         QSTG[:, oc * 512:(oc + 1) * 512], pb[:],
                         0.125, QB8S[:, oc:oc + 1], ALU.mult, ALU.add),
                     wait=["tensor"], sig=True)
                qev.append(P.count("#V"))
            for oc in range(NCH):
                P.op("vector",
                     lambda e, oc=oc, qb=qb: e.tensor_copy(
                         XLN[:, oc * S + qb * QB:oc * S + qb * QB + QB],
                         QSTG[:, oc * 512:(oc + 1) * 512]),
                     sig=(["tensor"] if oc == NCH - 1 else None))
        for wi, TS in ((0, XLN), (1, KF)):
            for qb in range(NQB):
                for c in range(NCH):
                    P.op("vector",
                         lambda e, TS=TS, c=c, qb=qb: e.tensor_mul(
                             PBUF[:, (c % 4) * 512:(c % 4) * 512 + 512],
                             TS[:, c * S + qb * QB:c * S + qb * QB + QB],
                             TS[:, c * S + qb * QB:c * S + qb * QB + QB]),
                         wait=([("tensor", 3)] if (c >= 4 or qb > 0 or wi > 0) else []) +
                              (["scalar"] if wi == 0 and qb == 0 and c == 0 else []),
                         sig="tensor")
                    P.op("tensor",
                         lambda e, c=c: e.matmul(
                             PS[4][0:16, :], BO16[:, c * 16:c * 16 + 16],
                             PBUF[:, (c % 4) * 512:(c % 4) * 512 + 512],
                             start=(c == 0), stop=(c == NCH - 1)),
                         wait=["vector"], sig=True)
                P.op("vector",
                     lambda e: e.reduce_max(STAT[0:16, 1:2], PS[4][0:16, :], mybir.AxisListType.X),
                     wait=["tensor"], sig="tensor")
                if qb == 0:
                    P.op("vector", lambda e, wi=wi: e.tensor_copy(
                        STAT[0:16, 4 + wi:5 + wi], STAT[0:16, 1:2]))
                else:
                    P.op("vector", lambda e, wi=wi: e.tensor_max(
                        STAT[0:16, 4 + wi:5 + wi], STAT[0:16, 4 + wi:5 + wi], STAT[0:16, 1:2]))
        P.op("vector", lambda e: e.tensor_mul(STAT[0:16, 6:7], STAT[0:16, 4:5], STAT[0:16, 5:6]),
             sig="scalar")
        P.op("scalar", lambda e: e.activation(STAT[0:16, 7:8], STAT[0:16, 6:7], AF.Ln),
             wait=["vector"])
        P.op("scalar", lambda e: e.activation(STAT[0:16, 8:9], STAT[0:16, 7:8], AF.Exp, scale=0.5),
             sig="vector")
        P.op("vector", lambda e: e.tensor_single_scalar(STAT[0:16, 9:10], STAT[0:16, 8:9],
                                                        -1.0, ALU.mult),
             wait=["scalar"])
        P.op("vector", lambda e: e.tensor_copy(BH16[:], STAT[0:16, 9:10]), sig="tensor")
        for h in range(NH):
            P.op("tensor",
                 lambda e, h=h: e.matmul(PS[5][:, h:h + 1],
                                         I128B[0:16, h:h + 1].broadcast_to([16, 128]),
                                         BH16[:], start=True, stop=True),
                 wait=(["vector"] if h == 0 else []),
                 sig=("scalar" if h == NH - 1 else None))
        P.op("scalar", lambda e: e.copy(BH[:], PS[5][:, 0:16]), wait=["tensor"],
             sig=["tensor", "vector"])

        P.wait("tensor", ["vector", "scalar", "#G"])
        for qb in range(NQB):
            nkt = 4 * qb + 4
            tile_ctr = [0]
            bank_rec = {0: 0, 1: 0, 2: 0, 3: 0}
            pair_tog = [0]

            def stile(h, kts, c0s):
                hc, hp = h // 2, (h % 2) * 64
                if len(kts) == 2:
                    b0 = pair_tog[0] * 2
                    pair_tog[0] ^= 1
                    banks = [b0, b0 + 1]
                    if tile_ctr[0] % 2:
                        tile_ctr[0] += 1
                else:
                    banks = [tile_ctr[0] % 4]
                slots = []
                for kt, c0, b in zip(kts, c0s, [banks[0], banks[-1]]):
                    sl = tile_ctr[0] % 4
                    tile_ctr[0] += 1
                    slots.append(sl)
                    w = []
                    if bank_rec[b] > 0:
                        w.append(("abs", "#S", bank_rec[b]))
                    P.op("tensor",
                         lambda e, b=b, hc=hc, hp=hp, kt=kt, qb=qb, c0=c0: e.matmul(
                             PS[b][:, c0:512],
                             KF[hp:hp + 64, hc * S + kt * 128:hc * S + kt * 128 + 128],
                             XLN[hp:hp + 64, hc * S + qb * QB + c0:hc * S + qb * QB + QB],
                             start=True, stop=True),
                         wait=w, sig=(None if (len(kts) == 2 and kt == kts[0]) else "scalar"))
                if len(kts) == 2:
                    b0, sl0 = banks[0], slots[0]
                    assert sl0 % 2 == 0
                    P.op("scalar",
                         lambda e, b0=b0, sl0=sl0, h=h: e.activation(
                             PRING[:, sl0 * 512:sl0 * 512 + 1024],
                             PSALL[:, b0 * 512:b0 * 512 + 1024],
                             AF.Exp, bias=BH[:, h:h + 1]),
                         wait=["tensor"], sig=["tensor"])
                    bank_rec[banks[0]] = bank_rec[banks[1]] = P.count("#S")
                else:
                    b, sl, c0 = banks[0], slots[0], c0s[0]
                    P.op("scalar",
                         lambda e, b=b, sl=sl, c0=c0, h=h: e.activation(
                             PRING[:, sl * 512 + c0:sl * 512 + 512], PS[b][:, c0:512],
                             AF.Exp, bias=BH[:, h:h + 1]),
                         wait=["tensor"], sig=["tensor"])
                    bank_rec[b] = P.count("#S")
                return slots

            def tri_mask(slots, c0s):
                sl, c0 = slots[0], c0s[0]
                P.op("vector",
                     lambda e, sl=sl, c0=c0: e.tensor_mul(
                         PRING[:, sl * 512 + c0:sl * 512 + c0 + 128],
                         PRING[:, sl * 512 + c0:sl * 512 + c0 + 128], TRI[:]),
                     wait=["scalar"], sig="tensor")

            def pvgrp(h, kts, c0s, slots, first, last):
                pvb = PS[6] if h % 2 == 0 else PS[7]
                diag = c0s[0] > 0 or kts[0] >= 4 * qb
                for kt, c0, sl in zip(kts, c0s, slots):
                    P.op("tensor",
                         lambda e, pvb=pvb, h=h, kt=kt, c0=c0, sl=sl,
                                st=(first and kt == kts[0]),
                                sp=(last and kt == kts[-1]): e.matmul(
                             pvb[0:65, c0:512],
                             VT[:, kt * 1040 + h * 65:kt * 1040 + h * 65 + 65],
                             PRING[:, sl * 512 + c0:sl * 512 + 512],
                             start=st, stop=sp),
                         wait=((["vector"] if diag else ["scalar"]) if kt == kts[0] else []) +
                              ([("abs", "#V", mult_rec[6 if h % 2 == 0 else 7])]
                               if first and kt == kts[0] else []),
                         sig=("scalar" if last and kt == kts[-1] else None))

            pend = []
            mult_rec = {6: 0, 7: 0}

            def flush_norm():
                for h, pvb, rt in pend:
                    hc, hp = h // 2, (h % 2) * 64
                    P.op("tensor",
                         lambda e, rt=rt: e.matmul(
                             PS[5][0:64, :], ONES1[0:1, 0:64], rt[:],
                             start=True, stop=True),
                         wait=["vector"], sig="scalar")
                    P.op("scalar",
                         lambda e: e.copy(RBM[:], PS[5][0:64, :]),
                         wait=["tensor", "vector"], sig=["tensor", "vector"])
                    P.op("vector",
                         lambda e, pvb=pvb, hc=hc, hp=hp: e.tensor_mul(
                             ATQ[hp:hp + 64, hc * 512:(hc + 1) * 512], pvb[0:64, :], RBM[:]),
                         wait=["scalar"], sig=["tensor"])
                    mult_rec[6 if hp == 0 else 7] = P.count("#V")
                del pend[:]

            for hp2 in range(NH // 2):
                h0, h1 = 2 * hp2, 2 * hp2 + 1
                chunks = []
                for k2 in range(0, 4 * qb, 2):
                    chunks.append(([k2, k2 + 1], [0, 0]))
                for j in range(4):
                    chunks.append(([4 * qb + j], [128 * j]))
                nch_ = len(chunks)
                for ci, (kts, c0s) in enumerate(chunks):
                    sl_h0 = stile(h0, kts, c0s)
                    if len(kts) == 1:
                        tri_mask(sl_h0, c0s)
                    sl_h1 = stile(h1, kts, c0s)
                    if len(kts) == 1:
                        tri_mask(sl_h1, c0s)
                    if ci == 0:
                        flush_norm()
                    pvgrp(h0, kts, c0s, sl_h0, ci == 0, ci == nch_ - 1)
                    pvgrp(h1, kts, c0s, sl_h1, ci == 0, ci == nch_ - 1)
                for h in (h0, h1):
                    pvb = PS[6] if h % 2 == 0 else PS[7]
                    rt = RROWM if h % 2 == 0 else RROWN
                    P.op("vector", lambda e, pvb=pvb: e.reciprocal(RR32F[:], pvb[64:65, :]),
                         wait=["tensor"], sig="tensor")
                    P.op("vector", lambda e, rt=rt: e.tensor_copy(rt[:], RR32F[:]), sig="tensor")
                    pend.append((h, pvb, rt))
            flush_norm()
            oev = []
            ocols = {}
            wa = wpush(dr["ow"][0, :, 0:512])
            wb = wpush(dr["ow"][0, :, 512:1024])
            ocols[0] = (wa, wb, P.count("#V"))
            for mc in range(NCH):
                if mc + 1 < NCH:
                    wa = wpush(dr["ow"][mc + 1, :, 0:512])
                    wb = wpush(dr["ow"][mc + 1, :, 512:1024])
                    ocols[mc + 1] = (wa, wb, P.count("#V"))
                wa, wb, vwo = ocols[mc]
                pb = PS[mc % 4]
                for hcc in range(NCH):
                    colof = (wa if hcc < 4 else wb) + (hcc % 4) * 128
                    P.op("tensor",
                         lambda e, pb=pb, hcc=hcc, colof=colof: e.matmul(
                             pb[:], WB16[:, colof:colof + 128], ATQ[:, hcc * 512:(hcc + 1) * 512],
                             start=(hcc == 0), stop=(hcc == NCH - 1)),
                         wait=([("abs", "#V", vwo)] if hcc == 0 else []) +
                              ([("abs", "#V", oev[mc - 4])] if hcc == 0 and mc >= 4 else []),
                         sig=("vector" if hcc == NCH - 1 else None))
                o = mc * S + qb * QB
                P.op("vector",
                     lambda e, pb=pb, o=o, mc=mc: e.scalar_tensor_tensor(
                         X[:, o:o + 512], pb[:], OB8[:, mc:mc + 1], X[:, o:o + 512],
                         ALU.add, ALU.add),
                     wait=["tensor"], sig=["tensor"])
                oev.append(P.count("#V"))
            P.wait("tensor", ["vector"])

    emit_ln("ln1")
    if dbg_dump("ln1"):
        return nc, P, dbg_outs
    emit_routing("r1")
    if dbg_dump("route1"):
        return nc, P, dbg_outs
    emit_tree("k", "KF", "t1k")
    if dbg_dump("ktree1"):
        return nc, P, dbg_outs
    emit_tree("v", "VT", "t1v", scale=0.5)
    if dbg_dump("vtree1"):
        return nc, P, dbg_outs
    emit_tree("q", "XLN", "t1q", scale=1.0 / 32.0)
    if dbg_dump("qtree1"):
        return nc, P, dbg_outs
    emit_fff_attn("a1")
    if dbg_dump("attn1"):
        return nc, P, dbg_outs
    emit_ln("ln2")
    if dbg_dump("ln2"):
        return nc, P, dbg_outs
    emit_mha("mha")
    if dbg_dump("mha"):
        return nc, P, dbg_outs
    emit_ln("ln3")
    if dbg_dump("ln3"):
        return nc, P, dbg_outs
    emit_routing("r2")
    if dbg_dump("route2"):
        return nc, P, dbg_outs
    emit_tree("k", "KF", "t2k")
    emit_tree("v", "VT", "t2v", scale=0.5)
    emit_tree("q", "XLN", "t2q", scale=1.0 / 32.0)
    if dbg_dump("trees2"):
        return nc, P, dbg_outs
    emit_fff_attn("a2", emit_output=True)
    for cn in ("o0", "o1"):
        s = P._chan(cn)
        P.q["sync"].append(lambda e, h=s[0], v=s[1]: e.wait_ge(h, v))
    dbg_dump("out")
    return nc, P, dbg_outs


def finalize(nc, P):
    with nc.Block() as block:
        @block.sync
        def _(e):
            for fn in P.q["sync"]:
                fn(e)

        @block.tensor
        def _(e):
            for fn in P.q["tensor"]:
                fn(e)

        @block.scalar
        def _(e):
            for fn in P.q["scalar"]:
                fn(e)

        @block.vector
        def _(e):
            for fn in P.q["vector"]:
                fn(e)

        @block.gpsimd
        def _(e):
            for fn in P.q["gpsimd"]:
                fn(e)
    return nc


_BUILD_CACHE = {}


def _get_program(upto=None, dbg=()):
    key = (upto, tuple(dbg))
    if key not in _BUILD_CACHE:
        nc, P, dbg_outs = build(upto=upto, dbg=dbg)
        finalize(nc, P)
        _BUILD_CACHE[key] = (nc, dbg_outs)
    return _BUILD_CACHE[key]


def run(inputs, upto=None, dbg=()):
    per_core = host_prepare(inputs)
    nc, dbg_outs = _get_program(upto=upto, dbg=dbg)
    res = run_bass_kernel_spmd(nc, per_core, core_ids=list(range(8)))
    return res, dbg_outs


def kernel(**inputs):
    res, _ = run(inputs)
    return np.stack([np.asarray(r["y"], np.float32) for r in res.results], axis=0)


# version 08
# speedup vs baseline: 1.1215x; 1.0204x over previous
import numpy as np
import concourse.bass as bass
import concourse.mybir as mybir
from concourse.bass_utils import run_bass_kernel_spmd

F32 = mybir.dt.float32
BF16 = mybir.dt.bfloat16
AF = mybir.ActivationFunctionType
ALU = mybir.AluOpType

S = 2048
D = 1024
NCH = 8
NQB = 4
QB = 512
NTT = 16
NH = 16
EPS = 1e-5

ENGINES = ("sync", "tensor", "scalar", "vector", "gpsimd")

_AUDIT = False
_AUDIT_ENGINES = ()


class Prog:

    def __init__(self, nc):
        self.nc = nc
        self.q = {e: [] for e in ENGINES}
        self.chans = {}

    def _chan(self, name):
        if name not in self.chans:
            self.chans[name] = [self.nc.alloc_semaphore(f"c_{name}"), 0]
        return self.chans[name]

    ENGCH = {"tensor": "T", "scalar": "S", "vector": "V", "gpsimd": "G"}

    def _mkwaits(self, wait):
        wl = []
        for w in wait:
            if isinstance(w, tuple) and len(w) == 3:
                _, src, v = w
                s = self._chan(src[1:] if src.startswith("#") else self.ENGCH[src])
                if v > 0:
                    wl.append((s[0], v))
                continue
            if isinstance(w, tuple):
                src, lag = w
            else:
                src, lag = w, 0
            name = src[1:] if src.startswith("#") else self.ENGCH[src]
            s = self._chan(name)
            v = s[1] - lag
            if v > 0:
                wl.append((s[0], v))
        return wl

    def op(self, eng, fn, wait=(), sig=None):
        wl = self._mkwaits(wait)
        if eng in ("gpsimd", "vector", "scalar") or (_AUDIT and eng != "sync") or eng in _AUDIT_ENGINES:
            s = self._chan(self.ENGCH[eng])
            if s[1] > 0:
                wl.append((s[0], s[1]))
            s[1] += 1
            sigs = [(s[0], 1)]
        elif sig is not None:
            s = self._chan(self.ENGCH[eng])
            s[1] += 1
            sigs = [(s[0], 1)]
        else:
            sigs = []

        def run(e, fn=fn, wl=wl, sigs=sigs):
            for h, v in wl:
                e.wait_ge(h, v)
            ins = fn(e)
            if sigs:
                ins.then_inc(sigs[0][0], sigs[0][1])

        self.q[eng].append(run)

    def count(self, name):
        return self._chan(name[1:] if name.startswith("#") else self.ENGCH[name])[1]

    def wait(self, eng, srcs):
        wl = self._mkwaits([srcs] if isinstance(srcs, (str, tuple)) else srcs)
        if wl:
            def run(e, wl=wl):
                for h, v in wl:
                    e.wait_ge(h, v)
            self.q[eng].append(run)

    def dma(self, out_ap, in_ap, chan, wait=()):
        s = self._chan(chan)
        s[1] += 16
        wl = self._mkwaits(wait)

        def run(e, wl=wl, h=s[0]):
            for hh, v in wl:
                e.wait_ge(hh, v)
            e.dma_start(out=out_ap, in_=in_ap).then_inc(h, 16)

        self.q["sync"].append(run)

    def barrier(self, engines=("tensor", "scalar", "vector", "gpsimd")):
        fence = {
            "tensor": lambda e: e.ldweights(self.fence_ap),
            "scalar": lambda e: e.copy(self.fence_sc[0:1, 1:2], self.fence_src),
            "vector": lambda e: e.memset(self.fence_sc[0:1, 2:3], 0.0),
            "gpsimd": lambda e: e.memset(self.fence_sc[0:1, 3:4], 0.0),
        }
        for a in engines:
            self.op(a, fence[a], wait=(["#G"] if a in ("tensor", "scalar") else []), sig=True)
        for a in engines:
            self.wait(a, [b for b in engines if b != a])


def ceil32(x):
    return (x + 31) & ~31


def host_prepare(inputs):
    g = {}
    for k, v in inputs.items():
        a = np.asarray(v)
        if a.dtype != np.float32:
            a = a.astype(np.float32)
        g[k] = a
    x = g["L0_attn_x"]
    out = {}
    for t in ("q", "k", "v"):
        w1 = g[f"L0_ffn_{t}_w1"]
        w1f = np.ascontiguousarray(w1.transpose(1, 0, 2).reshape(D, D))
        A = w1f.reshape(NCH, 128, NCH, 128)
        out[f"{t}w1"] = np.ascontiguousarray(A.transpose(2, 1, 0, 3).reshape(NCH, 128, D))
        w2f = g[f"L0_ffn_{t}_w2"].reshape(D, D)
        if t == "v":
            out["vw2"] = np.ascontiguousarray(w2f.reshape(NCH, 128, D))
        else:
            C = w2f.reshape(NCH, 128, NCH, 128)
            out[f"{t}w2"] = np.ascontiguousarray(C.transpose(2, 1, 0, 3).reshape(NCH, 128, D))
        out[f"{t}b1"] = np.ascontiguousarray(g[f"L0_ffn_{t}_b1"].reshape(NCH, 128).T)
        out[f"{t}b2"] = np.ascontiguousarray(g[f"L0_ffn_{t}_b2"])
    npk = np.zeros((128, 408), np.float32)
    for ti, t in enumerate(("q", "k", "v")):
        nwT = g[f"L0_ffn_{t}_node_w"].T.reshape(NCH, 128, 15)
        for c in range(NCH):
            npk[:, c * 45 + ti * 15:c * 45 + ti * 15 + 15] = nwT[c]
        npk[0, 360 + ti * 15:360 + ti * 15 + 15] = g[f"L0_ffn_{t}_node_b"]
    out["npk"] = npk
    qwf = np.ascontiguousarray(g["L0_attn_Q_w"].transpose(1, 0, 2).reshape(D, D))
    kwf = np.ascontiguousarray(g["L0_attn_K_w"].transpose(1, 0, 2).reshape(D, D))
    vwf = np.ascontiguousarray(g["L0_attn_V_w"].transpose(1, 0, 2).reshape(D, D))
    owf = np.ascontiguousarray(g["L0_attn_O_w"].reshape(D, D))
    for nm, wf in (("qw", qwf), ("kw", kwf), ("ow", owf)):
        Cw = wf.reshape(NCH, 128, NCH, 128)
        out[nm] = np.ascontiguousarray(Cw.transpose(2, 1, 0, 3).reshape(NCH, 128, D))
    out["vw"] = np.ascontiguousarray(vwf.reshape(NCH, 128, D))
    out["qb8"] = np.ascontiguousarray(g["L0_attn_Q_b"].reshape(D).reshape(NCH, 128).T)
    out["kb8"] = np.ascontiguousarray(g["L0_attn_K_b"].reshape(D).reshape(NCH, 128).T)
    out["ob8"] = np.ascontiguousarray(g["L0_attn_O_b"].reshape(NCH, 128).T)
    out["vbrow"] = np.ascontiguousarray(g["L0_attn_V_b"].reshape(1, D))
    per_core = []
    for b in range(8):
        m = dict(out)
        m["x"] = np.ascontiguousarray(x[b])
        per_core.append(m)
    return per_core


def build(upto=None, dbg=()):
    nc = bass.Bass("TRN2", target_bir_lowering=False, dynamic_dma_scratch_size=64,
                   detect_race_conditions=_AUDIT)
    P = Prog(nc)

    dr = {}
    dr["x"] = nc.dram_tensor("x", [S, D], F32, kind="ExternalInput").ap()
    for t in ("q", "k", "v"):
        for suf, shape in (("w1", [NCH, 128, D]), ("w2", [NCH, 128, D]),
                           ("b1", [128, 8]), ("b2", [16, D])):
            dr[f"{t}{suf}"] = nc.dram_tensor(f"{t}{suf}", shape, F32, kind="ExternalInput").ap()
    dr["npk"] = nc.dram_tensor("npk", [128, 408], F32, kind="ExternalInput").ap()
    for nm in ("qw", "kw", "vw", "ow"):
        dr[nm] = nc.dram_tensor(nm, [NCH, 128, D], F32, kind="ExternalInput").ap()
    for nm in ("qb8", "kb8", "ob8"):
        dr[nm] = nc.dram_tensor(nm, [128, 8], F32, kind="ExternalInput").ap()
    dr["vbrow"] = nc.dram_tensor("vbrow", [1, D], F32, kind="ExternalInput").ap()
    y = nc.dram_tensor("y", [S, D], F32, kind="ExternalOutput").ap()

    SBYTES = (nc.SBUF_PARTITION_SIZE_BYTES or 196608) - 32
    off = 512

    def at(name, shape, dtype, o=None):
        nonlocal off
        fixed = o is not None
        if o is None:
            o = off
        tb = {F32: 4, BF16: 2}[dtype]
        nbytes = int(np.prod(shape[1:])) * tb
        h = nc.alloc_sbuf_tensor_at(name, shape, dtype, offset=o)
        if not fixed:
            off = ceil32(o + nbytes)
        return h

    X = at("X", [128, NCH * S], F32)
    XLN = at("XLN", [128, NCH * S], BF16)
    KF = at("KF", [128, NCH * S], BF16)
    VT = at("VT", [128, NTT * 1040], BF16)
    SCRO = off
    PBUF = at("PBUF", [128, 8192], BF16, SCRO)
    SB = at("SB", [128, 2048], F32, SCRO)
    SFM = at("SFM", [128, 2048], F32, SCRO)
    ATQ = at("ATQ", [128, 4096], BF16, SCRO)
    VWA = at("VWA", [128, 4608], BF16, SCRO)
    XSTG0 = at("XSTG0", [128, 1024], F32, SCRO)
    XSTG1 = at("XSTG1", [128, 1024], F32, SCRO + 4096)
    MSB = at("MSB", [128, 2048], F32, SCRO + 8192)
    SALL = at("SALL", [128, 720], F32, SCRO + 8192)
    PRING = at("PRING", [128, 2048], BF16, SCRO + 8192)
    MUXT = at("MUXT", [128, 192], F32, SCRO + 12288)
    QSTG = at("QSTG", [128, 4096], BF16, SCRO + 12288)
    RROWM = at("RROWM", [1, 512], BF16, SCRO + 13312)
    RBM = at("RBM", [64, 512], F32, SCRO + 14336)
    RROWN = at("RROWN", [1, 512], BF16, SCRO + 12288)
    LEAFB = at("LEAFB", [128, 1024], BF16, SCRO + 16384)
    B2BF = at("B2BF", [16, 1024], BF16, SCRO + 16384)
    SCRN = at("SCRN", [128, 1024], F32, SCRO + 16384)
    RBF = at("RBF", [128, 512], F32, SCRO + 16384)
    E16H = at("E16H", [16, 1024], BF16, SCRO + 18432)
    MSK = at("MSK", [128, 1024], BF16, SCRO + 18432)
    LEAFT = at("LEAFT", [8, 128], BF16, SCRO + 18432)
    TMPF = at("TMPF", [128, 512], F32, SCRO + 18432)
    RROWF = at("RROWF", [1, 512], BF16, SCRO + 18432)
    off = ceil32(SCRO + 20480)
    WS32 = at("WS32", [128, 1024], F32)
    OSTG = at("OSTG", [128, 256], F32, off - 4096)
    WB16 = at("WB16", [128, 2048], BF16)
    ONES1 = at("ONES1", [1, 512], BF16)
    ONESC = at("ONESC", [128, 128], BF16)
    I128B = at("I128B", [128, 128], BF16)
    I128F = at("I128F", [128, 128], F32)
    TRI = at("TRI", [128, 128], BF16)
    IOTA16 = at("IOTA16", [128, 16], F32)
    C2 = at("C2", [128, 8], F32)
    BO16 = at("BO16", [128, 128], BF16)
    B1T = {t: at(f"B1{t}", [128, 8], F32) for t in ("q", "k", "v")}
    QB8 = at("QB8", [128, 8], F32)
    QB8S = at("QB8S", [128, 8], F32)
    KB8 = at("KB8", [128, 8], F32)
    OB8 = at("OB8", [128, 8], F32)
    LTF = at("LTF", [128, 16], F32)
    LT16 = {t: at(f"LT16{t}", [128, 16], BF16) for t in ("q", "k", "v")}
    LTFS = {t: at(f"LTFS{t}", [128, 16], F32) for t in ("q", "k", "v")}
    STAT = at("STAT", [128, 16], F32)
    STAIR = at("STAIR", [128, 32], BF16)
    PIDX = at("PIDX", [128, 1], F32)
    BN = at("BN", [128, 1], F32)
    BH = at("BH", [128, 16], F32)
    BH16 = at("BH16", [16, 1], BF16)
    ETOK = at("ETOK", [128, 16], BF16)
    ETOK1 = at("ETOK1", [1, 16], BF16)
    GT1 = at("GT1", [128, 128], F32)
    RR32F = at("RR32F", [1, 512], F32)
    EPSC = at("EPSC", [128, 1], F32)
    FEN = at("FEN", [128, 8], F32)
    P.fence_ap = ONESC[0:1, 0:1]
    P.fence_sc = FEN
    P.fence_src = EPSC[0:1, 0:1]
    assert off <= SBYTES, f"SBUF overflow: {off} > {SBYTES}"
    assert SCRO + 20480 <= SBYTES

    PSALL = nc.alloc_psum_tensor("psall", [128, 4096], F32)
    PS = [PSALL[:, i * 512:(i + 1) * 512] for i in range(8)]

    dbg_outs = {}
    named = {"X": X, "XLN": XLN, "KF": KF, "VT": VT, "PBUF": PBUF, "SALL": SALL,
             "SB": SB, "MSB": MSB, "SFM": SFM, "LEAFB": LEAFB, "E16H": E16H,
             "ATQ": ATQ, "BH": BH, "QSTG": QSTG, "STAT": STAT, "BN": BN,
             "LTFq": LTFS["q"], "LTFk": LTFS["k"], "LTFv": LTFS["v"]}

    def dbg_dump(phase):
        if upto == phase or any(p == phase for p, _ in dbg):
            P.barrier()
            P.op("scalar", lambda e: e.copy(P.fence_sc[0:1, 4:5], P.fence_src), wait=["#G"], sig=True)
            for p, name in dbg:
                if p != phase:
                    continue
                t = named[name]
                key = f"dbg_{phase}_{name}"
                shp = list(t.shape)
                d = nc.dram_tensor(key, shp, t.dtype, kind="ExternalOutput").ap()
                P.wait("sync", ["scalar"])
                P.dma(d, t[:], "dbg")
                dbg_outs[key] = tuple(shp)
            P.wait("scalar", ["#dbg"])
        return upto == phase

    cops = [
        lambda e: e.memset(FEN[:], 0.0),
        lambda e: e.memset(VT[:], 0.0),
        lambda e: e.memset(ONES1[:], 1.0),
        lambda e: e.memset(ONESC[:], 1.0),
        lambda e: e.memset(EPSC[:], EPS),
        lambda e: e.iota(I128B[:], [[1, 128]], channel_multiplier=-1,
                         allow_small_or_imprecise_dtypes=True),
        lambda e: e.tensor_single_scalar(I128B[:], I128B[:], 0.0, ALU.is_equal),
        lambda e: e.iota(I128F[:], [[1, 128]], channel_multiplier=-1,
                         allow_small_or_imprecise_dtypes=True),
        lambda e: e.tensor_single_scalar(I128F[:], I128F[:], 0.0, ALU.is_equal),
        lambda e: e.iota(TRI[:], [[1, 128]], channel_multiplier=-1,
                         allow_small_or_imprecise_dtypes=True),
        lambda e: e.tensor_single_scalar(TRI[:], TRI[:], 0.0, ALU.is_ge),
        lambda e: e.iota(IOTA16[:], [[1, 16]], channel_multiplier=0,
                         allow_small_or_imprecise_dtypes=True),
        lambda e: e.iota(GT1[:, 1:2], [[0, 1]], channel_multiplier=1,
                         allow_small_or_imprecise_dtypes=True),
        lambda e: e.tensor_single_scalar(GT1[:, 1:2], GT1[:, 1:2], 64.0, ALU.is_ge),
        lambda e: e.iota(C2[:], [[2, 8]], channel_multiplier=0,
                         allow_small_or_imprecise_dtypes=True),
        lambda e: e.tensor_tensor(C2[:], C2[:], GT1[:, 1:2].broadcast_to([128, 8]), ALU.add),
        lambda e: e.iota(GT1[:, 0:128], [[-2, 8], [1, 16]], channel_multiplier=0,
                         allow_small_or_imprecise_dtypes=True),
        lambda e: e.tensor_tensor(GT1[:, 0:128], GT1[:, 0:128],
                                  GT1[:, 1:2].broadcast_to([128, 128]), ALU.subtract),
        lambda e: e.tensor_single_scalar(BO16[:], GT1[:, 0:128], 0.0, ALU.is_equal),
        lambda e: e.iota(STAIR[:], [[1, 32]], channel_multiplier=0,
                         allow_small_or_imprecise_dtypes=True),
        lambda e: e.iota(PIDX[:], [[0, 1]], channel_multiplier=1,
                         allow_small_or_imprecise_dtypes=True),
        lambda e: e.tensor_single_scalar(STAIR[:], STAIR[:], 15.0, ALU.is_equal),
    ]
    for i, cop in enumerate(cops):
        P.op("gpsimd", cop,
             sig=(["tensor", "scalar", "vector"] if i == len(cops) - 1 else None))
    P.dma(QB8[:], dr["qb8"], "mc")
    P.dma(KB8[:], dr["kb8"], "mc")
    P.dma(OB8[:], dr["ob8"], "mc")
    for t in ("q", "k", "v"):
        P.dma(B1T[t][:], dr[f"{t}b1"], "mc")
    P.op("vector", lambda e: e.tensor_single_scalar(QB8S[:], QB8[:], 0.125, ALU.mult),
         wait=["#mc", "#G"], sig="scalar")

    xst = [XSTG0, XSTG1]
    for tt in range(NTT):
        sl = xst[tt % 2]
        P.dma(sl[:], dr["x"][tt * 128:(tt + 1) * 128, :], f"x{tt % 2}",
              wait=([("scalar", 1)] if tt >= 2 else []))
        for c in range(NCH):
            pb = PS[c % 2]
            P.op("tensor",
                 lambda e, pb=pb, sl=sl, c=c: e.transpose(
                     pb[:, 0:128], sl[:, c * 128:(c + 1) * 128], I128F[:]),
                 wait=(["#G"] if tt == 0 and c == 0 else []) +
                      ([f"#x{tt % 2}"] if c == 0 else []) +
                      ([("scalar", 1)] if (tt, c) > (0, 1) else []),
                 sig="scalar")
            P.op("scalar",
                 lambda e, pb=pb, c=c, tt=tt: e.copy(
                     X[:, c * S + tt * 128:c * S + tt * 128 + 128], pb[:, 0:128]),
                 wait=["tensor"], sig=["tensor"] + (["sync"] if c == 7 else []))
    if dbg_dump("xin"):
        return nc, P, dbg_outs

    wstate = {"n": 0, "gafter": [0, 0]}

    def wpush(dram_ap, ncols=512, dst16=None, extra_cast_wait=(), colw=None):
        u = wstate["n"]
        wstate["n"] += 1
        i32 = u % 2
        pr = dram_ap.shape[0]
        P.dma(WS32[0:pr, i32 * 512:i32 * 512 + ncols], dram_ap, f"w{i32}",
              wait=[("abs", "#V", wstate["gafter"][i32])])
        if dst16 is None:
            cw = colw or 512
            i16 = u % (2048 // cw)
            dst = WB16[0:pr, i16 * cw:i16 * cw + ncols]
            col = i16 * cw
            cw = ["tensor"]
        else:
            tns, col, prx = dst16
            dst = tns[0:pr, col:col + ncols]
        P.op("vector",
             lambda e, dst=dst, i32=i32, pr=pr, ncols=ncols: e.tensor_copy(
                 dst, WS32[0:pr, i32 * 512:i32 * 512 + ncols]),
             wait=[f"#w{i32}", "tensor"] + list(extra_cast_wait), sig=True)
        wstate["gafter"][i32] = P.count("#V")
        return col

    def emit_ln(tag):
        part = globals().get("_LN_PART")
        P.barrier()
        cast_rec = {}
        for c in range(NCH):
            eng = "gpsimd" if c < 4 else "vector"
            P.op(eng, lambda e, c=c: e.tensor_copy(XLN[:, c * S:(c + 1) * S], X[:, c * S:(c + 1) * S]),
                 sig=True)
            cast_rec[c] = ("#G" if c < 4 else "#V", P.count("#G" if c < 4 else "#V"))
        for qb in range(NQB):
            pm = PS[qb % 2]
            for c in range(NCH):
                P.op("tensor",
                     lambda e, pm=pm, c=c, qb=qb: e.matmul(
                         pm[:], ONESC[:], XLN[:, c * S + qb * QB:c * S + qb * QB + QB],
                         start=(c == 0), stop=(c == NCH - 1)),
                     wait=([("abs",) + cast_rec[c]] if qb == 0 else []) +
                          ([("vector", 1)] if qb >= 2 and c == 0 else []),
                     sig=("vector" if c == NCH - 1 else None))
            P.op("vector",
                 lambda e, pm=pm, qb=qb: e.tensor_single_scalar(
                     MSB[:, qb * QB:(qb + 1) * QB], pm[:], 1.0 / D, ALU.mult),
                 wait=["tensor"], sig="tensor")
        if part == "sums":
            return
        sq_rec = {}
        for c in range(NCH):
            if c < 4:
                P.op("scalar",
                     lambda e, c=c: e.activation(XLN[:, c * S:(c + 1) * S],
                                                 X[:, c * S:(c + 1) * S], AF.Square),
                     wait=(["tensor"] if c == 0 else []), sig=True)
                sq_rec[c] = ("#S", P.count("#S"))
            else:
                P.op("vector",
                     lambda e, c=c: e.tensor_mul(XLN[:, c * S:(c + 1) * S],
                                                 X[:, c * S:(c + 1) * S], X[:, c * S:(c + 1) * S]),
                     wait=(["tensor"] if c == 4 else []), sig=True)
                sq_rec[c] = ("#V", P.count("#V"))
        if part == "sq":
            return
        for qb in range(NQB):
            pv = PS[2 + qb % 2]
            q0, q1 = qb * QB, (qb + 1) * QB
            for c in range(NCH):
                P.op("tensor",
                     lambda e, pv=pv, c=c, qb=qb: e.matmul(
                         pv[:], ONESC[:], XLN[:, c * S + qb * QB:c * S + qb * QB + QB],
                         start=(c == 0), stop=(c == NCH - 1)),
                     wait=([("abs",) + sq_rec[c]] if qb == 0 else []) +
                          ([("vector", 2)] if qb >= 2 and c == 0 else []),
                     sig=("vector" if c == NCH - 1 else None))
            P.op("vector", lambda e, q0=q0, q1=q1: e.tensor_mul(
                SCRN[:, 0:512], MSB[:, q0:q1], MSB[:, q0:q1]))
            P.op("vector",
                 lambda e, pv=pv: e.scalar_tensor_tensor(
                     SCRN[:, 512:1024], pv[:], 1.0 / D, SCRN[:, 0:512],
                     ALU.mult, ALU.subtract),
                 wait=["tensor"], sig=["tensor", "scalar"])
            P.op("scalar", lambda e: e.activation(SCRN[:, 0:512], SCRN[:, 512:1024],
                                                  AF.Ln, bias=EPSC[:]),
                 wait=["vector"])
            P.op("scalar", lambda e, q0=q0, q1=q1: e.activation(
                SB[:, q0:q1], SCRN[:, 0:512], AF.Exp, scale=-0.5), sig="vector")
            P.op("vector", lambda e, q0=q0, q1=q1: e.tensor_mul(
                MSB[:, q0:q1], MSB[:, q0:q1], SB[:, q0:q1]),
                wait=["scalar"], sig="gpsimd")
        if part == "stats":
            return
        P.wait("gpsimd", ["vector"])
        pi = 0
        for c in range(NCH):
            for hh in range(4):
                o = c * S + hh * 512
                so = hh * 512 if False else (hh % 4) * 512
                so = (hh * 512) % 2048
                sl = (pi % 2) * 512
                P.op("gpsimd",
                     lambda e, o=o, so=so, sl=sl: e.tensor_mul(
                         SCRN[:, sl:sl + 512], X[:, o:o + 512], SB[:, so:so + 512]),
                     wait=([("vector", 1)] if pi >= 2 else []))
                P.op("vector",
                     lambda e, o=o, so=so, sl=sl: e.tensor_sub(
                         XLN[:, o:o + 512], SCRN[:, sl:sl + 512], MSB[:, so:so + 512]),
                     wait=["#G"],
                     sig=["gpsimd"] + (["tensor", "scalar"] if pi == 31 else []))
                pi += 1

    def emit_routing(tag):
        P.barrier()
        co = wpush(dr["npk"], ncols=408)
        for qb in range(NQB):
            pb = PS[4]
            for c in range(NCH):
                P.op("tensor",
                     lambda e, pb=pb, c=c, qb=qb, co=co: e.matmul(
                         pb[0:45, :], WB16[0:128, co + c * 45:co + c * 45 + 45],
                         XLN[:, c * S + qb * QB:c * S + qb * QB + QB],
                         start=(c == 0), stop=False),
                     wait=(["vector"] if qb == 0 and c == 0 else []) +
                          (["scalar"] if qb > 0 and c == 0 else []))
            P.op("tensor",
                 lambda e, pb=pb, co=co: e.matmul(
                     pb[0:45, :], WB16[0:1, co + 360:co + 405], ONES1[:],
                     start=False, stop=True),
                 sig="scalar")
            P.op("scalar",
                 lambda e, pb=pb, qb=qb: e.copy(SFM[0:45, qb * QB:(qb + 1) * QB], pb[0:45, :]),
                 wait=["tensor"], sig=["tensor"] + (["gpsimd"] if qb == NQB - 1 else []))
        for tt in range(NTT):
            pb = PS[5]
            P.op("tensor",
                 lambda e, pb=pb, tt=tt: e.transpose(
                     pb[0:128, 0:45], SFM[0:45, tt * 128:(tt + 1) * 128],
                     I128F[0:45, 0:45]),
                 wait=["scalar"], sig="scalar")
            P.op("scalar",
                 lambda e, pb=pb, tt=tt: e.copy(SALL[:, tt * 45:tt * 45 + 45], pb[0:128, 0:45]),
                 wait=["tensor"], sig=["tensor"] + (["vector"] if tt == NTT - 1 else []))

        def scol(col):
            return SALL[:, col:col + 45 * 15 + 1:45]

        def mcol(i):
            return MUXT[:, i * 16:(i + 1) * 16]

        for ti, t in enumerate(("q", "k", "v")):
            b0, b1, b2, b3 = mcol(0), mcol(1), mcol(2), mcol(3)
            t0, t1, t2, t3 = mcol(4), mcol(5), mcol(6), mcol(7)
            u0, u1 = mcol(8), mcol(9)
            base = ti * 15

            def V(fn, **kw):
                P.op("vector", fn, **kw)

            def mux(dst, a, b_, sel):
                V(lambda e, dst=dst, a=a, b_=b_: e.tensor_sub(dst, b_, a))
                V(lambda e, dst=dst, sel=sel: e.tensor_mul(dst, dst, sel))
                V(lambda e, dst=dst, a=a: e.tensor_add(dst, dst, a))

            V(lambda e, base=base, b0=b0: e.tensor_single_scalar(b0, scol(base + 0), 0.0, ALU.is_ge),
              wait=(["scalar"] if ti == 0 else []))
            mux(t0, scol(base + 1), scol(base + 2), b0)
            V(lambda e, b1=b1, t0=t0: e.tensor_single_scalar(b1, t0, 0.0, ALU.is_ge))
            mux(t0, scol(base + 3), scol(base + 4), b1)
            mux(t1, scol(base + 5), scol(base + 6), b1)
            mux(t2, t0, t1, b0)
            V(lambda e, b2=b2, t2=t2: e.tensor_single_scalar(b2, t2, 0.0, ALU.is_ge))
            mux(t0, scol(base + 7), scol(base + 8), b2)
            mux(t1, scol(base + 9), scol(base + 10), b2)
            mux(t2, scol(base + 11), scol(base + 12), b2)
            mux(t3, scol(base + 13), scol(base + 14), b2)
            mux(u0, t0, t1, b1)
            mux(u1, t2, t3, b1)
            mux(t0, u0, u1, b0)
            V(lambda e, b3=b3, t0=t0: e.tensor_single_scalar(b3, t0, 0.0, ALU.is_ge))
            V(lambda e, b0=b0, b1=b1: e.scalar_tensor_tensor(LTF[:], b0, 2.0, b1, ALU.mult, ALU.add))
            V(lambda e, b2=b2: e.scalar_tensor_tensor(LTF[:], LTF[:], 2.0, b2, ALU.mult, ALU.add))
            V(lambda e, b3=b3, t=t: e.scalar_tensor_tensor(LTFS[t][:], LTF[:], 2.0, b3, ALU.mult, ALU.add))
            V(lambda e, t=t: e.tensor_copy(LT16[t][:], LTFS[t][:]),
              sig=(["tensor", "scalar", "gpsimd"] if t == "v" else None))

    def emit_tree(tree, dst, tag, scale=1.0):
        P.barrier()
        for th in range(2):
            P.op("tensor",
                 lambda e, th=th, tree=tree: e.matmul(
                     PS[6][0:8, 0:128], LT16[tree][:, th * 8:th * 8 + 8], I128B[:],
                     start=True, stop=True),
                 wait=(["vector", "scalar", "#G"] if th == 1 else []),
                 sig="scalar")
            lb_state = {}

            def bstep(i):
                if i == 0:
                    P.op("scalar", lambda e: e.copy(LEAFT[:], PS[6][0:8, 0:128]),
                         wait=["tensor"], sig="tensor")
                elif i in (1, 2):
                    g = i - 1
                    for j in range(4):
                        tt8 = g * 4 + j
                        P.op("tensor",
                             lambda e, tt8=tt8, j=j: e.matmul(
                                 PS[5][:, j * 128:(j + 1) * 128],
                                 I128B[0:8, tt8:tt8 + 1].broadcast_to([8, 128]), LEAFT[:],
                                 start=True, stop=True),
                             wait=(["scalar"] if j == 0 else []),
                             sig=("scalar" if j == 3 else None))
                    P.op("scalar",
                         lambda e, g=g: e.copy(LEAFB[:, g * 512:(g + 1) * 512], PS[5][:]),
                         wait=["tensor"], sig=["tensor", "vector"])
                    if g == 1:
                        lb_state["done"] = P.count("#S")

            def masks(l):
                P.op("vector",
                     lambda e, l=l: e.tensor_single_scalar(MSK[:], LEAFB[:], C2[:, l:l + 1],
                                                           ALU.is_equal),
                     wait=[("abs", "#S", lb_state["done"])])
                P.op("vector",
                     lambda e, l=l: e.tensor_mul(PBUF[:, l * 1024:(l + 1) * 1024],
                                                 PBUF[:, l * 1024:(l + 1) * 1024], MSK[:]),
                     wait=["scalar"], sig=["tensor"])

            wcols = {}
            wa = wpush(dr[f"{tree}w1"][0, :, 0:512])
            wb = wpush(dr[f"{tree}w1"][0, :, 512:1024])
            wcols[0] = (wa, wb, P.count("#V"))
            for l in range(NCH):
                if l + 1 < NCH:
                    wa = wpush(dr[f"{tree}w1"][l + 1, :, 0:512])
                    wb = wpush(dr[f"{tree}w1"][l + 1, :, 512:1024])
                    wcols[l + 1] = (wa, wb, P.count("#V"))
                wa, wb, vw1 = wcols[l]
                for qh in range(2):
                    qb = th * 2 + qh
                    gi = (2 * l + qh) % 4
                    pb = PS[gi]
                    for c in range(NCH):
                        colof = (wa if c < 4 else wb) + (c % 4) * 128
                        P.op("tensor",
                             lambda e, pb=pb, c=c, qb=qb, colof=colof: e.matmul(
                                 pb[:], WB16[:, colof:colof + 128],
                                 XLN[:, c * S + qb * QB:c * S + qb * QB + QB],
                                 start=(c == 0), stop=(c == NCH - 1)),
                             wait=([("abs", "#V", vw1)] if c == 0 else []) +
                                  ([("scalar", 3)] if c == 0 else []),
                             sig=("scalar" if c == NCH - 1 else None))
                    P.op("scalar",
                         lambda e, pb=pb, l=l, qh=qh, tree=tree: e.activation(
                             PBUF[:, l * 1024 + qh * 512:l * 1024 + qh * 512 + 512],
                             pb[:], AF.Relu, bias=B1T[tree][:, l:l + 1]),
                         wait=["tensor"] + (["#mc"] if l == 0 and qh == 0 and th == 0 else []),
                         sig=["tensor"] + (["vector"] if qh == 1 else []))
                    if l == 0:
                        bstep(qh)
                if l == 1:
                    bstep(2)
                    masks(0)
                    masks(1)
                elif l >= 2:
                    masks(l)
            P.op("vector",
                 lambda e: e.tensor_single_scalar(E16H[:, 0:1024], LEAFB[0:16, :],
                                                  PIDX[0:16, 0:1], ALU.is_equal),
                 sig="tensor")
            e16v = P.count("#V")
            for half in range(2):
                wpush(dr[f"{tree}b2"][:, half * 512:half * 512 + 512],
                      dst16=(B2BF, half * 512, 16), extra_cast_wait=["scalar"])
            if dst == "VT":
                for g in range(4):
                    slots = []
                    for lcg in range(2):
                        for lcq in range(4):
                            lc = lcg * 4 + lcq
                            u = wpush(dr["vw2"][lc, :, g * 256:(g + 1) * 256], ncols=256, colw=256)
                            slots.append(u)
                    vwv2 = P.count("#V")
                    for lcg in range(2):
                        for tt8 in range(8):
                            tt = th * 8 + tt8
                            pb = PS[tt8]
                            for lcq in range(4):
                                lc = lcg * 4 + lcq
                                P.op("tensor",
                                     lambda e, pb=pb, lc=lc, tt8=tt8, g=g,
                                            cu=slots[lc]: e.matmul(
                                         pb[:, 0:256],
                                         PBUF[:, lc * 1024 + tt8 * 128:lc * 1024 + tt8 * 128 + 128],
                                         WB16[:, cu:cu + 256],
                                         start=(lc == 0), stop=False),
                                     wait=([("abs", "#V", vwv2)] if lc == 0 and tt8 == 0 and lcg == 0 else []) +
                                          (["vector"] if lc == 0 and tt8 == 0 and lcg == 0 and g == 0 else []) +
                                          (["scalar"] if lc == 0 and lcg == 0 and tt8 == 0 and g > 0 else []))
                            if lcg == 1:
                                P.op("tensor",
                                     lambda e, pb=pb, tt8=tt8, g=g: e.matmul(
                                         pb[:, 0:256],
                                         E16H[:, tt8 * 128:tt8 * 128 + 128],
                                         B2BF[:, g * 256:(g + 1) * 256],
                                         start=False, stop=True),
                                     wait=([("abs", "#V", e16v)] if tt8 == 0 and g == 0 else []),
                                     sig="scalar")
                                P.op("scalar",
                                     lambda e, pb=pb, tt=tt, g=g: e.activation(
                                         VT[:, tt * 1040 + g * 256:tt * 1040 + g * 256 + 256],
                                         pb[:, 0:256], AF.Copy, scale=scale),
                                     wait=["tensor"],
                                     sig=["tensor"] + (["gpsimd"] if tt8 >= 6 else []))
            else:
                dtens = KF if dst == "KF" else XLN
                w2cols = {}
                wa = wpush(dr[f"{tree}w2"][0, :, 0:512])
                wb = wpush(dr[f"{tree}w2"][0, :, 512:1024])
                w2cols[0] = (wa, wb, P.count("#V"))
                for oc in range(NCH):
                    if oc + 1 < NCH:
                        wa = wpush(dr[f"{tree}w2"][oc + 1, :, 0:512])
                        wb = wpush(dr[f"{tree}w2"][oc + 1, :, 512:1024])
                        w2cols[oc + 1] = (wa, wb, P.count("#V"))
                    wa, wb, vw2 = w2cols[oc]
                    for qh in range(2):
                        qb = th * 2 + qh
                        pb = PS[(2 * oc + qh) % 4]
                        for lc in range(NCH):
                            colof = (wa if lc < 4 else wb) + (lc % 4) * 128
                            P.op("tensor",
                                 lambda e, pb=pb, lc=lc, qh=qh, colof=colof: e.matmul(
                                     pb[:], WB16[:, colof:colof + 128],
                                     PBUF[:, lc * 1024 + qh * 512:lc * 1024 + qh * 512 + 512],
                                     start=(lc == 0), stop=False),
                                 wait=([("abs", "#V", vw2)] if lc == 0 else []) +
                                      ([("scalar", 3)] if lc == 0 else []))
                        P.op("tensor",
                             lambda e, pb=pb, oc=oc, qh=qh: e.matmul(
                                 pb[:], B2BF[:, oc * 128:(oc + 1) * 128],
                                 E16H[:, qh * 512:qh * 512 + 512],
                                 start=False, stop=True),
                             wait=([("abs", "#V", e16v)] if oc == 0 and qh == 0 else []),
                             sig="scalar")
                        P.op("scalar",
                             lambda e, pb=pb, oc=oc, qb=qb, dtens=dtens: e.activation(
                                 dtens[:, oc * S + qb * QB:oc * S + qb * QB + QB],
                                 pb[:], AF.Copy, scale=scale),
                             wait=["tensor"],
                             sig=["tensor"] + (["gpsimd"] if qh == 1 else []))

    def emit_fff_attn(tag, emit_output=False):
        P.barrier()
        for wi, TS in ((0, XLN), (1, KF)):
            for qb in range(NQB):
                for c in range(NCH):
                    P.op("vector",
                         lambda e, TS=TS, c=c, qb=qb: e.tensor_mul(
                             PBUF[:, (c % 4) * 512:(c % 4) * 512 + 512],
                             TS[:, c * S + qb * QB:c * S + qb * QB + QB],
                             TS[:, c * S + qb * QB:c * S + qb * QB + QB]),
                         wait=([("tensor", 3)] if (c >= 4 or qb > 0 or wi > 0) else []),
                         sig="tensor")
                    P.op("tensor",
                         lambda e, c=c: e.matmul(
                             PS[4][0:1, :], ONESC[:, 0:1],
                             PBUF[:, (c % 4) * 512:(c % 4) * 512 + 512],
                             start=(c == 0), stop=(c == NCH - 1)),
                         wait=["vector"], sig=True)
                P.op("vector",
                     lambda e: e.reduce_max(STAT[0:1, 1:2], PS[4][0:1, :], mybir.AxisListType.X),
                     wait=["tensor"], sig="tensor")
                if qb == 0:
                    P.op("vector", lambda e, wi=wi: e.tensor_copy(
                        STAT[0:1, 3 + wi:4 + wi], STAT[0:1, 1:2]))
                else:
                    P.op("vector", lambda e, wi=wi: e.tensor_max(
                        STAT[0:1, 3 + wi:4 + wi], STAT[0:1, 3 + wi:4 + wi], STAT[0:1, 1:2]))
        P.op("vector", lambda e: e.tensor_mul(STAT[0:1, 5:6], STAT[0:1, 3:4], STAT[0:1, 4:5]),
             sig="scalar")
        P.op("scalar", lambda e: e.activation(STAT[0:1, 6:7], STAT[0:1, 5:6], AF.Ln),
             wait=["vector"])
        P.op("scalar", lambda e: e.activation(STAT[0:1, 7:8], STAT[0:1, 6:7], AF.Exp, scale=0.5),
             sig="vector")
        P.op("vector", lambda e: e.tensor_single_scalar(STAT[0:1, 8:9], STAT[0:1, 7:8], -1.0, ALU.mult),
             wait=["scalar"])
        P.op("vector", lambda e: e.tensor_copy(ETOK1[0:1, 0:1], STAT[0:1, 8:9]), sig="tensor")
        P.op("tensor", lambda e: e.matmul(PS[5][:, 0:1], ONESC[0:1, :], ETOK1[0:1, 0:1],
                                          start=True, stop=True),
             wait=["vector"], sig="scalar")
        P.op("scalar", lambda e: e.copy(BN[:], PS[5][:, 0:1]), wait=["tensor"],
             sig=["tensor", "vector"])

        for qb in range(NQB):
            for kt in range(NTT):
                pb = PS[kt % 3]
                for c in range(NCH):
                    P.op("tensor",
                         lambda e, pb=pb, c=c, kt=kt, qb=qb: e.matmul(
                             pb[:], KF[:, c * S + kt * 128:c * S + kt * 128 + 128],
                             XLN[:, c * S + qb * QB:c * S + qb * QB + QB],
                             start=(c == 0), stop=(c == NCH - 1)),
                         wait=([("scalar", 2)] if c == 0 and kt >= 3 else []) +
                              (["vector", "scalar"] if c == 0 and kt == 0 and qb == 0 else []) +
                              (["vector"] if c == 0 and kt == 0 and qb > 0 else []),
                         sig=("scalar" if c == NCH - 1 else None))
                if kt > 0:
                    P.op("tensor",
                         lambda e, kt=kt: e.matmul(
                             PS[4][0:1, :], ONESC[:, 0:1],
                             PBUF[:, (kt - 1) * 512:kt * 512],
                             start=(kt - 1 == 0), stop=False),
                         wait=["scalar"])
                P.op("scalar",
                     lambda e, pb=pb, kt=kt: e.activation(
                         PBUF[:, kt * 512:(kt + 1) * 512], pb[:], AF.Exp, bias=BN[:]),
                     wait=["tensor"], sig="tensor")
            P.op("tensor",
                 lambda e: e.matmul(PS[4][0:1, :], ONESC[:, 0:1],
                                    PBUF[:, (NTT - 1) * 512:NTT * 512],
                                    start=False, stop=True),
                 wait=["scalar"], sig=True)
            P.op("vector", lambda e: e.reciprocal(RR32F[:], PS[4][0:1, :]),
                 wait=[("tensor", 0), "#G"], sig="tensor")
            P.op("vector", lambda e: e.tensor_copy(RROWF[:], RR32F[:]), sig="tensor")
            P.op("tensor", lambda e: e.matmul(PS[5][:], ONES1[0:1, 0:128], RROWF[:],
                                              start=True, stop=True),
                 wait=["vector", ("scalar", 0)], sig="scalar")
            P.op("scalar", lambda e: e.copy(RBF[:], PS[5][:]), wait=["tensor"],
                 sig=["tensor", "vector"])
            for dc in range(NCH):
                pb = PS[3 if dc % 2 == 0 else 7]
                for kt in range(NTT):
                    P.op("tensor",
                         lambda e, pb=pb, dc=dc, kt=kt: e.matmul(
                             pb[:], VT[:, kt * 1040 + dc * 128:kt * 1040 + dc * 128 + 128],
                             PBUF[:, kt * 512:(kt + 1) * 512],
                             start=(kt == 0), stop=(kt == NTT - 1)),
                         wait=([("vector", 2)] if kt == 0 and dc >= 2 else []),
                         sig=("vector" if kt == NTT - 1 else None))
                o = dc * S + qb * QB
                P.op("vector",
                     lambda e, pb=pb: e.tensor_mul(TMPF[:], pb[:], RBF[:]),
                     wait=["tensor", "#G"] + (["scalar"] if dc == 0 else []),
                     sig=["tensor", "gpsimd"])
                P.op("gpsimd",
                     lambda e, o=o: e.tensor_add(X[:, o:o + 512], X[:, o:o + 512], TMPF[:]),
                     wait=["vector"],
                     sig=["vector"] + (["tensor"] if emit_output else []))
                if emit_output:
                    for j in range(4):
                        P.op("tensor",
                             lambda e, o=o, j=j: e.transpose(
                                 PS[6][:, 0:128], X[:, o + j * 128:o + j * 128 + 128], I128F[:]),
                             wait=["#G"] + (["scalar"] if (dc, j) > (0, 0) or qb > 0 else []),
                             sig="scalar")
                        P.op("scalar",
                             lambda e, j=j: e.copy(OSTG[:, (j % 2) * 128:(j % 2) * 128 + 128],
                                                   PS[6][:, 0:128]),
                             wait=["tensor", f"#o{j % 2}"], sig=["tensor", "sync"])
                        P.dma(y[qb * QB + j * 128:qb * QB + j * 128 + 128,
                                dc * 128:dc * 128 + 128],
                              OSTG[:, (j % 2) * 128:(j % 2) * 128 + 128], f"o{j % 2}",
                              wait=["scalar"])

    def emit_mha(tag):
        P.barrier()
        kcols = {}
        wa = wpush(dr["kw"][0, :, 0:512])
        wb = wpush(dr["kw"][0, :, 512:1024])
        kcols[0] = (wa, wb, P.count("#V"))
        for oc in range(NCH):
            if oc + 1 < NCH:
                wa = wpush(dr["kw"][oc + 1, :, 0:512])
                wb = wpush(dr["kw"][oc + 1, :, 512:1024])
                kcols[oc + 1] = (wa, wb, P.count("#V"))
            wa, wb, vwk = kcols[oc]
            for qb in range(NQB):
                pb = PS[(4 * oc + qb) % 4]
                for mc in range(NCH):
                    colof = (wa if mc < 4 else wb) + (mc % 4) * 128
                    P.op("tensor",
                         lambda e, pb=pb, mc=mc, qb=qb, colof=colof: e.matmul(
                             pb[:], WB16[:, colof:colof + 128],
                             XLN[:, mc * S + qb * QB:mc * S + qb * QB + QB],
                             start=(mc == 0), stop=(mc == NCH - 1)),
                         wait=([("abs", "#V", vwk)] if mc == 0 and qb == 0 else []) +
                              ([("scalar", 3)] if mc == 0 else []),
                         sig=("scalar" if mc == NCH - 1 else None))
                P.op("scalar",
                     lambda e, pb=pb, oc=oc, qb=qb: e.add(
                         KF[:, oc * S + qb * QB:oc * S + qb * QB + QB], pb[:],
                         KB8[:, oc:oc + 1]),
                     wait=["tensor"],
                     sig=["tensor"] + (["gpsimd"] if qb == NQB - 1 else []))
        P.op("gpsimd", lambda e: e.memset(
            VT[:].rearrange("p (t h) -> p t h", h=1040)[:, :, 64:1040:65], 1.0),
            wait=["tensor", "scalar"], sig="tensor")
        for half in range(2):
            for mc in range(NCH):
                wpush(dr["vw"][mc, :, half * 512:half * 512 + 512],
                      dst16=(VWA, mc * 512, 128),
                      extra_cast_wait=(["tensor"] if half == 1 else []))
            wpush(dr["vbrow"][0:1, half * 512:half * 512 + 512],
                  dst16=(VWA, 4096, 1))
            vwv = P.count("#V")
            for tt in range(NTT):
                pb = PS[tt % 4]
                for mc in range(NCH):
                    P.op("tensor",
                         lambda e, pb=pb, mc=mc, tt=tt: e.matmul(
                             pb[:], XLN[:, mc * S + tt * 128:mc * S + tt * 128 + 128],
                             VWA[:, mc * 512:(mc + 1) * 512],
                             start=(mc == 0), stop=False),
                         wait=([("abs", "#V", vwv)] if mc == 0 and tt == 0 else []) +
                              ([("scalar", 3)] if mc == 0 and tt > 0 else []))
                P.op("tensor",
                     lambda e, pb=pb: e.matmul(pb[:], ONES1[0:1, 0:128], VWA[0:1, 4096:4608],
                                               start=False, stop=True),
                     sig="scalar")
                P.op("scalar",
                     lambda e, pb=pb, tt=tt, half=half: e.copy(
                         VT[:].rearrange("p (t h) -> p t h", h=1040)
                         [:, tt, half * 520:half * 520 + 520]
                         .rearrange("p (g c) -> p g c", c=65)[:, :, 0:64],
                         pb[:]),
                     wait=["tensor"],
                     sig=["tensor"] + (["gpsimd"] if tt >= NTT - 2 else []))
        for qb in range(NQB):
            qev = []
            qcols = {}
            wa = wpush(dr["qw"][0, :, 0:512])
            wb = wpush(dr["qw"][0, :, 512:1024])
            qcols[0] = (wa, wb, P.count("#V"))
            for oc in range(NCH):
                if oc + 1 < NCH:
                    wa = wpush(dr["qw"][oc + 1, :, 0:512])
                    wb = wpush(dr["qw"][oc + 1, :, 512:1024])
                    qcols[oc + 1] = (wa, wb, P.count("#V"))
                wa, wb, vwq = qcols[oc]
                pb = PS[oc % 4]
                for mc in range(NCH):
                    colof = (wa if mc < 4 else wb) + (mc % 4) * 128
                    P.op("tensor",
                         lambda e, pb=pb, mc=mc, qb=qb, colof=colof: e.matmul(
                             pb[:], WB16[:, colof:colof + 128],
                             XLN[:, mc * S + qb * QB:mc * S + qb * QB + QB],
                             start=(mc == 0), stop=(mc == NCH - 1)),
                         wait=([("abs", "#V", vwq)] if mc == 0 else []) +
                              ([("abs", "#V", qev[oc - 4])] if mc == 0 and oc >= 4 else []) +
                              (["vector"] if mc == 0 and oc == 0 and qb > 0 else []),
                         sig=("vector" if mc == NCH - 1 else None))
                P.op("vector",
                     lambda e, pb=pb, oc=oc: e.tensor_scalar(
                         QSTG[:, oc * 512:(oc + 1) * 512], pb[:],
                         0.125, QB8S[:, oc:oc + 1], ALU.mult, ALU.add),
                     wait=["tensor"], sig=True)
                qev.append(P.count("#V"))
            for oc in range(NCH):
                P.op("vector",
                     lambda e, oc=oc, qb=qb: e.tensor_copy(
                         XLN[:, oc * S + qb * QB:oc * S + qb * QB + QB],
                         QSTG[:, oc * 512:(oc + 1) * 512]),
                     sig=(["tensor"] if oc == NCH - 1 else None))
        for wi, TS in ((0, XLN), (1, KF)):
            for qb in range(NQB):
                for c in range(NCH):
                    P.op("vector",
                         lambda e, TS=TS, c=c, qb=qb: e.tensor_mul(
                             PBUF[:, (c % 4) * 512:(c % 4) * 512 + 512],
                             TS[:, c * S + qb * QB:c * S + qb * QB + QB],
                             TS[:, c * S + qb * QB:c * S + qb * QB + QB]),
                         wait=([("tensor", 3)] if (c >= 4 or qb > 0 or wi > 0) else []) +
                              (["scalar"] if wi == 0 and qb == 0 and c == 0 else []),
                         sig="tensor")
                    P.op("tensor",
                         lambda e, c=c: e.matmul(
                             PS[4][0:16, :], BO16[:, c * 16:c * 16 + 16],
                             PBUF[:, (c % 4) * 512:(c % 4) * 512 + 512],
                             start=(c == 0), stop=(c == NCH - 1)),
                         wait=["vector"], sig=True)
                P.op("vector",
                     lambda e: e.reduce_max(STAT[0:16, 1:2], PS[4][0:16, :], mybir.AxisListType.X),
                     wait=["tensor"], sig="tensor")
                if qb == 0:
                    P.op("vector", lambda e, wi=wi: e.tensor_copy(
                        STAT[0:16, 4 + wi:5 + wi], STAT[0:16, 1:2]))
                else:
                    P.op("vector", lambda e, wi=wi: e.tensor_max(
                        STAT[0:16, 4 + wi:5 + wi], STAT[0:16, 4 + wi:5 + wi], STAT[0:16, 1:2]))
        P.op("vector", lambda e: e.tensor_mul(STAT[0:16, 6:7], STAT[0:16, 4:5], STAT[0:16, 5:6]),
             sig="scalar")
        P.op("scalar", lambda e: e.activation(STAT[0:16, 7:8], STAT[0:16, 6:7], AF.Ln),
             wait=["vector"])
        P.op("scalar", lambda e: e.activation(STAT[0:16, 8:9], STAT[0:16, 7:8], AF.Exp, scale=0.5),
             sig="vector")
        P.op("vector", lambda e: e.tensor_single_scalar(STAT[0:16, 9:10], STAT[0:16, 8:9],
                                                        -1.0, ALU.mult),
             wait=["scalar"])
        P.op("vector", lambda e: e.tensor_copy(BH16[:], STAT[0:16, 9:10]), sig="tensor")
        for h in range(NH):
            P.op("tensor",
                 lambda e, h=h: e.matmul(PS[5][:, h:h + 1],
                                         I128B[0:16, h:h + 1].broadcast_to([16, 128]),
                                         BH16[:], start=True, stop=True),
                 wait=(["vector"] if h == 0 else []),
                 sig=("scalar" if h == NH - 1 else None))
        P.op("scalar", lambda e: e.copy(BH[:], PS[5][:, 0:16]), wait=["tensor"],
             sig=["tensor", "vector"])

        P.wait("tensor", ["vector", "scalar", "#G"])
        for qb in range(NQB):
            nkt = 4 * qb + 4
            tile_ctr = [0]
            bank_rec = {0: 0, 1: 0, 2: 0, 3: 0}
            pair_tog = [0]

            def stile(h, kts, c0s):
                hc, hp = h // 2, (h % 2) * 64
                if len(kts) == 2:
                    b0 = pair_tog[0] * 2
                    pair_tog[0] ^= 1
                    banks = [b0, b0 + 1]
                    if tile_ctr[0] % 2:
                        tile_ctr[0] += 1
                else:
                    banks = [tile_ctr[0] % 4]
                slots = []
                for kt, c0, b in zip(kts, c0s, [banks[0], banks[-1]]):
                    sl = tile_ctr[0] % 4
                    tile_ctr[0] += 1
                    slots.append(sl)
                    w = []
                    if bank_rec[b] > 0:
                        w.append(("abs", "#S", bank_rec[b]))
                    P.op("tensor",
                         lambda e, b=b, hc=hc, hp=hp, kt=kt, qb=qb, c0=c0: e.matmul(
                             PS[b][:, c0:512],
                             KF[hp:hp + 64, hc * S + kt * 128:hc * S + kt * 128 + 128],
                             XLN[hp:hp + 64, hc * S + qb * QB + c0:hc * S + qb * QB + QB],
                             start=True, stop=True),
                         wait=w, sig=(None if (len(kts) == 2 and kt == kts[0]) else "scalar"))
                if len(kts) == 2:
                    b0, sl0 = banks[0], slots[0]
                    assert sl0 % 2 == 0
                    P.op("scalar",
                         lambda e, b0=b0, sl0=sl0, h=h: e.activation(
                             PRING[:, sl0 * 512:sl0 * 512 + 1024],
                             PSALL[:, b0 * 512:b0 * 512 + 1024],
                             AF.Exp, bias=BH[:, h:h + 1]),
                         wait=["tensor"], sig=["tensor"])
                    bank_rec[banks[0]] = bank_rec[banks[1]] = P.count("#S")
                else:
                    b, sl, c0 = banks[0], slots[0], c0s[0]
                    P.op("scalar",
                         lambda e, b=b, sl=sl, c0=c0, h=h: e.activation(
                             PRING[:, sl * 512 + c0:sl * 512 + 512], PS[b][:, c0:512],
                             AF.Exp, bias=BH[:, h:h + 1]),
                         wait=["tensor"], sig=["tensor"])
                    bank_rec[b] = P.count("#S")
                return slots

            def tri_mask(slots, c0s):
                sl, c0 = slots[0], c0s[0]
                P.op("vector",
                     lambda e, sl=sl, c0=c0: e.tensor_mul(
                         PRING[:, sl * 512 + c0:sl * 512 + c0 + 128],
                         PRING[:, sl * 512 + c0:sl * 512 + c0 + 128], TRI[:]),
                     wait=["scalar"], sig="tensor")

            def pvgrp(h, kts, c0s, slots, first, last):
                pvb = PS[6] if h % 2 == 0 else PS[7]
                diag = c0s[0] > 0 or kts[0] >= 4 * qb
                for kt, c0, sl in zip(kts, c0s, slots):
                    P.op("tensor",
                         lambda e, pvb=pvb, h=h, kt=kt, c0=c0, sl=sl,
                                st=(first and kt == kts[0]),
                                sp=(last and kt == kts[-1]): e.matmul(
                             pvb[0:65, c0:512],
                             VT[:, kt * 1040 + h * 65:kt * 1040 + h * 65 + 65],
                             PRING[:, sl * 512 + c0:sl * 512 + 512],
                             start=st, stop=sp),
                         wait=((["vector"] if diag else ["scalar"]) if kt == kts[0] else []) +
                              ([("abs", "#V", mult_rec[6 if h % 2 == 0 else 7])]
                               if first and kt == kts[0] else []),
                         sig=("scalar" if last and kt == kts[-1] else None))

            pend = []
            mult_rec = {6: 0, 7: 0}

            def flush_norm():
                for h, pvb, rt in pend:
                    hc, hp = h // 2, (h % 2) * 64
                    P.op("tensor",
                         lambda e, rt=rt: e.matmul(
                             PS[5][0:64, :], ONES1[0:1, 0:64], rt[:],
                             start=True, stop=True),
                         wait=["vector"], sig="scalar")
                    P.op("scalar",
                         lambda e: e.copy(RBM[:], PS[5][0:64, :]),
                         wait=["tensor", "vector"], sig=["tensor", "vector"])
                    P.op("vector",
                         lambda e, pvb=pvb, hc=hc, hp=hp: e.tensor_mul(
                             ATQ[hp:hp + 64, hc * 512:(hc + 1) * 512], pvb[0:64, :], RBM[:]),
                         wait=["scalar"], sig=["tensor"])
                    mult_rec[6 if hp == 0 else 7] = P.count("#V")
                del pend[:]

            for hp2 in range(NH // 2):
                h0, h1 = 2 * hp2, 2 * hp2 + 1
                chunks = []
                for k2 in range(0, 4 * qb, 2):
                    chunks.append(([k2, k2 + 1], [0, 0]))
                for j in range(4):
                    chunks.append(([4 * qb + j], [128 * j]))
                nch_ = len(chunks)
                for ci, (kts, c0s) in enumerate(chunks):
                    sl_h0 = stile(h0, kts, c0s)
                    if len(kts) == 1:
                        tri_mask(sl_h0, c0s)
                    sl_h1 = stile(h1, kts, c0s)
                    if len(kts) == 1:
                        tri_mask(sl_h1, c0s)
                    if ci == 0:
                        flush_norm()
                    pvgrp(h0, kts, c0s, sl_h0, ci == 0, ci == nch_ - 1)
                    pvgrp(h1, kts, c0s, sl_h1, ci == 0, ci == nch_ - 1)
                for h in (h0, h1):
                    pvb = PS[6] if h % 2 == 0 else PS[7]
                    rt = RROWM if h % 2 == 0 else RROWN
                    P.op("vector", lambda e, pvb=pvb: e.reciprocal(RR32F[:], pvb[64:65, :]),
                         wait=["tensor"], sig="tensor")
                    P.op("vector", lambda e, rt=rt: e.tensor_copy(rt[:], RR32F[:]), sig="tensor")
                    pend.append((h, pvb, rt))
            flush_norm()
            oev = []
            ocols = {}
            wa = wpush(dr["ow"][0, :, 0:512])
            wb = wpush(dr["ow"][0, :, 512:1024])
            ocols[0] = (wa, wb, P.count("#V"))
            for mc in range(NCH):
                if mc + 1 < NCH:
                    wa = wpush(dr["ow"][mc + 1, :, 0:512])
                    wb = wpush(dr["ow"][mc + 1, :, 512:1024])
                    ocols[mc + 1] = (wa, wb, P.count("#V"))
                wa, wb, vwo = ocols[mc]
                pb = PS[mc % 4]
                for hcc in range(NCH):
                    colof = (wa if hcc < 4 else wb) + (hcc % 4) * 128
                    P.op("tensor",
                         lambda e, pb=pb, hcc=hcc, colof=colof: e.matmul(
                             pb[:], WB16[:, colof:colof + 128], ATQ[:, hcc * 512:(hcc + 1) * 512],
                             start=(hcc == 0), stop=(hcc == NCH - 1)),
                         wait=([("abs", "#V", vwo)] if hcc == 0 else []) +
                              ([("abs", "#V", oev[mc - 4])] if hcc == 0 and mc >= 4 else []),
                         sig=("vector" if hcc == NCH - 1 else None))
                o = mc * S + qb * QB
                P.op("vector",
                     lambda e, pb=pb, o=o, mc=mc: e.scalar_tensor_tensor(
                         X[:, o:o + 512], pb[:], OB8[:, mc:mc + 1], X[:, o:o + 512],
                         ALU.add, ALU.add),
                     wait=["tensor"], sig=["tensor"])
                oev.append(P.count("#V"))
            P.wait("tensor", ["vector"])

    emit_ln("ln1")
    if dbg_dump("ln1"):
        return nc, P, dbg_outs
    emit_routing("r1")
    if dbg_dump("route1"):
        return nc, P, dbg_outs
    emit_tree("k", "KF", "t1k")
    if dbg_dump("ktree1"):
        return nc, P, dbg_outs
    emit_tree("v", "VT", "t1v", scale=0.5)
    if dbg_dump("vtree1"):
        return nc, P, dbg_outs
    emit_tree("q", "XLN", "t1q", scale=1.0 / 32.0)
    if dbg_dump("qtree1"):
        return nc, P, dbg_outs
    emit_fff_attn("a1")
    if dbg_dump("attn1"):
        return nc, P, dbg_outs
    emit_ln("ln2")
    if dbg_dump("ln2"):
        return nc, P, dbg_outs
    emit_mha("mha")
    if dbg_dump("mha"):
        return nc, P, dbg_outs
    emit_ln("ln3")
    if dbg_dump("ln3"):
        return nc, P, dbg_outs
    emit_routing("r2")
    if dbg_dump("route2"):
        return nc, P, dbg_outs
    emit_tree("k", "KF", "t2k")
    emit_tree("v", "VT", "t2v", scale=0.5)
    emit_tree("q", "XLN", "t2q", scale=1.0 / 32.0)
    if dbg_dump("trees2"):
        return nc, P, dbg_outs
    emit_fff_attn("a2", emit_output=True)
    for cn in ("o0", "o1"):
        s = P._chan(cn)
        P.q["sync"].append(lambda e, h=s[0], v=s[1]: e.wait_ge(h, v))
    dbg_dump("out")
    return nc, P, dbg_outs


def finalize(nc, P):
    with nc.Block() as block:
        @block.sync
        def _(e):
            for fn in P.q["sync"]:
                fn(e)

        @block.tensor
        def _(e):
            for fn in P.q["tensor"]:
                fn(e)

        @block.scalar
        def _(e):
            for fn in P.q["scalar"]:
                fn(e)

        @block.vector
        def _(e):
            for fn in P.q["vector"]:
                fn(e)

        @block.gpsimd
        def _(e):
            for fn in P.q["gpsimd"]:
                fn(e)
    return nc


_BUILD_CACHE = {}


def _get_program(upto=None, dbg=()):
    key = (upto, tuple(dbg))
    if key not in _BUILD_CACHE:
        nc, P, dbg_outs = build(upto=upto, dbg=dbg)
        finalize(nc, P)
        _BUILD_CACHE[key] = (nc, dbg_outs)
    return _BUILD_CACHE[key]


def run(inputs, upto=None, dbg=()):
    per_core = host_prepare(inputs)
    nc, dbg_outs = _get_program(upto=upto, dbg=dbg)
    res = run_bass_kernel_spmd(nc, per_core, core_ids=list(range(8)))
    return res, dbg_outs


def kernel(**inputs):
    res, _ = run(inputs)
    return np.stack([np.asarray(r["y"], np.float32) for r in res.results], axis=0)


# version 09
# speedup vs baseline: 1.1232x; 1.0015x over previous
import numpy as np
import concourse.bass as bass
import concourse.mybir as mybir
from concourse.bass_utils import run_bass_kernel_spmd

F32 = mybir.dt.float32
BF16 = mybir.dt.bfloat16
AF = mybir.ActivationFunctionType
ALU = mybir.AluOpType

S = 2048
D = 1024
NCH = 8
NQB = 4
QB = 512
NTT = 16
NH = 16
EPS = 1e-5

ENGINES = ("sync", "tensor", "scalar", "vector", "gpsimd")

_AUDIT = False
_AUDIT_ENGINES = ()


class Prog:

    def __init__(self, nc):
        self.nc = nc
        self.q = {e: [] for e in ENGINES}
        self.chans = {}

    def _chan(self, name):
        if name not in self.chans:
            self.chans[name] = [self.nc.alloc_semaphore(f"c_{name}"), 0]
        return self.chans[name]

    ENGCH = {"tensor": "T", "scalar": "S", "vector": "V", "gpsimd": "G"}

    def _mkwaits(self, wait):
        wl = []
        for w in wait:
            if isinstance(w, tuple) and len(w) == 3:
                _, src, v = w
                s = self._chan(src[1:] if src.startswith("#") else self.ENGCH[src])
                if v > 0:
                    wl.append((s[0], v))
                continue
            if isinstance(w, tuple):
                src, lag = w
            else:
                src, lag = w, 0
            name = src[1:] if src.startswith("#") else self.ENGCH[src]
            s = self._chan(name)
            v = s[1] - lag
            if v > 0:
                wl.append((s[0], v))
        return wl

    def op(self, eng, fn, wait=(), sig=None):
        wl = self._mkwaits(wait)
        if eng in ("gpsimd", "vector", "scalar") or (_AUDIT and eng != "sync") or eng in _AUDIT_ENGINES:
            s = self._chan(self.ENGCH[eng])
            if s[1] > 0:
                wl.append((s[0], s[1]))
            s[1] += 1
            sigs = [(s[0], 1)]
        elif sig is not None:
            s = self._chan(self.ENGCH[eng])
            s[1] += 1
            sigs = [(s[0], 1)]
        else:
            sigs = []

        def run(e, fn=fn, wl=wl, sigs=sigs):
            for h, v in wl:
                e.wait_ge(h, v)
            ins = fn(e)
            if sigs:
                ins.then_inc(sigs[0][0], sigs[0][1])

        self.q[eng].append(run)

    def count(self, name):
        return self._chan(name[1:] if name.startswith("#") else self.ENGCH[name])[1]

    def wait(self, eng, srcs):
        wl = self._mkwaits([srcs] if isinstance(srcs, (str, tuple)) else srcs)
        if wl:
            def run(e, wl=wl):
                for h, v in wl:
                    e.wait_ge(h, v)
            self.q[eng].append(run)

    def dma(self, out_ap, in_ap, chan, wait=()):
        s = self._chan(chan)
        s[1] += 16
        wl = self._mkwaits(wait)

        def run(e, wl=wl, h=s[0]):
            for hh, v in wl:
                e.wait_ge(hh, v)
            e.dma_start(out=out_ap, in_=in_ap).then_inc(h, 16)

        self.q["sync"].append(run)

    def barrier(self, engines=("tensor", "scalar", "vector", "gpsimd")):
        fence = {
            "tensor": lambda e: e.ldweights(self.fence_ap),
            "scalar": lambda e: e.copy(self.fence_sc[0:1, 1:2], self.fence_src),
            "vector": lambda e: e.memset(self.fence_sc[0:1, 2:3], 0.0),
            "gpsimd": lambda e: e.memset(self.fence_sc[0:1, 3:4], 0.0),
        }
        for a in engines:
            self.op(a, fence[a], wait=(["#G"] if a in ("tensor", "scalar") else []), sig=True)
        for a in engines:
            self.wait(a, [b for b in engines if b != a])


def ceil32(x):
    return (x + 31) & ~31


def host_prepare(inputs):
    g = {}
    for k, v in inputs.items():
        a = np.asarray(v)
        if a.dtype != np.float32:
            a = a.astype(np.float32)
        g[k] = a
    x = g["L0_attn_x"]
    out = {}
    for t in ("q", "k", "v"):
        w1 = g[f"L0_ffn_{t}_w1"]
        w1f = np.ascontiguousarray(w1.transpose(1, 0, 2).reshape(D, D))
        A = w1f.reshape(NCH, 128, NCH, 128)
        out[f"{t}w1"] = np.ascontiguousarray(A.transpose(2, 1, 0, 3).reshape(NCH, 128, D))
        w2f = g[f"L0_ffn_{t}_w2"].reshape(D, D)
        if t == "v":
            out["vw2"] = np.ascontiguousarray(w2f.reshape(NCH, 128, D))
        else:
            C = w2f.reshape(NCH, 128, NCH, 128)
            out[f"{t}w2"] = np.ascontiguousarray(C.transpose(2, 1, 0, 3).reshape(NCH, 128, D))
        out[f"{t}b1"] = np.ascontiguousarray(g[f"L0_ffn_{t}_b1"].reshape(NCH, 128).T)
        out[f"{t}b2"] = np.ascontiguousarray(g[f"L0_ffn_{t}_b2"])
    npk = np.zeros((128, 408), np.float32)
    for ti, t in enumerate(("q", "k", "v")):
        nwT = g[f"L0_ffn_{t}_node_w"].T.reshape(NCH, 128, 15)
        for c in range(NCH):
            npk[:, c * 45 + ti * 15:c * 45 + ti * 15 + 15] = nwT[c]
        npk[0, 360 + ti * 15:360 + ti * 15 + 15] = g[f"L0_ffn_{t}_node_b"]
    out["npk"] = npk
    qwf = np.ascontiguousarray(g["L0_attn_Q_w"].transpose(1, 0, 2).reshape(D, D))
    kwf = np.ascontiguousarray(g["L0_attn_K_w"].transpose(1, 0, 2).reshape(D, D))
    vwf = np.ascontiguousarray(g["L0_attn_V_w"].transpose(1, 0, 2).reshape(D, D))
    owf = np.ascontiguousarray(g["L0_attn_O_w"].reshape(D, D))
    for nm, wf in (("qw", qwf), ("kw", kwf), ("ow", owf)):
        Cw = wf.reshape(NCH, 128, NCH, 128)
        out[nm] = np.ascontiguousarray(Cw.transpose(2, 1, 0, 3).reshape(NCH, 128, D))
    out["vw"] = np.ascontiguousarray(vwf.reshape(NCH, 128, D))
    out["qb8"] = np.ascontiguousarray(g["L0_attn_Q_b"].reshape(D).reshape(NCH, 128).T)
    out["kb8"] = np.ascontiguousarray(g["L0_attn_K_b"].reshape(D).reshape(NCH, 128).T)
    out["ob8"] = np.ascontiguousarray(g["L0_attn_O_b"].reshape(NCH, 128).T)
    out["vbrow"] = np.ascontiguousarray(g["L0_attn_V_b"].reshape(1, D))
    per_core = []
    for b in range(8):
        m = dict(out)
        m["x"] = np.ascontiguousarray(x[b])
        per_core.append(m)
    return per_core


def build(upto=None, dbg=()):
    nc = bass.Bass("TRN2", target_bir_lowering=False, dynamic_dma_scratch_size=64,
                   detect_race_conditions=_AUDIT)
    P = Prog(nc)

    dr = {}
    dr["x"] = nc.dram_tensor("x", [S, D], F32, kind="ExternalInput").ap()
    for t in ("q", "k", "v"):
        for suf, shape in (("w1", [NCH, 128, D]), ("w2", [NCH, 128, D]),
                           ("b1", [128, 8]), ("b2", [16, D])):
            dr[f"{t}{suf}"] = nc.dram_tensor(f"{t}{suf}", shape, F32, kind="ExternalInput").ap()
    dr["npk"] = nc.dram_tensor("npk", [128, 408], F32, kind="ExternalInput").ap()
    for nm in ("qw", "kw", "vw", "ow"):
        dr[nm] = nc.dram_tensor(nm, [NCH, 128, D], F32, kind="ExternalInput").ap()
    for nm in ("qb8", "kb8", "ob8"):
        dr[nm] = nc.dram_tensor(nm, [128, 8], F32, kind="ExternalInput").ap()
    dr["vbrow"] = nc.dram_tensor("vbrow", [1, D], F32, kind="ExternalInput").ap()
    y = nc.dram_tensor("y", [S, D], F32, kind="ExternalOutput").ap()

    SBYTES = (nc.SBUF_PARTITION_SIZE_BYTES or 196608) - 32
    off = 512

    def at(name, shape, dtype, o=None):
        nonlocal off
        fixed = o is not None
        if o is None:
            o = off
        tb = {F32: 4, BF16: 2}[dtype]
        nbytes = int(np.prod(shape[1:])) * tb
        h = nc.alloc_sbuf_tensor_at(name, shape, dtype, offset=o)
        if not fixed:
            off = ceil32(o + nbytes)
        return h

    X = at("X", [128, NCH * S], F32)
    XLN = at("XLN", [128, NCH * S], BF16)
    KF = at("KF", [128, NCH * S], BF16)
    VT = at("VT", [128, NTT * 1040], BF16)
    SCRO = off
    PBUF = at("PBUF", [128, 8192], BF16, SCRO)
    SB = at("SB", [128, 2048], F32, SCRO)
    SFM = at("SFM", [128, 2048], F32, SCRO)
    ATQ = at("ATQ", [128, 4096], BF16, SCRO)
    VWA = at("VWA", [128, 4608], BF16, SCRO)
    XSTG0 = at("XSTG0", [128, 1024], F32, SCRO)
    XSTG1 = at("XSTG1", [128, 1024], F32, SCRO + 4096)
    MSB = at("MSB", [128, 2048], F32, SCRO + 8192)
    SALL = at("SALL", [128, 720], F32, SCRO + 8192)
    PRING = at("PRING", [128, 2048], BF16, SCRO + 8192)
    MUXT = at("MUXT", [128, 192], F32, SCRO + 12288)
    QSTG = at("QSTG", [128, 4096], BF16, SCRO + 12288)
    RROWM = at("RROWM", [1, 512], BF16, SCRO + 13312)
    RBM = at("RBM", [64, 512], F32, SCRO + 14336)
    RROWN = at("RROWN", [1, 512], BF16, SCRO + 12288)
    LEAFB = at("LEAFB", [128, 1024], BF16, SCRO + 16384)
    B2BF = at("B2BF", [16, 1024], BF16, SCRO + 16384)
    SCRN = at("SCRN", [128, 1024], F32, SCRO + 16384)
    RBF = at("RBF", [128, 512], F32, SCRO + 16384)
    E16H = at("E16H", [16, 1024], BF16, SCRO + 18432)
    MSK = at("MSK", [128, 1024], BF16, SCRO + 18432)
    LEAFT = at("LEAFT", [8, 128], BF16, SCRO + 18432)
    TMPF = at("TMPF", [128, 512], F32, SCRO + 18432)
    RROWF = at("RROWF", [1, 512], BF16, SCRO + 18432)
    off = ceil32(SCRO + 20480)
    WS32 = at("WS32", [128, 1024], F32)
    OSTG = at("OSTG", [128, 256], F32, off - 4096)
    WB16 = at("WB16", [128, 3072], BF16)
    ONES1 = at("ONES1", [1, 512], BF16)
    ONESC = at("ONESC", [128, 128], BF16)
    I128B = at("I128B", [128, 128], BF16)
    I128F = at("I128F", [128, 128], F32)
    TRI = at("TRI", [128, 128], BF16)
    IOTA16 = at("IOTA16", [128, 16], F32)
    C2 = at("C2", [128, 8], F32)
    BO16 = at("BO16", [128, 128], BF16)
    B1T = {t: at(f"B1{t}", [128, 8], F32) for t in ("q", "k", "v")}
    QB8 = at("QB8", [128, 8], F32)
    QB8S = at("QB8S", [128, 8], F32)
    KB8 = at("KB8", [128, 8], F32)
    OB8 = at("OB8", [128, 8], F32)
    LTF = at("LTF", [128, 16], F32)
    LT16 = {t: at(f"LT16{t}", [128, 16], BF16) for t in ("q", "k", "v")}
    LTFS = {t: at(f"LTFS{t}", [128, 16], F32) for t in ("q", "k", "v")}
    STAT = at("STAT", [128, 16], F32)
    STAIR = at("STAIR", [128, 32], BF16)
    PIDX = at("PIDX", [128, 1], F32)
    BN = at("BN", [128, 1], F32)
    BH = at("BH", [128, 16], F32)
    BH16 = at("BH16", [16, 1], BF16)
    ETOK = at("ETOK", [128, 16], BF16)
    ETOK1 = at("ETOK1", [1, 16], BF16)
    GT1 = at("GT1", [128, 128], F32)
    RR32F = at("RR32F", [1, 512], F32, SCRO + 16384)
    EPSC = at("EPSC", [128, 1], F32)
    FEN = at("FEN", [128, 8], F32)
    P.fence_ap = ONESC[0:1, 0:1]
    P.fence_sc = FEN
    P.fence_src = EPSC[0:1, 0:1]
    assert off <= SBYTES, f"SBUF overflow: {off} > {SBYTES}"
    assert SCRO + 20480 <= SBYTES

    PSALL = nc.alloc_psum_tensor("psall", [128, 4096], F32)
    PS = [PSALL[:, i * 512:(i + 1) * 512] for i in range(8)]

    dbg_outs = {}
    named = {"X": X, "XLN": XLN, "KF": KF, "VT": VT, "PBUF": PBUF, "SALL": SALL,
             "SB": SB, "MSB": MSB, "SFM": SFM, "LEAFB": LEAFB, "E16H": E16H,
             "ATQ": ATQ, "BH": BH, "QSTG": QSTG, "STAT": STAT, "BN": BN,
             "LTFq": LTFS["q"], "LTFk": LTFS["k"], "LTFv": LTFS["v"]}

    def dbg_dump(phase):
        if upto == phase or any(p == phase for p, _ in dbg):
            P.barrier()
            P.op("scalar", lambda e: e.copy(P.fence_sc[0:1, 4:5], P.fence_src), wait=["#G"], sig=True)
            for p, name in dbg:
                if p != phase:
                    continue
                t = named[name]
                key = f"dbg_{phase}_{name}"
                shp = list(t.shape)
                d = nc.dram_tensor(key, shp, t.dtype, kind="ExternalOutput").ap()
                P.wait("sync", ["scalar"])
                P.dma(d, t[:], "dbg")
                dbg_outs[key] = tuple(shp)
            P.wait("scalar", ["#dbg"])
        return upto == phase

    cops = [
        lambda e: e.memset(FEN[:], 0.0),
        lambda e: e.memset(VT[:], 0.0),
        lambda e: e.memset(ONES1[:], 1.0),
        lambda e: e.memset(ONESC[:], 1.0),
        lambda e: e.memset(EPSC[:], EPS),
        lambda e: e.iota(I128B[:], [[1, 128]], channel_multiplier=-1,
                         allow_small_or_imprecise_dtypes=True),
        lambda e: e.tensor_single_scalar(I128B[:], I128B[:], 0.0, ALU.is_equal),
        lambda e: e.iota(I128F[:], [[1, 128]], channel_multiplier=-1,
                         allow_small_or_imprecise_dtypes=True),
        lambda e: e.tensor_single_scalar(I128F[:], I128F[:], 0.0, ALU.is_equal),
        lambda e: e.iota(TRI[:], [[1, 128]], channel_multiplier=-1,
                         allow_small_or_imprecise_dtypes=True),
        lambda e: e.tensor_single_scalar(TRI[:], TRI[:], 0.0, ALU.is_ge),
        lambda e: e.iota(IOTA16[:], [[1, 16]], channel_multiplier=0,
                         allow_small_or_imprecise_dtypes=True),
        lambda e: e.iota(GT1[:, 1:2], [[0, 1]], channel_multiplier=1,
                         allow_small_or_imprecise_dtypes=True),
        lambda e: e.tensor_single_scalar(GT1[:, 1:2], GT1[:, 1:2], 64.0, ALU.is_ge),
        lambda e: e.iota(C2[:], [[2, 8]], channel_multiplier=0,
                         allow_small_or_imprecise_dtypes=True),
        lambda e: e.tensor_tensor(C2[:], C2[:], GT1[:, 1:2].broadcast_to([128, 8]), ALU.add),
        lambda e: e.iota(GT1[:, 0:128], [[-2, 8], [1, 16]], channel_multiplier=0,
                         allow_small_or_imprecise_dtypes=True),
        lambda e: e.tensor_tensor(GT1[:, 0:128], GT1[:, 0:128],
                                  GT1[:, 1:2].broadcast_to([128, 128]), ALU.subtract),
        lambda e: e.tensor_single_scalar(BO16[:], GT1[:, 0:128], 0.0, ALU.is_equal),
        lambda e: e.iota(STAIR[:], [[1, 32]], channel_multiplier=0,
                         allow_small_or_imprecise_dtypes=True),
        lambda e: e.iota(PIDX[:], [[0, 1]], channel_multiplier=1,
                         allow_small_or_imprecise_dtypes=True),
        lambda e: e.tensor_single_scalar(STAIR[:], STAIR[:], 15.0, ALU.is_equal),
    ]
    for i, cop in enumerate(cops):
        P.op("gpsimd", cop,
             sig=(["tensor", "scalar", "vector"] if i == len(cops) - 1 else None))
    P.dma(QB8[:], dr["qb8"], "mc")
    P.dma(KB8[:], dr["kb8"], "mc")
    P.dma(OB8[:], dr["ob8"], "mc")
    for t in ("q", "k", "v"):
        P.dma(B1T[t][:], dr[f"{t}b1"], "mc")
    P.op("vector", lambda e: e.tensor_single_scalar(QB8S[:], QB8[:], 0.125, ALU.mult),
         wait=["#mc", "#G"], sig="scalar")

    xst = [XSTG0, XSTG1]
    for tt in range(NTT):
        sl = xst[tt % 2]
        P.dma(sl[:], dr["x"][tt * 128:(tt + 1) * 128, :], f"x{tt % 2}",
              wait=([("scalar", 1)] if tt >= 2 else []))
        for c in range(NCH):
            pb = PS[c % 2]
            P.op("tensor",
                 lambda e, pb=pb, sl=sl, c=c: e.transpose(
                     pb[:, 0:128], sl[:, c * 128:(c + 1) * 128], I128F[:]),
                 wait=(["#G"] if tt == 0 and c == 0 else []) +
                      ([f"#x{tt % 2}"] if c == 0 else []) +
                      ([("scalar", 1)] if (tt, c) > (0, 1) else []),
                 sig="scalar")
            P.op("scalar",
                 lambda e, pb=pb, c=c, tt=tt: e.copy(
                     X[:, c * S + tt * 128:c * S + tt * 128 + 128], pb[:, 0:128]),
                 wait=["tensor"], sig=["tensor"] + (["sync"] if c == 7 else []))
    if dbg_dump("xin"):
        return nc, P, dbg_outs

    wstate = {"n": 0, "gafter": [0, 0]}

    def wpush(dram_ap, ncols=512, dst16=None, extra_cast_wait=(), colw=None):
        u = wstate["n"]
        wstate["n"] += 1
        i32 = u % 2
        pr = dram_ap.shape[0]
        P.dma(WS32[0:pr, i32 * 512:i32 * 512 + ncols], dram_ap, f"w{i32}",
              wait=[("abs", "#V", wstate["gafter"][i32])])
        if dst16 is None:
            cw = colw or 512
            i16 = u % (3072 // cw)
            dst = WB16[0:pr, i16 * cw:i16 * cw + ncols]
            col = i16 * cw
            cw = ["tensor"]
        else:
            tns, col, prx = dst16
            dst = tns[0:pr, col:col + ncols]
        P.op("vector",
             lambda e, dst=dst, i32=i32, pr=pr, ncols=ncols: e.tensor_copy(
                 dst, WS32[0:pr, i32 * 512:i32 * 512 + ncols]),
             wait=[f"#w{i32}", "tensor"] + list(extra_cast_wait), sig=True)
        wstate["gafter"][i32] = P.count("#V")
        return col

    def emit_ln(tag):
        part = globals().get("_LN_PART")
        P.barrier()
        cast_rec = {}
        for c in range(NCH):
            eng = "gpsimd" if c < 4 else "vector"
            P.op(eng, lambda e, c=c: e.tensor_copy(XLN[:, c * S:(c + 1) * S], X[:, c * S:(c + 1) * S]),
                 sig=True)
            cast_rec[c] = ("#G" if c < 4 else "#V", P.count("#G" if c < 4 else "#V"))
        for qb in range(NQB):
            pm = PS[qb % 2]
            for c in range(NCH):
                P.op("tensor",
                     lambda e, pm=pm, c=c, qb=qb: e.matmul(
                         pm[:], ONESC[:], XLN[:, c * S + qb * QB:c * S + qb * QB + QB],
                         start=(c == 0), stop=(c == NCH - 1)),
                     wait=([("abs",) + cast_rec[c]] if qb == 0 else []) +
                          ([("vector", 1)] if qb >= 2 and c == 0 else []),
                     sig=("vector" if c == NCH - 1 else None))
            P.op("vector",
                 lambda e, pm=pm, qb=qb: e.tensor_single_scalar(
                     MSB[:, qb * QB:(qb + 1) * QB], pm[:], 1.0 / D, ALU.mult),
                 wait=["tensor"], sig="tensor")
        if part == "sums":
            return
        sq_rec = {}
        for c in range(NCH):
            if c < 4:
                P.op("scalar",
                     lambda e, c=c: e.activation(XLN[:, c * S:(c + 1) * S],
                                                 X[:, c * S:(c + 1) * S], AF.Square),
                     wait=(["tensor"] if c == 0 else []), sig=True)
                sq_rec[c] = ("#S", P.count("#S"))
            else:
                P.op("vector",
                     lambda e, c=c: e.tensor_mul(XLN[:, c * S:(c + 1) * S],
                                                 X[:, c * S:(c + 1) * S], X[:, c * S:(c + 1) * S]),
                     wait=(["tensor"] if c == 4 else []), sig=True)
                sq_rec[c] = ("#V", P.count("#V"))
        if part == "sq":
            return
        for qb in range(NQB):
            pv = PS[2 + qb % 2]
            q0, q1 = qb * QB, (qb + 1) * QB
            for c in range(NCH):
                P.op("tensor",
                     lambda e, pv=pv, c=c, qb=qb: e.matmul(
                         pv[:], ONESC[:], XLN[:, c * S + qb * QB:c * S + qb * QB + QB],
                         start=(c == 0), stop=(c == NCH - 1)),
                     wait=([("abs",) + sq_rec[c]] if qb == 0 else []) +
                          ([("vector", 2)] if qb >= 2 and c == 0 else []),
                     sig=("vector" if c == NCH - 1 else None))
            P.op("vector", lambda e, q0=q0, q1=q1: e.tensor_mul(
                SCRN[:, 0:512], MSB[:, q0:q1], MSB[:, q0:q1]))
            P.op("vector",
                 lambda e, pv=pv: e.scalar_tensor_tensor(
                     SCRN[:, 512:1024], pv[:], 1.0 / D, SCRN[:, 0:512],
                     ALU.mult, ALU.subtract),
                 wait=["tensor"], sig=["tensor", "scalar"])
            P.op("scalar", lambda e: e.activation(SCRN[:, 0:512], SCRN[:, 512:1024],
                                                  AF.Ln, bias=EPSC[:]),
                 wait=["vector"])
            P.op("scalar", lambda e, q0=q0, q1=q1: e.activation(
                SB[:, q0:q1], SCRN[:, 0:512], AF.Exp, scale=-0.5), sig="vector")
            P.op("vector", lambda e, q0=q0, q1=q1: e.tensor_mul(
                MSB[:, q0:q1], MSB[:, q0:q1], SB[:, q0:q1]),
                wait=["scalar"], sig="gpsimd")
        if part == "stats":
            return
        P.wait("gpsimd", ["vector"])
        pi = 0
        for c in range(NCH):
            for hh in range(4):
                o = c * S + hh * 512
                so = hh * 512 if False else (hh % 4) * 512
                so = (hh * 512) % 2048
                sl = (pi % 2) * 512
                P.op("gpsimd",
                     lambda e, o=o, so=so, sl=sl: e.tensor_mul(
                         SCRN[:, sl:sl + 512], X[:, o:o + 512], SB[:, so:so + 512]),
                     wait=([("vector", 1)] if pi >= 2 else []))
                P.op("vector",
                     lambda e, o=o, so=so, sl=sl: e.tensor_sub(
                         XLN[:, o:o + 512], SCRN[:, sl:sl + 512], MSB[:, so:so + 512]),
                     wait=["#G"],
                     sig=["gpsimd"] + (["tensor", "scalar"] if pi == 31 else []))
                pi += 1

    def emit_routing(tag):
        P.barrier()
        co = wpush(dr["npk"], ncols=408)
        for qb in range(NQB):
            pb = PS[4]
            for c in range(NCH):
                P.op("tensor",
                     lambda e, pb=pb, c=c, qb=qb, co=co: e.matmul(
                         pb[0:45, :], WB16[0:128, co + c * 45:co + c * 45 + 45],
                         XLN[:, c * S + qb * QB:c * S + qb * QB + QB],
                         start=(c == 0), stop=False),
                     wait=(["vector"] if qb == 0 and c == 0 else []) +
                          (["scalar"] if qb > 0 and c == 0 else []))
            P.op("tensor",
                 lambda e, pb=pb, co=co: e.matmul(
                     pb[0:45, :], WB16[0:1, co + 360:co + 405], ONES1[:],
                     start=False, stop=True),
                 sig="scalar")
            P.op("scalar",
                 lambda e, pb=pb, qb=qb: e.copy(SFM[0:45, qb * QB:(qb + 1) * QB], pb[0:45, :]),
                 wait=["tensor"], sig=["tensor"] + (["gpsimd"] if qb == NQB - 1 else []))
        for tt in range(NTT):
            pb = PS[5]
            P.op("tensor",
                 lambda e, pb=pb, tt=tt: e.transpose(
                     pb[0:128, 0:45], SFM[0:45, tt * 128:(tt + 1) * 128],
                     I128F[0:45, 0:45]),
                 wait=["scalar"], sig="scalar")
            P.op("scalar",
                 lambda e, pb=pb, tt=tt: e.copy(SALL[:, tt * 45:tt * 45 + 45], pb[0:128, 0:45]),
                 wait=["tensor"], sig=["tensor"] + (["vector"] if tt == NTT - 1 else []))

        def scol(col):
            return SALL[:, col:col + 45 * 15 + 1:45]

        def mcol(i):
            return MUXT[:, i * 16:(i + 1) * 16]

        for ti, t in enumerate(("q", "k", "v")):
            b0, b1, b2, b3 = mcol(0), mcol(1), mcol(2), mcol(3)
            t0, t1, t2, t3 = mcol(4), mcol(5), mcol(6), mcol(7)
            u0, u1 = mcol(8), mcol(9)
            base = ti * 15

            def V(fn, **kw):
                P.op("vector", fn, **kw)

            def mux(dst, a, b_, sel):
                V(lambda e, dst=dst, a=a, b_=b_: e.tensor_sub(dst, b_, a))
                V(lambda e, dst=dst, sel=sel: e.tensor_mul(dst, dst, sel))
                V(lambda e, dst=dst, a=a: e.tensor_add(dst, dst, a))

            V(lambda e, base=base, b0=b0: e.tensor_single_scalar(b0, scol(base + 0), 0.0, ALU.is_ge),
              wait=(["scalar"] if ti == 0 else []))
            mux(t0, scol(base + 1), scol(base + 2), b0)
            V(lambda e, b1=b1, t0=t0: e.tensor_single_scalar(b1, t0, 0.0, ALU.is_ge))
            mux(t0, scol(base + 3), scol(base + 4), b1)
            mux(t1, scol(base + 5), scol(base + 6), b1)
            mux(t2, t0, t1, b0)
            V(lambda e, b2=b2, t2=t2: e.tensor_single_scalar(b2, t2, 0.0, ALU.is_ge))
            mux(t0, scol(base + 7), scol(base + 8), b2)
            mux(t1, scol(base + 9), scol(base + 10), b2)
            mux(t2, scol(base + 11), scol(base + 12), b2)
            mux(t3, scol(base + 13), scol(base + 14), b2)
            mux(u0, t0, t1, b1)
            mux(u1, t2, t3, b1)
            mux(t0, u0, u1, b0)
            V(lambda e, b3=b3, t0=t0: e.tensor_single_scalar(b3, t0, 0.0, ALU.is_ge))
            V(lambda e, b0=b0, b1=b1: e.scalar_tensor_tensor(LTF[:], b0, 2.0, b1, ALU.mult, ALU.add))
            V(lambda e, b2=b2: e.scalar_tensor_tensor(LTF[:], LTF[:], 2.0, b2, ALU.mult, ALU.add))
            V(lambda e, b3=b3, t=t: e.scalar_tensor_tensor(LTFS[t][:], LTF[:], 2.0, b3, ALU.mult, ALU.add))
            V(lambda e, t=t: e.tensor_copy(LT16[t][:], LTFS[t][:]),
              sig=(["tensor", "scalar", "gpsimd"] if t == "v" else None))

    def emit_tree(tree, dst, tag, scale=1.0):
        P.barrier()
        for th in range(2):
            P.op("tensor",
                 lambda e, th=th, tree=tree: e.matmul(
                     PS[6][0:8, 0:128], LT16[tree][:, th * 8:th * 8 + 8], I128B[:],
                     start=True, stop=True),
                 wait=(["vector", "scalar", "#G"] if th == 1 else []),
                 sig="scalar")
            lb_state = {}

            def bstep(i):
                if i == 0:
                    P.op("scalar", lambda e: e.copy(LEAFT[:], PS[6][0:8, 0:128]),
                         wait=["tensor"], sig="tensor")
                elif i in (1, 2):
                    g = i - 1
                    for j in range(4):
                        tt8 = g * 4 + j
                        P.op("tensor",
                             lambda e, tt8=tt8, j=j: e.matmul(
                                 PS[5][:, j * 128:(j + 1) * 128],
                                 I128B[0:8, tt8:tt8 + 1].broadcast_to([8, 128]), LEAFT[:],
                                 start=True, stop=True),
                             wait=(["scalar"] if j == 0 else []),
                             sig=("scalar" if j == 3 else None))
                    P.op("scalar",
                         lambda e, g=g: e.copy(LEAFB[:, g * 512:(g + 1) * 512], PS[5][:]),
                         wait=["tensor"], sig=["tensor", "vector"])
                    if g == 1:
                        lb_state["done"] = P.count("#S")

            def masks(l):
                P.op("vector",
                     lambda e, l=l: e.tensor_single_scalar(MSK[:], LEAFB[:], C2[:, l:l + 1],
                                                           ALU.is_equal),
                     wait=[("abs", "#S", lb_state["done"])])
                P.op("vector",
                     lambda e, l=l: e.tensor_mul(PBUF[:, l * 1024:(l + 1) * 1024],
                                                 PBUF[:, l * 1024:(l + 1) * 1024], MSK[:]),
                     wait=["scalar"], sig=["tensor"])

            wcols = {}
            wa = wpush(dr[f"{tree}w1"][0, :, 0:512])
            wb = wpush(dr[f"{tree}w1"][0, :, 512:1024])
            wcols[0] = (wa, wb, P.count("#V"))
            for l in range(NCH):
                if l + 1 < NCH:
                    wa = wpush(dr[f"{tree}w1"][l + 1, :, 0:512])
                    wb = wpush(dr[f"{tree}w1"][l + 1, :, 512:1024])
                    wcols[l + 1] = (wa, wb, P.count("#V"))
                wa, wb, vw1 = wcols[l]
                for qh in range(2):
                    qb = th * 2 + qh
                    gi = (2 * l + qh) % 4
                    pb = PS[gi]
                    for c in range(NCH):
                        colof = (wa if c < 4 else wb) + (c % 4) * 128
                        P.op("tensor",
                             lambda e, pb=pb, c=c, qb=qb, colof=colof: e.matmul(
                                 pb[:], WB16[:, colof:colof + 128],
                                 XLN[:, c * S + qb * QB:c * S + qb * QB + QB],
                                 start=(c == 0), stop=(c == NCH - 1)),
                             wait=([("abs", "#V", vw1)] if c == 0 else []) +
                                  ([("scalar", 3)] if c == 0 else []),
                             sig=("scalar" if c == NCH - 1 else None))
                    P.op("scalar",
                         lambda e, pb=pb, l=l, qh=qh, tree=tree: e.activation(
                             PBUF[:, l * 1024 + qh * 512:l * 1024 + qh * 512 + 512],
                             pb[:], AF.Relu, bias=B1T[tree][:, l:l + 1]),
                         wait=["tensor"] + (["#mc"] if l == 0 and qh == 0 and th == 0 else []),
                         sig=["tensor"] + (["vector"] if qh == 1 else []))
                    if l == 0:
                        bstep(qh)
                if l == 1:
                    bstep(2)
                    masks(0)
                    masks(1)
                elif l >= 2:
                    masks(l)
            P.op("vector",
                 lambda e: e.tensor_single_scalar(E16H[:, 0:1024], LEAFB[0:16, :],
                                                  PIDX[0:16, 0:1], ALU.is_equal),
                 sig="tensor")
            e16v = P.count("#V")
            for half in range(2):
                wpush(dr[f"{tree}b2"][:, half * 512:half * 512 + 512],
                      dst16=(B2BF, half * 512, 16), extra_cast_wait=["scalar"])
            if dst == "VT":
                for g in range(4):
                    slots = []
                    for lcg in range(2):
                        for lcq in range(4):
                            lc = lcg * 4 + lcq
                            u = wpush(dr["vw2"][lc, :, g * 256:(g + 1) * 256], ncols=256, colw=256)
                            slots.append(u)
                    vwv2 = P.count("#V")
                    for lcg in range(2):
                        for tt8 in range(8):
                            tt = th * 8 + tt8
                            pb = PS[tt8]
                            for lcq in range(4):
                                lc = lcg * 4 + lcq
                                P.op("tensor",
                                     lambda e, pb=pb, lc=lc, tt8=tt8, g=g,
                                            cu=slots[lc]: e.matmul(
                                         pb[:, 0:256],
                                         PBUF[:, lc * 1024 + tt8 * 128:lc * 1024 + tt8 * 128 + 128],
                                         WB16[:, cu:cu + 256],
                                         start=(lc == 0), stop=False),
                                     wait=([("abs", "#V", vwv2)] if lc == 0 and tt8 == 0 and lcg == 0 else []) +
                                          (["vector"] if lc == 0 and tt8 == 0 and lcg == 0 and g == 0 else []) +
                                          (["scalar"] if lc == 0 and lcg == 0 and tt8 == 0 and g > 0 else []))
                            if lcg == 1:
                                P.op("tensor",
                                     lambda e, pb=pb, tt8=tt8, g=g: e.matmul(
                                         pb[:, 0:256],
                                         E16H[:, tt8 * 128:tt8 * 128 + 128],
                                         B2BF[:, g * 256:(g + 1) * 256],
                                         start=False, stop=True),
                                     wait=([("abs", "#V", e16v)] if tt8 == 0 and g == 0 else []),
                                     sig="scalar")
                                P.op("scalar",
                                     lambda e, pb=pb, tt=tt, g=g: e.activation(
                                         VT[:, tt * 1040 + g * 256:tt * 1040 + g * 256 + 256],
                                         pb[:, 0:256], AF.Copy, scale=scale),
                                     wait=["tensor"],
                                     sig=["tensor"] + (["gpsimd"] if tt8 >= 6 else []))
            else:
                dtens = KF if dst == "KF" else XLN
                w2cols = {}
                wa = wpush(dr[f"{tree}w2"][0, :, 0:512])
                wb = wpush(dr[f"{tree}w2"][0, :, 512:1024])
                w2cols[0] = (wa, wb, P.count("#V"))
                for oc in range(NCH):
                    if oc + 1 < NCH:
                        wa = wpush(dr[f"{tree}w2"][oc + 1, :, 0:512])
                        wb = wpush(dr[f"{tree}w2"][oc + 1, :, 512:1024])
                        w2cols[oc + 1] = (wa, wb, P.count("#V"))
                    wa, wb, vw2 = w2cols[oc]
                    for qh in range(2):
                        qb = th * 2 + qh
                        pb = PS[(2 * oc + qh) % 4]
                        for lc in range(NCH):
                            colof = (wa if lc < 4 else wb) + (lc % 4) * 128
                            P.op("tensor",
                                 lambda e, pb=pb, lc=lc, qh=qh, colof=colof: e.matmul(
                                     pb[:], WB16[:, colof:colof + 128],
                                     PBUF[:, lc * 1024 + qh * 512:lc * 1024 + qh * 512 + 512],
                                     start=(lc == 0), stop=False),
                                 wait=([("abs", "#V", vw2)] if lc == 0 else []) +
                                      ([("scalar", 3)] if lc == 0 else []))
                        P.op("tensor",
                             lambda e, pb=pb, oc=oc, qh=qh: e.matmul(
                                 pb[:], B2BF[:, oc * 128:(oc + 1) * 128],
                                 E16H[:, qh * 512:qh * 512 + 512],
                                 start=False, stop=True),
                             wait=([("abs", "#V", e16v)] if oc == 0 and qh == 0 else []),
                             sig="scalar")
                        P.op("scalar",
                             lambda e, pb=pb, oc=oc, qb=qb, dtens=dtens: e.activation(
                                 dtens[:, oc * S + qb * QB:oc * S + qb * QB + QB],
                                 pb[:], AF.Copy, scale=scale),
                             wait=["tensor"],
                             sig=["tensor"] + (["gpsimd"] if qh == 1 else []))

    def emit_fff_attn(tag, emit_output=False):
        P.barrier()
        for wi, TS in ((0, XLN), (1, KF)):
            for qb in range(NQB):
                for c in range(NCH):
                    P.op("vector",
                         lambda e, TS=TS, c=c, qb=qb: e.tensor_mul(
                             PBUF[:, (c % 4) * 512:(c % 4) * 512 + 512],
                             TS[:, c * S + qb * QB:c * S + qb * QB + QB],
                             TS[:, c * S + qb * QB:c * S + qb * QB + QB]),
                         wait=([("tensor", 3)] if (c >= 4 or qb > 0 or wi > 0) else []),
                         sig="tensor")
                    P.op("tensor",
                         lambda e, c=c: e.matmul(
                             PS[4][0:1, :], ONESC[:, 0:1],
                             PBUF[:, (c % 4) * 512:(c % 4) * 512 + 512],
                             start=(c == 0), stop=(c == NCH - 1)),
                         wait=["vector"], sig=True)
                P.op("vector",
                     lambda e: e.reduce_max(STAT[0:1, 1:2], PS[4][0:1, :], mybir.AxisListType.X),
                     wait=["tensor"], sig="tensor")
                if qb == 0:
                    P.op("vector", lambda e, wi=wi: e.tensor_copy(
                        STAT[0:1, 3 + wi:4 + wi], STAT[0:1, 1:2]))
                else:
                    P.op("vector", lambda e, wi=wi: e.tensor_max(
                        STAT[0:1, 3 + wi:4 + wi], STAT[0:1, 3 + wi:4 + wi], STAT[0:1, 1:2]))
        P.op("vector", lambda e: e.tensor_mul(STAT[0:1, 5:6], STAT[0:1, 3:4], STAT[0:1, 4:5]),
             sig="scalar")
        P.op("scalar", lambda e: e.activation(STAT[0:1, 6:7], STAT[0:1, 5:6], AF.Ln),
             wait=["vector"])
        P.op("scalar", lambda e: e.activation(STAT[0:1, 7:8], STAT[0:1, 6:7], AF.Exp, scale=0.5),
             sig="vector")
        P.op("vector", lambda e: e.tensor_single_scalar(STAT[0:1, 8:9], STAT[0:1, 7:8], -1.0, ALU.mult),
             wait=["scalar"])
        P.op("vector", lambda e: e.tensor_copy(ETOK1[0:1, 0:1], STAT[0:1, 8:9]), sig="tensor")
        P.op("tensor", lambda e: e.matmul(PS[5][:, 0:1], ONESC[0:1, :], ETOK1[0:1, 0:1],
                                          start=True, stop=True),
             wait=["vector"], sig="scalar")
        P.op("scalar", lambda e: e.copy(BN[:], PS[5][:, 0:1]), wait=["tensor"],
             sig=["tensor", "vector"])

        for qb in range(NQB):
            for kt in range(NTT):
                pb = PS[kt % 3]
                for c in range(NCH):
                    P.op("tensor",
                         lambda e, pb=pb, c=c, kt=kt, qb=qb: e.matmul(
                             pb[:], KF[:, c * S + kt * 128:c * S + kt * 128 + 128],
                             XLN[:, c * S + qb * QB:c * S + qb * QB + QB],
                             start=(c == 0), stop=(c == NCH - 1)),
                         wait=([("scalar", 2)] if c == 0 and kt >= 3 else []) +
                              (["vector", "scalar"] if c == 0 and kt == 0 and qb == 0 else []) +
                              (["vector"] if c == 0 and kt == 0 and qb > 0 else []),
                         sig=("scalar" if c == NCH - 1 else None))
                if kt > 0:
                    P.op("tensor",
                         lambda e, kt=kt: e.matmul(
                             PS[4][0:1, :], ONESC[:, 0:1],
                             PBUF[:, (kt - 1) * 512:kt * 512],
                             start=(kt - 1 == 0), stop=False),
                         wait=["scalar"])
                P.op("scalar",
                     lambda e, pb=pb, kt=kt: e.activation(
                         PBUF[:, kt * 512:(kt + 1) * 512], pb[:], AF.Exp, bias=BN[:]),
                     wait=["tensor"], sig="tensor")
            P.op("tensor",
                 lambda e: e.matmul(PS[4][0:1, :], ONESC[:, 0:1],
                                    PBUF[:, (NTT - 1) * 512:NTT * 512],
                                    start=False, stop=True),
                 wait=["scalar"], sig=True)
            P.op("vector", lambda e: e.reciprocal(RR32F[:], PS[4][0:1, :]),
                 wait=[("tensor", 0), "#G"], sig="tensor")
            P.op("vector", lambda e: e.tensor_copy(RROWF[:], RR32F[:]), sig="tensor")
            P.op("tensor", lambda e: e.matmul(PS[5][:], ONES1[0:1, 0:128], RROWF[:],
                                              start=True, stop=True),
                 wait=["vector", ("scalar", 0)], sig="scalar")
            P.op("scalar", lambda e: e.copy(RBF[:], PS[5][:]), wait=["tensor"],
                 sig=["tensor", "vector"])
            for dc in range(NCH):
                pb = PS[3 if dc % 2 == 0 else 7]
                for kt in range(NTT):
                    P.op("tensor",
                         lambda e, pb=pb, dc=dc, kt=kt: e.matmul(
                             pb[:], VT[:, kt * 1040 + dc * 128:kt * 1040 + dc * 128 + 128],
                             PBUF[:, kt * 512:(kt + 1) * 512],
                             start=(kt == 0), stop=(kt == NTT - 1)),
                         wait=([("vector", 2)] if kt == 0 and dc >= 2 else []),
                         sig=("vector" if kt == NTT - 1 else None))
                o = dc * S + qb * QB
                P.op("vector",
                     lambda e, pb=pb: e.tensor_mul(TMPF[:], pb[:], RBF[:]),
                     wait=["tensor", "#G"] + (["scalar"] if dc == 0 else []),
                     sig=["tensor", "gpsimd"])
                P.op("gpsimd",
                     lambda e, o=o: e.tensor_add(X[:, o:o + 512], X[:, o:o + 512], TMPF[:]),
                     wait=["vector"],
                     sig=["vector"] + (["tensor"] if emit_output else []))
                if emit_output:
                    for j in range(4):
                        P.op("tensor",
                             lambda e, o=o, j=j: e.transpose(
                                 PS[6][:, 0:128], X[:, o + j * 128:o + j * 128 + 128], I128F[:]),
                             wait=["#G"] + (["scalar"] if (dc, j) > (0, 0) or qb > 0 else []),
                             sig="scalar")
                        P.op("scalar",
                             lambda e, j=j: e.copy(OSTG[:, (j % 2) * 128:(j % 2) * 128 + 128],
                                                   PS[6][:, 0:128]),
                             wait=["tensor", f"#o{j % 2}"], sig=["tensor", "sync"])
                        P.dma(y[qb * QB + j * 128:qb * QB + j * 128 + 128,
                                dc * 128:dc * 128 + 128],
                              OSTG[:, (j % 2) * 128:(j % 2) * 128 + 128], f"o{j % 2}",
                              wait=["scalar"])

    def emit_mha(tag):
        P.barrier()
        kcols = {}
        wa = wpush(dr["kw"][0, :, 0:512])
        wb = wpush(dr["kw"][0, :, 512:1024])
        kcols[0] = (wa, wb, P.count("#V"))
        for oc in range(NCH):
            if oc + 1 < NCH:
                wa = wpush(dr["kw"][oc + 1, :, 0:512])
                wb = wpush(dr["kw"][oc + 1, :, 512:1024])
                kcols[oc + 1] = (wa, wb, P.count("#V"))
            wa, wb, vwk = kcols[oc]
            for qb in range(NQB):
                pb = PS[(4 * oc + qb) % 4]
                for mc in range(NCH):
                    colof = (wa if mc < 4 else wb) + (mc % 4) * 128
                    P.op("tensor",
                         lambda e, pb=pb, mc=mc, qb=qb, colof=colof: e.matmul(
                             pb[:], WB16[:, colof:colof + 128],
                             XLN[:, mc * S + qb * QB:mc * S + qb * QB + QB],
                             start=(mc == 0), stop=(mc == NCH - 1)),
                         wait=([("abs", "#V", vwk)] if mc == 0 and qb == 0 else []) +
                              ([("scalar", 3)] if mc == 0 else []),
                         sig=("scalar" if mc == NCH - 1 else None))
                P.op("scalar",
                     lambda e, pb=pb, oc=oc, qb=qb: e.add(
                         KF[:, oc * S + qb * QB:oc * S + qb * QB + QB], pb[:],
                         KB8[:, oc:oc + 1]),
                     wait=["tensor"],
                     sig=["tensor"] + (["gpsimd"] if qb == NQB - 1 else []))
        P.op("gpsimd", lambda e: e.memset(
            VT[:].rearrange("p (t h) -> p t h", h=1040)[:, :, 64:1040:65], 1.0),
            wait=["tensor", "scalar"], sig="tensor")
        for half in range(2):
            for mc in range(NCH):
                wpush(dr["vw"][mc, :, half * 512:half * 512 + 512],
                      dst16=(VWA, mc * 512, 128),
                      extra_cast_wait=(["tensor"] if half == 1 else []))
            wpush(dr["vbrow"][0:1, half * 512:half * 512 + 512],
                  dst16=(VWA, 4096, 1))
            vwv = P.count("#V")
            for tt in range(NTT):
                pb = PS[tt % 4]
                for mc in range(NCH):
                    P.op("tensor",
                         lambda e, pb=pb, mc=mc, tt=tt: e.matmul(
                             pb[:], XLN[:, mc * S + tt * 128:mc * S + tt * 128 + 128],
                             VWA[:, mc * 512:(mc + 1) * 512],
                             start=(mc == 0), stop=False),
                         wait=([("abs", "#V", vwv)] if mc == 0 and tt == 0 else []) +
                              ([("scalar", 3)] if mc == 0 and tt > 0 else []))
                P.op("tensor",
                     lambda e, pb=pb: e.matmul(pb[:], ONES1[0:1, 0:128], VWA[0:1, 4096:4608],
                                               start=False, stop=True),
                     sig="scalar")
                P.op("scalar",
                     lambda e, pb=pb, tt=tt, half=half: e.copy(
                         VT[:].rearrange("p (t h) -> p t h", h=1040)
                         [:, tt, half * 520:half * 520 + 520]
                         .rearrange("p (g c) -> p g c", c=65)[:, :, 0:64],
                         pb[:]),
                     wait=["tensor"],
                     sig=["tensor"] + (["gpsimd"] if tt >= NTT - 2 else []))
        for qb in range(NQB):
            qev = []
            qcols = {}
            for j in range(2):
                wa = wpush(dr["qw"][j, :, 0:512])
                wb = wpush(dr["qw"][j, :, 512:1024])
                qcols[j] = (wa, wb, P.count("#V"))
            for oc in range(NCH):
                if oc + 2 < NCH:
                    wa = wpush(dr["qw"][oc + 2, :, 0:512])
                    wb = wpush(dr["qw"][oc + 2, :, 512:1024])
                    qcols[oc + 2] = (wa, wb, P.count("#V"))
                wa, wb, vwq = qcols[oc]
                pb = PS[oc % 4]
                for mc in range(NCH):
                    colof = (wa if mc < 4 else wb) + (mc % 4) * 128
                    P.op("tensor",
                         lambda e, pb=pb, mc=mc, qb=qb, colof=colof: e.matmul(
                             pb[:], WB16[:, colof:colof + 128],
                             XLN[:, mc * S + qb * QB:mc * S + qb * QB + QB],
                             start=(mc == 0), stop=(mc == NCH - 1)),
                         wait=([("abs", "#V", vwq)] if mc == 0 else []) +
                              ([("abs", "#V", qev[oc - 4])] if mc == 0 and oc >= 4 else []) +
                              (["vector"] if mc == 0 and oc == 0 and qb > 0 else []),
                         sig=("vector" if mc == NCH - 1 else None))
                P.op("vector",
                     lambda e, pb=pb, oc=oc: e.tensor_scalar(
                         QSTG[:, oc * 512:(oc + 1) * 512], pb[:],
                         0.125, QB8S[:, oc:oc + 1], ALU.mult, ALU.add),
                     wait=["tensor"], sig=True)
                qev.append(P.count("#V"))
            for oc in range(NCH):
                P.op("vector",
                     lambda e, oc=oc, qb=qb: e.tensor_copy(
                         XLN[:, oc * S + qb * QB:oc * S + qb * QB + QB],
                         QSTG[:, oc * 512:(oc + 1) * 512]),
                     sig=(["tensor"] if oc == NCH - 1 else None))
        for wi, TS in ((0, XLN), (1, KF)):
            for qb in range(NQB):
                for c in range(NCH):
                    P.op("vector",
                         lambda e, TS=TS, c=c, qb=qb: e.tensor_mul(
                             PBUF[:, (c % 4) * 512:(c % 4) * 512 + 512],
                             TS[:, c * S + qb * QB:c * S + qb * QB + QB],
                             TS[:, c * S + qb * QB:c * S + qb * QB + QB]),
                         wait=([("tensor", 3)] if (c >= 4 or qb > 0 or wi > 0) else []) +
                              (["scalar"] if wi == 0 and qb == 0 and c == 0 else []),
                         sig="tensor")
                    P.op("tensor",
                         lambda e, c=c: e.matmul(
                             PS[4][0:16, :], BO16[:, c * 16:c * 16 + 16],
                             PBUF[:, (c % 4) * 512:(c % 4) * 512 + 512],
                             start=(c == 0), stop=(c == NCH - 1)),
                         wait=["vector"], sig=True)
                P.op("vector",
                     lambda e: e.reduce_max(STAT[0:16, 1:2], PS[4][0:16, :], mybir.AxisListType.X),
                     wait=["tensor"], sig="tensor")
                if qb == 0:
                    P.op("vector", lambda e, wi=wi: e.tensor_copy(
                        STAT[0:16, 4 + wi:5 + wi], STAT[0:16, 1:2]))
                else:
                    P.op("vector", lambda e, wi=wi: e.tensor_max(
                        STAT[0:16, 4 + wi:5 + wi], STAT[0:16, 4 + wi:5 + wi], STAT[0:16, 1:2]))
        P.op("vector", lambda e: e.tensor_mul(STAT[0:16, 6:7], STAT[0:16, 4:5], STAT[0:16, 5:6]),
             sig="scalar")
        P.op("scalar", lambda e: e.activation(STAT[0:16, 7:8], STAT[0:16, 6:7], AF.Ln),
             wait=["vector"])
        P.op("scalar", lambda e: e.activation(STAT[0:16, 8:9], STAT[0:16, 7:8], AF.Exp, scale=0.5),
             sig="vector")
        P.op("vector", lambda e: e.tensor_single_scalar(STAT[0:16, 9:10], STAT[0:16, 8:9],
                                                        -1.0, ALU.mult),
             wait=["scalar"])
        P.op("vector", lambda e: e.tensor_copy(BH16[:], STAT[0:16, 9:10]), sig="tensor")
        for h in range(NH):
            P.op("tensor",
                 lambda e, h=h: e.matmul(PS[5][:, h:h + 1],
                                         I128B[0:16, h:h + 1].broadcast_to([16, 128]),
                                         BH16[:], start=True, stop=True),
                 wait=(["vector"] if h == 0 else []),
                 sig=("scalar" if h == NH - 1 else None))
        P.op("scalar", lambda e: e.copy(BH[:], PS[5][:, 0:16]), wait=["tensor"],
             sig=["tensor", "vector"])

        P.wait("tensor", ["vector", "scalar", "#G"])
        for qb in range(NQB):
            nkt = 4 * qb + 4
            tile_ctr = [0]
            bank_rec = {0: 0, 1: 0, 2: 0, 3: 0}
            pair_tog = [0]

            def stile(h, kts, c0s):
                hc, hp = h // 2, (h % 2) * 64
                if len(kts) == 2:
                    b0 = pair_tog[0] * 2
                    pair_tog[0] ^= 1
                    banks = [b0, b0 + 1]
                    if tile_ctr[0] % 2:
                        tile_ctr[0] += 1
                else:
                    banks = [tile_ctr[0] % 4]
                slots = []
                for kt, c0, b in zip(kts, c0s, [banks[0], banks[-1]]):
                    sl = tile_ctr[0] % 4
                    tile_ctr[0] += 1
                    slots.append(sl)
                    w = []
                    if bank_rec[b] > 0:
                        w.append(("abs", "#S", bank_rec[b]))
                    P.op("tensor",
                         lambda e, b=b, hc=hc, hp=hp, kt=kt, qb=qb, c0=c0: e.matmul(
                             PS[b][:, c0:512],
                             KF[hp:hp + 64, hc * S + kt * 128:hc * S + kt * 128 + 128],
                             XLN[hp:hp + 64, hc * S + qb * QB + c0:hc * S + qb * QB + QB],
                             start=True, stop=True),
                         wait=w, sig=(None if (len(kts) == 2 and kt == kts[0]) else "scalar"))
                if len(kts) == 2:
                    b0, sl0 = banks[0], slots[0]
                    assert sl0 % 2 == 0
                    P.op("scalar",
                         lambda e, b0=b0, sl0=sl0, h=h: e.activation(
                             PRING[:, sl0 * 512:sl0 * 512 + 1024],
                             PSALL[:, b0 * 512:b0 * 512 + 1024],
                             AF.Exp, bias=BH[:, h:h + 1]),
                         wait=["tensor"], sig=["tensor"])
                    bank_rec[banks[0]] = bank_rec[banks[1]] = P.count("#S")
                else:
                    b, sl, c0 = banks[0], slots[0], c0s[0]
                    P.op("scalar",
                         lambda e, b=b, sl=sl, c0=c0, h=h: e.activation(
                             PRING[:, sl * 512 + c0:sl * 512 + 512], PS[b][:, c0:512],
                             AF.Exp, bias=BH[:, h:h + 1]),
                         wait=["tensor"], sig=["tensor"])
                    bank_rec[b] = P.count("#S")
                return slots

            def tri_mask(slots, c0s):
                sl, c0 = slots[0], c0s[0]
                P.op("vector",
                     lambda e, sl=sl, c0=c0: e.tensor_mul(
                         PRING[:, sl * 512 + c0:sl * 512 + c0 + 128],
                         PRING[:, sl * 512 + c0:sl * 512 + c0 + 128], TRI[:]),
                     wait=["scalar"], sig="tensor")

            def pvgrp(h, kts, c0s, slots, first, last):
                pvb = PS[6] if h % 2 == 0 else PS[7]
                diag = c0s[0] > 0 or kts[0] >= 4 * qb
                for kt, c0, sl in zip(kts, c0s, slots):
                    P.op("tensor",
                         lambda e, pvb=pvb, h=h, kt=kt, c0=c0, sl=sl,
                                st=(first and kt == kts[0]),
                                sp=(last and kt == kts[-1]): e.matmul(
                             pvb[0:65, c0:512],
                             VT[:, kt * 1040 + h * 65:kt * 1040 + h * 65 + 65],
                             PRING[:, sl * 512 + c0:sl * 512 + 512],
                             start=st, stop=sp),
                         wait=((["vector"] if diag else ["scalar"]) if kt == kts[0] else []) +
                              ([("abs", "#V", mult_rec[6 if h % 2 == 0 else 7])]
                               if first and kt == kts[0] else []),
                         sig=("scalar" if last and kt == kts[-1] else None))

            pend = []
            mult_rec = {6: 0, 7: 0}

            def flush_norm():
                for h, pvb, rt in pend:
                    hc, hp = h // 2, (h % 2) * 64
                    P.op("tensor",
                         lambda e, rt=rt: e.matmul(
                             PS[5][0:64, :], ONES1[0:1, 0:64], rt[:],
                             start=True, stop=True),
                         wait=["vector"], sig="scalar")
                    P.op("scalar",
                         lambda e: e.copy(RBM[:], PS[5][0:64, :]),
                         wait=["tensor", "vector"], sig=["tensor", "vector"])
                    P.op("vector",
                         lambda e, pvb=pvb, hc=hc, hp=hp: e.tensor_mul(
                             ATQ[hp:hp + 64, hc * 512:(hc + 1) * 512], pvb[0:64, :], RBM[:]),
                         wait=["scalar"], sig=["tensor"])
                    mult_rec[6 if hp == 0 else 7] = P.count("#V")
                del pend[:]

            for hp2 in range(NH // 2):
                h0, h1 = 2 * hp2, 2 * hp2 + 1
                chunks = []
                for k2 in range(0, 4 * qb, 2):
                    chunks.append(([k2, k2 + 1], [0, 0]))
                for j in range(4):
                    chunks.append(([4 * qb + j], [128 * j]))
                nch_ = len(chunks)
                for ci, (kts, c0s) in enumerate(chunks):
                    sl_h0 = stile(h0, kts, c0s)
                    if len(kts) == 1:
                        tri_mask(sl_h0, c0s)
                    sl_h1 = stile(h1, kts, c0s)
                    if len(kts) == 1:
                        tri_mask(sl_h1, c0s)
                    if ci == 0:
                        flush_norm()
                    pvgrp(h0, kts, c0s, sl_h0, ci == 0, ci == nch_ - 1)
                    pvgrp(h1, kts, c0s, sl_h1, ci == 0, ci == nch_ - 1)
                for h in (h0, h1):
                    pvb = PS[6] if h % 2 == 0 else PS[7]
                    rt = RROWM if h % 2 == 0 else RROWN
                    P.op("vector", lambda e, pvb=pvb: e.reciprocal(RR32F[:], pvb[64:65, :]),
                         wait=["tensor"], sig="tensor")
                    P.op("vector", lambda e, rt=rt: e.tensor_copy(rt[:], RR32F[:]), sig="tensor")
                    pend.append((h, pvb, rt))
            flush_norm()
            oev = []
            ocols = {}
            for j in range(2):
                wa = wpush(dr["ow"][j, :, 0:512])
                wb = wpush(dr["ow"][j, :, 512:1024])
                ocols[j] = (wa, wb, P.count("#V"))
            for mc in range(NCH):
                if mc + 2 < NCH:
                    wa = wpush(dr["ow"][mc + 2, :, 0:512])
                    wb = wpush(dr["ow"][mc + 2, :, 512:1024])
                    ocols[mc + 2] = (wa, wb, P.count("#V"))
                wa, wb, vwo = ocols[mc]
                pb = PS[mc % 4]
                for hcc in range(NCH):
                    colof = (wa if hcc < 4 else wb) + (hcc % 4) * 128
                    P.op("tensor",
                         lambda e, pb=pb, hcc=hcc, colof=colof: e.matmul(
                             pb[:], WB16[:, colof:colof + 128], ATQ[:, hcc * 512:(hcc + 1) * 512],
                             start=(hcc == 0), stop=(hcc == NCH - 1)),
                         wait=([("abs", "#V", vwo)] if hcc == 0 else []) +
                              ([("abs", "#V", oev[mc - 4])] if hcc == 0 and mc >= 4 else []),
                         sig=("vector" if hcc == NCH - 1 else None))
                o = mc * S + qb * QB
                P.op("vector",
                     lambda e, pb=pb, o=o, mc=mc: e.scalar_tensor_tensor(
                         X[:, o:o + 512], pb[:], OB8[:, mc:mc + 1], X[:, o:o + 512],
                         ALU.add, ALU.add),
                     wait=["tensor"], sig=["tensor"])
                oev.append(P.count("#V"))
            P.wait("tensor", ["vector"])

    emit_ln("ln1")
    if dbg_dump("ln1"):
        return nc, P, dbg_outs
    emit_routing("r1")
    if dbg_dump("route1"):
        return nc, P, dbg_outs
    emit_tree("k", "KF", "t1k")
    if dbg_dump("ktree1"):
        return nc, P, dbg_outs
    emit_tree("v", "VT", "t1v", scale=0.5)
    if dbg_dump("vtree1"):
        return nc, P, dbg_outs
    emit_tree("q", "XLN", "t1q", scale=1.0 / 32.0)
    if dbg_dump("qtree1"):
        return nc, P, dbg_outs
    emit_fff_attn("a1")
    if dbg_dump("attn1"):
        return nc, P, dbg_outs
    emit_ln("ln2")
    if dbg_dump("ln2"):
        return nc, P, dbg_outs
    emit_mha("mha")
    if dbg_dump("mha"):
        return nc, P, dbg_outs
    emit_ln("ln3")
    if dbg_dump("ln3"):
        return nc, P, dbg_outs
    emit_routing("r2")
    if dbg_dump("route2"):
        return nc, P, dbg_outs
    emit_tree("k", "KF", "t2k")
    emit_tree("v", "VT", "t2v", scale=0.5)
    emit_tree("q", "XLN", "t2q", scale=1.0 / 32.0)
    if dbg_dump("trees2"):
        return nc, P, dbg_outs
    emit_fff_attn("a2", emit_output=True)
    for cn in ("o0", "o1"):
        s = P._chan(cn)
        P.q["sync"].append(lambda e, h=s[0], v=s[1]: e.wait_ge(h, v))
    dbg_dump("out")
    return nc, P, dbg_outs


def finalize(nc, P):
    with nc.Block() as block:
        @block.sync
        def _(e):
            for fn in P.q["sync"]:
                fn(e)

        @block.tensor
        def _(e):
            for fn in P.q["tensor"]:
                fn(e)

        @block.scalar
        def _(e):
            for fn in P.q["scalar"]:
                fn(e)

        @block.vector
        def _(e):
            for fn in P.q["vector"]:
                fn(e)

        @block.gpsimd
        def _(e):
            for fn in P.q["gpsimd"]:
                fn(e)
    return nc


_BUILD_CACHE = {}


def _get_program(upto=None, dbg=()):
    key = (upto, tuple(dbg))
    if key not in _BUILD_CACHE:
        nc, P, dbg_outs = build(upto=upto, dbg=dbg)
        finalize(nc, P)
        _BUILD_CACHE[key] = (nc, dbg_outs)
    return _BUILD_CACHE[key]


def run(inputs, upto=None, dbg=()):
    per_core = host_prepare(inputs)
    nc, dbg_outs = _get_program(upto=upto, dbg=dbg)
    res = run_bass_kernel_spmd(nc, per_core, core_ids=list(range(8)))
    return res, dbg_outs


def kernel(**inputs):
    res, _ = run(inputs)
    return np.stack([np.asarray(r["y"], np.float32) for r in res.results], axis=0)


# version 13
# speedup vs baseline: 1.2571x; 1.1192x over previous
import numpy as np
import concourse.bass as bass
import concourse.mybir as mybir
from concourse.bass_utils import run_bass_kernel_spmd

F32 = mybir.dt.float32
BF16 = mybir.dt.bfloat16
AF = mybir.ActivationFunctionType
ALU = mybir.AluOpType

S = 2048
D = 1024
NCH = 8
NQB = 4
QB = 512
NTT = 16
NH = 16
EPS = 1e-5

ENGINES = ("sync", "tensor", "scalar", "vector", "gpsimd")

_AUDIT = False
_AUDIT_ENGINES = ()


class Prog:

    def __init__(self, nc):
        self.nc = nc
        self.q = {e: [] for e in ENGINES}
        self.chans = {}

    def _chan(self, name):
        if name not in self.chans:
            self.chans[name] = [self.nc.alloc_semaphore(f"c_{name}"), 0]
        return self.chans[name]

    ENGCH = {"tensor": "T", "scalar": "S", "vector": "V", "gpsimd": "G"}

    def _mkwaits(self, wait):
        wl = []
        for w in wait:
            if isinstance(w, tuple) and len(w) == 3:
                _, src, v = w
                s = self._chan(src[1:] if src.startswith("#") else self.ENGCH[src])
                if v > 0:
                    wl.append((s[0], v))
                continue
            if isinstance(w, tuple):
                src, lag = w
            else:
                src, lag = w, 0
            name = src[1:] if src.startswith("#") else self.ENGCH[src]
            s = self._chan(name)
            v = s[1] - lag
            if v > 0:
                wl.append((s[0], v))
        return wl

    def op(self, eng, fn, wait=(), sig=None):
        wl = self._mkwaits(wait)
        if eng in ("gpsimd", "vector", "scalar") or (_AUDIT and eng != "sync") or eng in _AUDIT_ENGINES:
            s = self._chan(self.ENGCH[eng])
            if s[1] > 0:
                wl.append((s[0], s[1]))
            s[1] += 1
            sigs = [(s[0], 1)]
        elif sig is not None:
            s = self._chan(self.ENGCH[eng])
            s[1] += 1
            sigs = [(s[0], 1)]
        else:
            sigs = []

        def run(e, fn=fn, wl=wl, sigs=sigs):
            for h, v in wl:
                e.wait_ge(h, v)
            ins = fn(e)
            if sigs:
                ins.then_inc(sigs[0][0], sigs[0][1])

        self.q[eng].append(run)

    def count(self, name):
        return self._chan(name[1:] if name.startswith("#") else self.ENGCH[name])[1]

    def wait(self, eng, srcs):
        wl = self._mkwaits([srcs] if isinstance(srcs, (str, tuple)) else srcs)
        if wl:
            def run(e, wl=wl):
                for h, v in wl:
                    e.wait_ge(h, v)
            self.q[eng].append(run)

    def dma(self, out_ap, in_ap, chan, wait=()):
        s = self._chan(chan)
        s[1] += 16
        wl = self._mkwaits(wait)

        def run(e, wl=wl, h=s[0]):
            for hh, v in wl:
                e.wait_ge(hh, v)
            e.dma_start(out=out_ap, in_=in_ap).then_inc(h, 16)

        self.q["sync"].append(run)

    def barrier(self, engines=("tensor", "scalar", "vector", "gpsimd")):
        fence = {
            "tensor": lambda e: e.ldweights(self.fence_ap),
            "scalar": lambda e: e.copy(self.fence_sc[0:1, 1:2], self.fence_src),
            "vector": lambda e: e.memset(self.fence_sc[0:1, 2:3], 0.0),
            "gpsimd": lambda e: e.memset(self.fence_sc[0:1, 3:4], 0.0),
        }
        for a in engines:
            self.op(a, fence[a], wait=(["#G"] if a in ("tensor", "scalar") else []), sig=True)
        for a in engines:
            self.wait(a, [b for b in engines if b != a])


def ceil32(x):
    return (x + 31) & ~31


def host_prepare(inputs):
    g = {}
    for k, v in inputs.items():
        a = np.asarray(v)
        if a.dtype != np.float32:
            a = a.astype(np.float32)
        g[k] = a
    x = g["L0_attn_x"]
    out = {}
    for t in ("q", "k", "v"):
        w1 = g[f"L0_ffn_{t}_w1"]
        w1f = np.ascontiguousarray(w1.transpose(1, 0, 2).reshape(D, D))
        A = w1f.reshape(NCH, 128, NCH, 128)
        out[f"{t}w1"] = np.ascontiguousarray(A.transpose(2, 1, 0, 3).reshape(NCH, 128, D))
        w2f = g[f"L0_ffn_{t}_w2"].reshape(D, D)
        if t == "v":
            out["vw2"] = np.ascontiguousarray(w2f.reshape(NCH, 128, D))
        else:
            C = w2f.reshape(NCH, 128, NCH, 128)
            out[f"{t}w2"] = np.ascontiguousarray(C.transpose(2, 1, 0, 3).reshape(NCH, 128, D))
        out[f"{t}b1"] = np.ascontiguousarray(g[f"L0_ffn_{t}_b1"].reshape(NCH, 128).T)
        out[f"{t}b2"] = np.ascontiguousarray(g[f"L0_ffn_{t}_b2"])
    npk = np.zeros((128, 408), np.float32)
    for ti, t in enumerate(("q", "k", "v")):
        nwT = g[f"L0_ffn_{t}_node_w"].T.reshape(NCH, 128, 15)
        for c in range(NCH):
            npk[:, c * 45 + ti * 15:c * 45 + ti * 15 + 15] = nwT[c]
        npk[0, 360 + ti * 15:360 + ti * 15 + 15] = g[f"L0_ffn_{t}_node_b"]
    out["npk"] = npk
    qwf = np.ascontiguousarray(g["L0_attn_Q_w"].transpose(1, 0, 2).reshape(D, D))
    kwf = np.ascontiguousarray(g["L0_attn_K_w"].transpose(1, 0, 2).reshape(D, D))
    vwf = np.ascontiguousarray(g["L0_attn_V_w"].transpose(1, 0, 2).reshape(D, D))
    owf = np.ascontiguousarray(g["L0_attn_O_w"].reshape(D, D))
    for nm, wf in (("qw", qwf), ("kw", kwf), ("ow", owf)):
        Cw = wf.reshape(NCH, 128, NCH, 128)
        out[nm] = np.ascontiguousarray(Cw.transpose(2, 1, 0, 3).reshape(NCH, 128, D))
    out["vw"] = np.ascontiguousarray(vwf.reshape(NCH, 128, D))
    out["qb8"] = np.ascontiguousarray(g["L0_attn_Q_b"].reshape(D).reshape(NCH, 128).T)
    out["kb8"] = np.ascontiguousarray(g["L0_attn_K_b"].reshape(D).reshape(NCH, 128).T)
    out["ob8"] = np.ascontiguousarray(g["L0_attn_O_b"].reshape(NCH, 128).T)
    out["vbrow"] = np.ascontiguousarray(g["L0_attn_V_b"].reshape(1, D))
    per_core = []
    for b in range(8):
        m = dict(out)
        m["x"] = np.ascontiguousarray(x[b])
        per_core.append(m)
    return per_core


def build(upto=None, dbg=()):
    nc = bass.Bass("TRN2", target_bir_lowering=False, dynamic_dma_scratch_size=64,
                   detect_race_conditions=_AUDIT)
    P = Prog(nc)

    dr = {}
    dr["x"] = nc.dram_tensor("x", [S, D], F32, kind="ExternalInput").ap()
    for t in ("q", "k", "v"):
        for suf, shape in (("w1", [NCH, 128, D]), ("w2", [NCH, 128, D]),
                           ("b1", [128, 8]), ("b2", [16, D])):
            dr[f"{t}{suf}"] = nc.dram_tensor(f"{t}{suf}", shape, F32, kind="ExternalInput").ap()
    dr["npk"] = nc.dram_tensor("npk", [128, 408], F32, kind="ExternalInput").ap()
    for nm in ("qw", "kw", "vw", "ow"):
        dr[nm] = nc.dram_tensor(nm, [NCH, 128, D], F32, kind="ExternalInput").ap()
    for nm in ("qb8", "kb8", "ob8"):
        dr[nm] = nc.dram_tensor(nm, [128, 8], F32, kind="ExternalInput").ap()
    dr["vbrow"] = nc.dram_tensor("vbrow", [1, D], F32, kind="ExternalInput").ap()
    y = nc.dram_tensor("y", [S, D], F32, kind="ExternalOutput").ap()

    SBYTES = (nc.SBUF_PARTITION_SIZE_BYTES or 196608) - 32
    off = 512

    def at(name, shape, dtype, o=None):
        nonlocal off
        fixed = o is not None
        if o is None:
            o = off
        tb = {F32: 4, BF16: 2}[dtype]
        nbytes = int(np.prod(shape[1:])) * tb
        h = nc.alloc_sbuf_tensor_at(name, shape, dtype, offset=o)
        if not fixed:
            off = ceil32(o + nbytes)
        return h

    X = at("X", [128, NCH * S], F32)
    XLN = at("XLN", [128, NCH * S], BF16)
    KF = at("KF", [128, NCH * S], BF16)
    VT = at("VT", [128, NTT * 1040], BF16)
    SCRO = off
    PBUF = at("PBUF", [128, 8192], BF16, SCRO)
    SB = at("SB", [128, 2048], F32, SCRO)
    SFM = at("SFM", [128, 2048], F32, SCRO)
    ATQ = at("ATQ", [128, 4096], BF16, SCRO)
    VWA = at("VWA", [128, 4608], BF16, SCRO)
    XSTG0 = at("XSTG0", [128, 1024], F32, SCRO)
    XSTG1 = at("XSTG1", [128, 1024], F32, SCRO + 4096)
    MSB = at("MSB", [128, 2048], F32, SCRO + 8192)
    SALL = at("SALL", [128, 720], F32, SCRO + 8192)
    PRING = at("PRING", [128, 4096], BF16, SCRO + 8192)
    MUXT = at("MUXT", [128, 192], F32, SCRO + 12288)
    QSTG = at("QSTG", [128, 4096], BF16, SCRO + 12288)
    RROWM = at("RROWM", [1, 512], BF16, SCRO + 18432)
    RROWN = at("RROWN", [1, 512], BF16, SCRO + 19456)
    LEAFB = at("LEAFB", [128, 1024], BF16, SCRO + 16384)
    B2BF = at("B2BF", [16, 1024], BF16, SCRO + 16384)
    SCRN = at("SCRN", [128, 1024], F32, SCRO + 16384)
    RBF = at("RBF", [128, 512], F32, SCRO + 16384)
    E16H = at("E16H", [16, 1024], BF16, SCRO + 18432)
    MSK = at("MSK", [128, 1024], BF16, SCRO + 18432)
    LEAFT = at("LEAFT", [8, 128], BF16, SCRO + 18432)
    TMPF = at("TMPF", [128, 512], F32, SCRO + 18432)
    RROWF = at("RROWF", [1, 512], BF16, SCRO + 18432)
    off = ceil32(SCRO + 20480)
    WS32 = at("WS32", [128, 1024], F32)
    OSTG = at("OSTG", [128, 256], F32, off - 4096)
    WB16 = at("WB16", [128, 3072], BF16)
    ONES1 = at("ONES1", [1, 512], BF16)
    ONESC = at("ONESC", [128, 128], BF16)
    I128B = at("I128B", [128, 128], BF16)
    I128F = at("I128F", [128, 128], F32)
    TRI = at("TRI", [128, 128], BF16)
    IOTA16 = at("IOTA16", [128, 16], F32)
    C2 = at("C2", [128, 8], F32)
    BO16 = at("BO16", [128, 128], BF16)
    B1T = {t: at(f"B1{t}", [128, 8], F32) for t in ("q", "k", "v")}
    QB8 = at("QB8", [128, 8], F32)
    QB8S = at("QB8S", [128, 8], F32)
    KB8 = at("KB8", [128, 8], F32)
    OB8 = at("OB8", [128, 8], F32)
    LTF = at("LTF", [128, 16], F32)
    LT16 = {t: at(f"LT16{t}", [128, 16], BF16) for t in ("q", "k", "v")}
    LTFS = {t: at(f"LTFS{t}", [128, 16], F32) for t in ("q", "k", "v")}
    STAT = at("STAT", [128, 16], F32)
    STAIR = at("STAIR", [128, 32], BF16)
    PIDX = at("PIDX", [128, 1], F32)
    BN = at("BN", [128, 1], F32)
    BH = at("BH", [128, 16], F32)
    BH16 = at("BH16", [16, 1], BF16)
    ETOK = at("ETOK", [128, 16], BF16)
    ETOK1 = at("ETOK1", [1, 16], BF16)
    GT1 = at("GT1", [128, 128], F32)
    RR32F = at("RR32F", [1, 512], F32, SCRO + 16384)
    EPSC = at("EPSC", [128, 1], F32)
    FEN = at("FEN", [128, 8], F32)
    P.fence_ap = ONESC[0:1, 0:1]
    P.fence_sc = FEN
    P.fence_src = EPSC[0:1, 0:1]
    assert off <= SBYTES, f"SBUF overflow: {off} > {SBYTES}"
    assert SCRO + 20480 <= SBYTES

    PSALL = nc.alloc_psum_tensor("psall", [128, 4096], F32)
    PS = [PSALL[:, i * 512:(i + 1) * 512] for i in range(8)]

    dbg_outs = {}
    named = {"X": X, "XLN": XLN, "KF": KF, "VT": VT, "PBUF": PBUF, "SALL": SALL,
             "SB": SB, "MSB": MSB, "SFM": SFM, "LEAFB": LEAFB, "E16H": E16H,
             "ATQ": ATQ, "BH": BH, "QSTG": QSTG, "STAT": STAT, "BN": BN,
             "LTFq": LTFS["q"], "LTFk": LTFS["k"], "LTFv": LTFS["v"]}

    def dbg_dump(phase):
        if upto == phase or any(p == phase for p, _ in dbg):
            P.barrier()
            P.op("scalar", lambda e: e.copy(P.fence_sc[0:1, 4:5], P.fence_src), wait=["#G"], sig=True)
            for p, name in dbg:
                if p != phase:
                    continue
                t = named[name]
                key = f"dbg_{phase}_{name}"
                shp = list(t.shape)
                d = nc.dram_tensor(key, shp, t.dtype, kind="ExternalOutput").ap()
                P.wait("sync", ["scalar"])
                P.dma(d, t[:], "dbg")
                dbg_outs[key] = tuple(shp)
            P.wait("scalar", ["#dbg"])
        return upto == phase

    cops = [
        lambda e: e.memset(FEN[:], 0.0),
        lambda e: e.memset(VT[:], 0.0),
        lambda e: e.memset(ONES1[:], 1.0),
        lambda e: e.memset(ONESC[:], 1.0),
        lambda e: e.memset(EPSC[:], EPS),
        lambda e: e.iota(I128B[:], [[1, 128]], channel_multiplier=-1,
                         allow_small_or_imprecise_dtypes=True),
        lambda e: e.tensor_single_scalar(I128B[:], I128B[:], 0.0, ALU.is_equal),
        lambda e: e.iota(I128F[:], [[1, 128]], channel_multiplier=-1,
                         allow_small_or_imprecise_dtypes=True),
        lambda e: e.tensor_single_scalar(I128F[:], I128F[:], 0.0, ALU.is_equal),
        lambda e: e.iota(TRI[:], [[1, 128]], channel_multiplier=-1,
                         allow_small_or_imprecise_dtypes=True),
        lambda e: e.tensor_single_scalar(TRI[:], TRI[:], 0.0, ALU.is_ge),
        lambda e: e.iota(IOTA16[:], [[1, 16]], channel_multiplier=0,
                         allow_small_or_imprecise_dtypes=True),
        lambda e: e.iota(GT1[:, 1:2], [[0, 1]], channel_multiplier=1,
                         allow_small_or_imprecise_dtypes=True),
        lambda e: e.tensor_single_scalar(GT1[:, 1:2], GT1[:, 1:2], 64.0, ALU.is_ge),
        lambda e: e.iota(C2[:], [[2, 8]], channel_multiplier=0,
                         allow_small_or_imprecise_dtypes=True),
        lambda e: e.tensor_tensor(C2[:], C2[:], GT1[:, 1:2].broadcast_to([128, 8]), ALU.add),
        lambda e: e.iota(GT1[:, 0:128], [[-2, 8], [1, 16]], channel_multiplier=0,
                         allow_small_or_imprecise_dtypes=True),
        lambda e: e.tensor_tensor(GT1[:, 0:128], GT1[:, 0:128],
                                  GT1[:, 1:2].broadcast_to([128, 128]), ALU.subtract),
        lambda e: e.tensor_single_scalar(BO16[:], GT1[:, 0:128], 0.0, ALU.is_equal),
        lambda e: e.iota(STAIR[:], [[1, 32]], channel_multiplier=0,
                         allow_small_or_imprecise_dtypes=True),
        lambda e: e.iota(PIDX[:], [[0, 1]], channel_multiplier=1,
                         allow_small_or_imprecise_dtypes=True),
        lambda e: e.tensor_single_scalar(STAIR[:], STAIR[:], 15.0, ALU.is_equal),
    ]
    for i, cop in enumerate(cops):
        P.op("gpsimd", cop,
             sig=(["tensor", "scalar", "vector"] if i == len(cops) - 1 else None))
    P.dma(QB8[:], dr["qb8"], "mc")
    P.dma(KB8[:], dr["kb8"], "mc")
    P.dma(OB8[:], dr["ob8"], "mc")
    for t in ("q", "k", "v"):
        P.dma(B1T[t][:], dr[f"{t}b1"], "mc")
    P.op("vector", lambda e: e.tensor_single_scalar(QB8S[:], QB8[:], 0.125, ALU.mult),
         wait=["#mc", "#G"], sig="scalar")

    xst = [XSTG0, XSTG1]
    for tt in range(NTT):
        sl = xst[tt % 2]
        P.dma(sl[:], dr["x"][tt * 128:(tt + 1) * 128, :], f"x{tt % 2}",
              wait=([("scalar", 1)] if tt >= 2 else []))
        for c in range(NCH):
            pb = PS[c % 2]
            P.op("tensor",
                 lambda e, pb=pb, sl=sl, c=c: e.transpose(
                     pb[:, 0:128], sl[:, c * 128:(c + 1) * 128], I128F[:]),
                 wait=(["#G"] if tt == 0 and c == 0 else []) +
                      ([f"#x{tt % 2}"] if c == 0 else []) +
                      ([("scalar", 1)] if (tt, c) > (0, 1) else []),
                 sig="scalar")
            P.op("scalar",
                 lambda e, pb=pb, c=c, tt=tt: e.copy(
                     X[:, c * S + tt * 128:c * S + tt * 128 + 128], pb[:, 0:128]),
                 wait=["tensor"], sig=["tensor"] + (["sync"] if c == 7 else []))
    if dbg_dump("xin"):
        return nc, P, dbg_outs

    wstate = {"n": 0, "gafter": [0, 0]}

    def wpush(dram_ap, ncols=512, dst16=None, extra_cast_wait=(), colw=None):
        u = wstate["n"]
        wstate["n"] += 1
        i32 = u % 2
        pr = dram_ap.shape[0]
        P.dma(WS32[0:pr, i32 * 512:i32 * 512 + ncols], dram_ap, f"w{i32}",
              wait=[("abs", "#V", wstate["gafter"][i32])])
        if dst16 is None:
            cw = colw or 512
            i16 = u % (3072 // cw)
            dst = WB16[0:pr, i16 * cw:i16 * cw + ncols]
            col = i16 * cw
            cw = ["tensor"]
        else:
            tns, col, prx = dst16
            dst = tns[0:pr, col:col + ncols]
        P.op("vector",
             lambda e, dst=dst, i32=i32, pr=pr, ncols=ncols: e.tensor_copy(
                 dst, WS32[0:pr, i32 * 512:i32 * 512 + ncols]),
             wait=[f"#w{i32}", "tensor"] + list(extra_cast_wait), sig=True)
        wstate["gafter"][i32] = P.count("#V")
        return col

    def emit_ln(tag):
        part = globals().get("_LN_PART")
        P.barrier()
        cast_rec = {}
        for c in range(NCH):
            eng = "gpsimd" if c < 4 else "vector"
            P.op(eng, lambda e, c=c: e.tensor_copy(XLN[:, c * S:(c + 1) * S], X[:, c * S:(c + 1) * S]),
                 sig=True)
            cast_rec[c] = ("#G" if c < 4 else "#V", P.count("#G" if c < 4 else "#V"))
        for qb in range(NQB):
            pm = PS[qb % 2]
            for c in range(NCH):
                P.op("tensor",
                     lambda e, pm=pm, c=c, qb=qb: e.matmul(
                         pm[:], ONESC[:], XLN[:, c * S + qb * QB:c * S + qb * QB + QB],
                         start=(c == 0), stop=(c == NCH - 1)),
                     wait=([("abs",) + cast_rec[c]] if qb == 0 else []) +
                          ([("vector", 1)] if qb >= 2 and c == 0 else []),
                     sig=("vector" if c == NCH - 1 else None))
            P.op("vector",
                 lambda e, pm=pm, qb=qb: e.tensor_single_scalar(
                     MSB[:, qb * QB:(qb + 1) * QB], pm[:], 1.0 / D, ALU.mult),
                 wait=["tensor"], sig="tensor")
        if part == "sums":
            return
        sq_rec = {}
        for c in range(NCH):
            if c < 4:
                P.op("scalar",
                     lambda e, c=c: e.activation(XLN[:, c * S:(c + 1) * S],
                                                 X[:, c * S:(c + 1) * S], AF.Square),
                     wait=(["tensor"] if c == 0 else []), sig=True)
                sq_rec[c] = ("#S", P.count("#S"))
            else:
                P.op("vector",
                     lambda e, c=c: e.tensor_mul(XLN[:, c * S:(c + 1) * S],
                                                 X[:, c * S:(c + 1) * S], X[:, c * S:(c + 1) * S]),
                     wait=(["tensor"] if c == 4 else []), sig=True)
                sq_rec[c] = ("#V", P.count("#V"))
        if part == "sq":
            return
        for qb in range(NQB):
            pv = PS[2 + qb % 2]
            q0, q1 = qb * QB, (qb + 1) * QB
            for c in range(NCH):
                P.op("tensor",
                     lambda e, pv=pv, c=c, qb=qb: e.matmul(
                         pv[:], ONESC[:], XLN[:, c * S + qb * QB:c * S + qb * QB + QB],
                         start=(c == 0), stop=(c == NCH - 1)),
                     wait=([("abs",) + sq_rec[c]] if qb == 0 else []) +
                          ([("vector", 2)] if qb >= 2 and c == 0 else []),
                     sig=("vector" if c == NCH - 1 else None))
            P.op("vector", lambda e, q0=q0, q1=q1: e.tensor_mul(
                SCRN[:, 0:512], MSB[:, q0:q1], MSB[:, q0:q1]))
            P.op("vector",
                 lambda e, pv=pv: e.scalar_tensor_tensor(
                     SCRN[:, 512:1024], pv[:], 1.0 / D, SCRN[:, 0:512],
                     ALU.mult, ALU.subtract),
                 wait=["tensor"], sig=["tensor", "scalar"])
            P.op("scalar", lambda e: e.activation(SCRN[:, 0:512], SCRN[:, 512:1024],
                                                  AF.Ln, bias=EPSC[:]),
                 wait=["vector"])
            P.op("scalar", lambda e, q0=q0, q1=q1: e.activation(
                SB[:, q0:q1], SCRN[:, 0:512], AF.Exp, scale=-0.5), sig="vector")
            P.op("vector", lambda e, q0=q0, q1=q1: e.tensor_mul(
                MSB[:, q0:q1], MSB[:, q0:q1], SB[:, q0:q1]),
                wait=["scalar"], sig="gpsimd")
        if part == "stats":
            return
        P.wait("gpsimd", ["vector"])
        pi = 0
        for c in range(NCH):
            for hh in range(4):
                o = c * S + hh * 512
                so = hh * 512 if False else (hh % 4) * 512
                so = (hh * 512) % 2048
                sl = (pi % 2) * 512
                P.op("gpsimd",
                     lambda e, o=o, so=so, sl=sl: e.tensor_mul(
                         SCRN[:, sl:sl + 512], X[:, o:o + 512], SB[:, so:so + 512]),
                     wait=([("vector", 1)] if pi >= 2 else []))
                P.op("vector",
                     lambda e, o=o, so=so, sl=sl: e.tensor_sub(
                         XLN[:, o:o + 512], SCRN[:, sl:sl + 512], MSB[:, so:so + 512]),
                     wait=["#G"],
                     sig=["gpsimd"] + (["tensor", "scalar"] if pi == 31 else []))
                pi += 1

    def emit_routing(tag):
        P.barrier()
        co = wpush(dr["npk"], ncols=408)
        for qb in range(NQB):
            pb = PS[4]
            for c in range(NCH):
                P.op("tensor",
                     lambda e, pb=pb, c=c, qb=qb, co=co: e.matmul(
                         pb[0:45, :], WB16[0:128, co + c * 45:co + c * 45 + 45],
                         XLN[:, c * S + qb * QB:c * S + qb * QB + QB],
                         start=(c == 0), stop=False),
                     wait=(["vector"] if qb == 0 and c == 0 else []) +
                          (["scalar"] if qb > 0 and c == 0 else []))
            P.op("tensor",
                 lambda e, pb=pb, co=co: e.matmul(
                     pb[0:45, :], WB16[0:1, co + 360:co + 405], ONES1[:],
                     start=False, stop=True),
                 sig="scalar")
            P.op("scalar",
                 lambda e, pb=pb, qb=qb: e.copy(SFM[0:45, qb * QB:(qb + 1) * QB], pb[0:45, :]),
                 wait=["tensor"], sig=["tensor"] + (["gpsimd"] if qb == NQB - 1 else []))
        for tt in range(NTT):
            pb = PS[5]
            P.op("tensor",
                 lambda e, pb=pb, tt=tt: e.transpose(
                     pb[0:128, 0:45], SFM[0:45, tt * 128:(tt + 1) * 128],
                     I128F[0:45, 0:45]),
                 wait=["scalar"], sig="scalar")
            P.op("scalar",
                 lambda e, pb=pb, tt=tt: e.copy(SALL[:, tt * 45:tt * 45 + 45], pb[0:128, 0:45]),
                 wait=["tensor"], sig=["tensor"] + (["vector"] if tt == NTT - 1 else []))

        def scol(col):
            return SALL[:, col:col + 45 * 15 + 1:45]

        def mcol(i):
            return MUXT[:, i * 16:(i + 1) * 16]

        for ti, t in enumerate(("q", "k", "v")):
            b0, b1, b2, b3 = mcol(0), mcol(1), mcol(2), mcol(3)
            t0, t1, t2, t3 = mcol(4), mcol(5), mcol(6), mcol(7)
            u0, u1 = mcol(8), mcol(9)
            base = ti * 15

            def V(fn, **kw):
                P.op("vector", fn, **kw)

            def mux(dst, a, b_, sel):
                V(lambda e, dst=dst, a=a, b_=b_: e.tensor_sub(dst, b_, a))
                V(lambda e, dst=dst, sel=sel: e.tensor_mul(dst, dst, sel))
                V(lambda e, dst=dst, a=a: e.tensor_add(dst, dst, a))

            V(lambda e, base=base, b0=b0: e.tensor_single_scalar(b0, scol(base + 0), 0.0, ALU.is_ge),
              wait=(["scalar"] if ti == 0 else []))
            mux(t0, scol(base + 1), scol(base + 2), b0)
            V(lambda e, b1=b1, t0=t0: e.tensor_single_scalar(b1, t0, 0.0, ALU.is_ge))
            mux(t0, scol(base + 3), scol(base + 4), b1)
            mux(t1, scol(base + 5), scol(base + 6), b1)
            mux(t2, t0, t1, b0)
            V(lambda e, b2=b2, t2=t2: e.tensor_single_scalar(b2, t2, 0.0, ALU.is_ge))
            mux(t0, scol(base + 7), scol(base + 8), b2)
            mux(t1, scol(base + 9), scol(base + 10), b2)
            mux(t2, scol(base + 11), scol(base + 12), b2)
            mux(t3, scol(base + 13), scol(base + 14), b2)
            mux(u0, t0, t1, b1)
            mux(u1, t2, t3, b1)
            mux(t0, u0, u1, b0)
            V(lambda e, b3=b3, t0=t0: e.tensor_single_scalar(b3, t0, 0.0, ALU.is_ge))
            V(lambda e, b0=b0, b1=b1: e.scalar_tensor_tensor(LTF[:], b0, 2.0, b1, ALU.mult, ALU.add))
            V(lambda e, b2=b2: e.scalar_tensor_tensor(LTF[:], LTF[:], 2.0, b2, ALU.mult, ALU.add))
            V(lambda e, b3=b3, t=t: e.scalar_tensor_tensor(LTFS[t][:], LTF[:], 2.0, b3, ALU.mult, ALU.add))
            V(lambda e, t=t: e.tensor_copy(LT16[t][:], LTFS[t][:]),
              sig=(["tensor", "scalar", "gpsimd"] if t == "v" else None))

    def emit_tree(tree, dst, tag, scale=1.0):
        P.barrier()
        for th in range(2):
            P.op("tensor",
                 lambda e, th=th, tree=tree: e.matmul(
                     PS[6][0:8, 0:128], LT16[tree][:, th * 8:th * 8 + 8], I128B[:],
                     start=True, stop=True),
                 wait=(["vector", "scalar", "#G"] if th == 1 else []),
                 sig="scalar")
            lb_state = {}

            def bstep(i):
                if i == 0:
                    P.op("scalar", lambda e: e.copy(LEAFT[:], PS[6][0:8, 0:128]),
                         wait=["tensor"], sig="tensor")
                elif i in (1, 2):
                    g = i - 1
                    for j in range(4):
                        tt8 = g * 4 + j
                        P.op("tensor",
                             lambda e, tt8=tt8, j=j: e.matmul(
                                 PS[5][:, j * 128:(j + 1) * 128],
                                 I128B[0:8, tt8:tt8 + 1].broadcast_to([8, 128]), LEAFT[:],
                                 start=True, stop=True),
                             wait=(["scalar"] if j == 0 else []),
                             sig=("scalar" if j == 3 else None))
                    P.op("scalar",
                         lambda e, g=g: e.copy(LEAFB[:, g * 512:(g + 1) * 512], PS[5][:]),
                         wait=["tensor"], sig=["tensor", "vector"])
                    if g == 1:
                        lb_state["done"] = P.count("#S")

            def masks(l):
                P.op("vector",
                     lambda e, l=l: e.tensor_single_scalar(MSK[:], LEAFB[:], C2[:, l:l + 1],
                                                           ALU.is_equal),
                     wait=[("abs", "#S", lb_state["done"])])
                P.op("vector",
                     lambda e, l=l: e.tensor_mul(PBUF[:, l * 1024:(l + 1) * 1024],
                                                 PBUF[:, l * 1024:(l + 1) * 1024], MSK[:]),
                     wait=["scalar"], sig=["tensor"])

            wcols = {}
            wa = wpush(dr[f"{tree}w1"][0, :, 0:512])
            wb = wpush(dr[f"{tree}w1"][0, :, 512:1024])
            wcols[0] = (wa, wb, P.count("#V"))
            for l in range(NCH):
                if l + 1 < NCH:
                    wa = wpush(dr[f"{tree}w1"][l + 1, :, 0:512])
                    wb = wpush(dr[f"{tree}w1"][l + 1, :, 512:1024])
                    wcols[l + 1] = (wa, wb, P.count("#V"))
                wa, wb, vw1 = wcols[l]
                for qh in range(2):
                    qb = th * 2 + qh
                    gi = (2 * l + qh) % 4
                    pb = PS[gi]
                    for c in range(NCH):
                        colof = (wa if c < 4 else wb) + (c % 4) * 128
                        P.op("tensor",
                             lambda e, pb=pb, c=c, qb=qb, colof=colof: e.matmul(
                                 pb[:], WB16[:, colof:colof + 128],
                                 XLN[:, c * S + qb * QB:c * S + qb * QB + QB],
                                 start=(c == 0), stop=(c == NCH - 1)),
                             wait=([("abs", "#V", vw1)] if c == 0 else []) +
                                  ([("scalar", 3)] if c == 0 else []),
                             sig=("scalar" if c == NCH - 1 else None))
                    P.op("scalar",
                         lambda e, pb=pb, l=l, qh=qh, tree=tree: e.activation(
                             PBUF[:, l * 1024 + qh * 512:l * 1024 + qh * 512 + 512],
                             pb[:], AF.Relu, bias=B1T[tree][:, l:l + 1]),
                         wait=["tensor"] + (["#mc"] if l == 0 and qh == 0 and th == 0 else []),
                         sig=["tensor"] + (["vector"] if qh == 1 else []))
                    if l == 0:
                        bstep(qh)
                if l == 1:
                    bstep(2)
                    masks(0)
                    masks(1)
                elif l >= 2:
                    masks(l)
            P.op("vector",
                 lambda e: e.tensor_single_scalar(E16H[:, 0:1024], LEAFB[0:16, :],
                                                  PIDX[0:16, 0:1], ALU.is_equal),
                 sig="tensor")
            e16v = P.count("#V")
            for half in range(2):
                wpush(dr[f"{tree}b2"][:, half * 512:half * 512 + 512],
                      dst16=(B2BF, half * 512, 16), extra_cast_wait=["scalar"])
            if dst == "VT":
                for g in range(4):
                    slots = []
                    for lcg in range(2):
                        for lcq in range(4):
                            lc = lcg * 4 + lcq
                            u = wpush(dr["vw2"][lc, :, g * 256:(g + 1) * 256], ncols=256, colw=256)
                            slots.append(u)
                    vwv2 = P.count("#V")
                    for lcg in range(2):
                        for tt8 in range(8):
                            tt = th * 8 + tt8
                            pb = PS[tt8]
                            for lcq in range(4):
                                lc = lcg * 4 + lcq
                                P.op("tensor",
                                     lambda e, pb=pb, lc=lc, tt8=tt8, g=g,
                                            cu=slots[lc]: e.matmul(
                                         pb[:, 0:256],
                                         PBUF[:, lc * 1024 + tt8 * 128:lc * 1024 + tt8 * 128 + 128],
                                         WB16[:, cu:cu + 256],
                                         start=(lc == 0), stop=False),
                                     wait=([("abs", "#V", vwv2)] if lc == 0 and tt8 == 0 and lcg == 0 else []) +
                                          (["vector"] if lc == 0 and tt8 == 0 and lcg == 0 and g == 0 else []) +
                                          (["scalar"] if lc == 0 and lcg == 0 and tt8 == 0 and g > 0 else []))
                            if lcg == 1:
                                P.op("tensor",
                                     lambda e, pb=pb, tt8=tt8, g=g: e.matmul(
                                         pb[:, 0:256],
                                         E16H[:, tt8 * 128:tt8 * 128 + 128],
                                         B2BF[:, g * 256:(g + 1) * 256],
                                         start=False, stop=True),
                                     wait=([("abs", "#V", e16v)] if tt8 == 0 and g == 0 else []),
                                     sig="scalar")
                                P.op("scalar",
                                     lambda e, pb=pb, tt=tt, g=g: e.activation(
                                         VT[:, tt * 1040 + g * 256:tt * 1040 + g * 256 + 256],
                                         pb[:, 0:256], AF.Copy, scale=scale),
                                     wait=["tensor"],
                                     sig=["tensor"] + (["gpsimd"] if tt8 >= 6 else []))
            else:
                dtens = KF if dst == "KF" else XLN
                w2cols = {}
                wa = wpush(dr[f"{tree}w2"][0, :, 0:512])
                wb = wpush(dr[f"{tree}w2"][0, :, 512:1024])
                w2cols[0] = (wa, wb, P.count("#V"))
                for oc in range(NCH):
                    if oc + 1 < NCH:
                        wa = wpush(dr[f"{tree}w2"][oc + 1, :, 0:512])
                        wb = wpush(dr[f"{tree}w2"][oc + 1, :, 512:1024])
                        w2cols[oc + 1] = (wa, wb, P.count("#V"))
                    wa, wb, vw2 = w2cols[oc]
                    for qh in range(2):
                        qb = th * 2 + qh
                        pb = PS[(2 * oc + qh) % 4]
                        for lc in range(NCH):
                            colof = (wa if lc < 4 else wb) + (lc % 4) * 128
                            P.op("tensor",
                                 lambda e, pb=pb, lc=lc, qh=qh, colof=colof: e.matmul(
                                     pb[:], WB16[:, colof:colof + 128],
                                     PBUF[:, lc * 1024 + qh * 512:lc * 1024 + qh * 512 + 512],
                                     start=(lc == 0), stop=False),
                                 wait=([("abs", "#V", vw2)] if lc == 0 else []) +
                                      ([("scalar", 3)] if lc == 0 else []))
                        P.op("tensor",
                             lambda e, pb=pb, oc=oc, qh=qh: e.matmul(
                                 pb[:], B2BF[:, oc * 128:(oc + 1) * 128],
                                 E16H[:, qh * 512:qh * 512 + 512],
                                 start=False, stop=True),
                             wait=([("abs", "#V", e16v)] if oc == 0 and qh == 0 else []),
                             sig="scalar")
                        P.op("scalar",
                             lambda e, pb=pb, oc=oc, qb=qb, dtens=dtens: e.activation(
                                 dtens[:, oc * S + qb * QB:oc * S + qb * QB + QB],
                                 pb[:], AF.Copy, scale=scale),
                             wait=["tensor"],
                             sig=["tensor"] + (["gpsimd"] if qh == 1 else []))

    def emit_fff_attn(tag, emit_output=False):
        P.barrier()
        for wi, TS in ((0, XLN), (1, KF)):
            for qb in range(NQB):
                for c in range(NCH):
                    P.op("vector",
                         lambda e, TS=TS, c=c, qb=qb: e.tensor_mul(
                             PBUF[:, (c % 4) * 512:(c % 4) * 512 + 512],
                             TS[:, c * S + qb * QB:c * S + qb * QB + QB],
                             TS[:, c * S + qb * QB:c * S + qb * QB + QB]),
                         wait=([("tensor", 3)] if (c >= 4 or qb > 0 or wi > 0) else []),
                         sig="tensor")
                    P.op("tensor",
                         lambda e, c=c: e.matmul(
                             PS[4][0:1, :], ONESC[:, 0:1],
                             PBUF[:, (c % 4) * 512:(c % 4) * 512 + 512],
                             start=(c == 0), stop=(c == NCH - 1)),
                         wait=["vector"], sig=True)
                P.op("vector",
                     lambda e: e.reduce_max(STAT[0:1, 1:2], PS[4][0:1, :], mybir.AxisListType.X),
                     wait=["tensor"], sig="tensor")
                if qb == 0:
                    P.op("vector", lambda e, wi=wi: e.tensor_copy(
                        STAT[0:1, 3 + wi:4 + wi], STAT[0:1, 1:2]))
                else:
                    P.op("vector", lambda e, wi=wi: e.tensor_max(
                        STAT[0:1, 3 + wi:4 + wi], STAT[0:1, 3 + wi:4 + wi], STAT[0:1, 1:2]))
        P.op("vector", lambda e: e.tensor_mul(STAT[0:1, 5:6], STAT[0:1, 3:4], STAT[0:1, 4:5]),
             sig="scalar")
        P.op("scalar", lambda e: e.activation(STAT[0:1, 6:7], STAT[0:1, 5:6], AF.Ln),
             wait=["vector"])
        P.op("scalar", lambda e: e.activation(STAT[0:1, 7:8], STAT[0:1, 6:7], AF.Exp, scale=0.5),
             sig="vector")
        P.op("vector", lambda e: e.tensor_single_scalar(STAT[0:1, 8:9], STAT[0:1, 7:8], -1.0, ALU.mult),
             wait=["scalar"])
        P.op("vector", lambda e: e.tensor_copy(ETOK1[0:1, 0:1], STAT[0:1, 8:9]), sig="tensor")
        P.op("tensor", lambda e: e.matmul(PS[5][:, 0:1], ONESC[0:1, :], ETOK1[0:1, 0:1],
                                          start=True, stop=True),
             wait=["vector"], sig="scalar")
        P.op("scalar", lambda e: e.copy(BN[:], PS[5][:, 0:1]), wait=["tensor"],
             sig=["tensor", "vector"])

        for qb in range(NQB):
            for kt in range(NTT):
                pb = PS[kt % 3]
                for c in range(NCH):
                    P.op("tensor",
                         lambda e, pb=pb, c=c, kt=kt, qb=qb: e.matmul(
                             pb[:], KF[:, c * S + kt * 128:c * S + kt * 128 + 128],
                             XLN[:, c * S + qb * QB:c * S + qb * QB + QB],
                             start=(c == 0), stop=(c == NCH - 1)),
                         wait=([("scalar", 2)] if c == 0 and kt >= 3 else []) +
                              (["vector", "scalar"] if c == 0 and kt == 0 and qb == 0 else []) +
                              (["vector"] if c == 0 and kt == 0 and qb > 0 else []),
                         sig=("scalar" if c == NCH - 1 else None))
                if kt > 0:
                    P.op("tensor",
                         lambda e, kt=kt: e.matmul(
                             PS[4][0:1, :], ONESC[:, 0:1],
                             PBUF[:, (kt - 1) * 512:kt * 512],
                             start=(kt - 1 == 0), stop=False),
                         wait=["scalar"])
                P.op("scalar",
                     lambda e, pb=pb, kt=kt: e.activation(
                         PBUF[:, kt * 512:(kt + 1) * 512], pb[:], AF.Exp, bias=BN[:]),
                     wait=["tensor"], sig="tensor")
            P.op("tensor",
                 lambda e: e.matmul(PS[4][0:1, :], ONESC[:, 0:1],
                                    PBUF[:, (NTT - 1) * 512:NTT * 512],
                                    start=False, stop=True),
                 wait=["scalar"], sig=True)
            P.op("vector", lambda e: e.reciprocal(RR32F[:], PS[4][0:1, :]),
                 wait=[("tensor", 0), "#G"], sig="tensor")
            P.op("vector", lambda e: e.tensor_copy(RROWF[:], RR32F[:]), sig="tensor")
            P.op("tensor", lambda e: e.matmul(PS[5][:], ONES1[0:1, 0:128], RROWF[:],
                                              start=True, stop=True),
                 wait=["vector", ("scalar", 0)], sig="scalar")
            P.op("scalar", lambda e: e.copy(RBF[:], PS[5][:]), wait=["tensor"],
                 sig=["tensor", "vector"])
            for dc in range(NCH):
                pb = PS[3 if dc % 2 == 0 else 7]
                for kt in range(NTT):
                    P.op("tensor",
                         lambda e, pb=pb, dc=dc, kt=kt: e.matmul(
                             pb[:], VT[:, kt * 1040 + dc * 128:kt * 1040 + dc * 128 + 128],
                             PBUF[:, kt * 512:(kt + 1) * 512],
                             start=(kt == 0), stop=(kt == NTT - 1)),
                         wait=([("vector", 2)] if kt == 0 and dc >= 2 else []),
                         sig=("vector" if kt == NTT - 1 else None))
                o = dc * S + qb * QB
                P.op("vector",
                     lambda e, pb=pb: e.tensor_mul(TMPF[:], pb[:], RBF[:]),
                     wait=["tensor", "#G"] + (["scalar"] if dc == 0 else []),
                     sig=["tensor", "gpsimd"])
                P.op("gpsimd",
                     lambda e, o=o: e.tensor_add(X[:, o:o + 512], X[:, o:o + 512], TMPF[:]),
                     wait=["vector"],
                     sig=["vector"] + (["tensor"] if emit_output else []))
                if emit_output:
                    for j in range(4):
                        P.op("tensor",
                             lambda e, o=o, j=j: e.transpose(
                                 PS[6][:, 0:128], X[:, o + j * 128:o + j * 128 + 128], I128F[:]),
                             wait=["#G"] + (["scalar"] if (dc, j) > (0, 0) or qb > 0 else []),
                             sig="scalar")
                        P.op("scalar",
                             lambda e, j=j: e.copy(OSTG[:, (j % 2) * 128:(j % 2) * 128 + 128],
                                                   PS[6][:, 0:128]),
                             wait=["tensor", f"#o{j % 2}"], sig=["tensor", "sync"])
                        P.dma(y[qb * QB + j * 128:qb * QB + j * 128 + 128,
                                dc * 128:dc * 128 + 128],
                              OSTG[:, (j % 2) * 128:(j % 2) * 128 + 128], f"o{j % 2}",
                              wait=["scalar"])

    def emit_mha(tag):
        P.barrier()
        kcols = {}
        wa = wpush(dr["kw"][0, :, 0:512])
        wb = wpush(dr["kw"][0, :, 512:1024])
        kcols[0] = (wa, wb, P.count("#V"))
        for oc in range(NCH):
            if oc + 1 < NCH:
                wa = wpush(dr["kw"][oc + 1, :, 0:512])
                wb = wpush(dr["kw"][oc + 1, :, 512:1024])
                kcols[oc + 1] = (wa, wb, P.count("#V"))
            wa, wb, vwk = kcols[oc]
            for qb in range(NQB):
                pb = PS[(4 * oc + qb) % 4]
                for mc in range(NCH):
                    colof = (wa if mc < 4 else wb) + (mc % 4) * 128
                    P.op("tensor",
                         lambda e, pb=pb, mc=mc, qb=qb, colof=colof: e.matmul(
                             pb[:], WB16[:, colof:colof + 128],
                             XLN[:, mc * S + qb * QB:mc * S + qb * QB + QB],
                             start=(mc == 0), stop=(mc == NCH - 1)),
                         wait=([("abs", "#V", vwk)] if mc == 0 and qb == 0 else []) +
                              ([("scalar", 3)] if mc == 0 else []),
                         sig=("scalar" if mc == NCH - 1 else None))
                P.op("scalar",
                     lambda e, pb=pb, oc=oc, qb=qb: e.add(
                         KF[:, oc * S + qb * QB:oc * S + qb * QB + QB], pb[:],
                         KB8[:, oc:oc + 1]),
                     wait=["tensor"],
                     sig=["tensor"] + (["gpsimd"] if qb == NQB - 1 else []))
        P.op("gpsimd", lambda e: e.memset(
            VT[:].rearrange("p (t h) -> p t h", h=1040)[:, :, 64:1040:65], 1.0),
            wait=["tensor", "scalar"], sig="tensor")
        for half in range(2):
            for mc in range(NCH):
                wpush(dr["vw"][mc, :, half * 512:half * 512 + 512],
                      dst16=(VWA, mc * 512, 128),
                      extra_cast_wait=(["tensor"] if half == 1 else []))
            wpush(dr["vbrow"][0:1, half * 512:half * 512 + 512],
                  dst16=(VWA, 4096, 1))
            vwv = P.count("#V")
            for tt in range(NTT):
                pb = PS[tt % 4]
                for mc in range(NCH):
                    P.op("tensor",
                         lambda e, pb=pb, mc=mc, tt=tt: e.matmul(
                             pb[:], XLN[:, mc * S + tt * 128:mc * S + tt * 128 + 128],
                             VWA[:, mc * 512:(mc + 1) * 512],
                             start=(mc == 0), stop=False),
                         wait=([("abs", "#V", vwv)] if mc == 0 and tt == 0 else []) +
                              ([("scalar", 3)] if mc == 0 and tt > 0 else []))
                P.op("tensor",
                     lambda e, pb=pb: e.matmul(pb[:], ONES1[0:1, 0:128], VWA[0:1, 4096:4608],
                                               start=False, stop=True),
                     sig="scalar")
                P.op("scalar",
                     lambda e, pb=pb, tt=tt, half=half: e.copy(
                         VT[:].rearrange("p (t h) -> p t h", h=1040)
                         [:, tt, half * 520:half * 520 + 520]
                         .rearrange("p (g c) -> p g c", c=65)[:, :, 0:64],
                         pb[:]),
                     wait=["tensor"],
                     sig=["tensor"] + (["gpsimd"] if tt >= NTT - 2 else []))
        for qb in range(NQB):
            qev = []
            qcols = {}
            for j in range(2):
                wa = wpush(dr["qw"][j, :, 0:512])
                wb = wpush(dr["qw"][j, :, 512:1024])
                qcols[j] = (wa, wb, P.count("#V"))
            for oc in range(NCH):
                if oc + 2 < NCH:
                    wa = wpush(dr["qw"][oc + 2, :, 0:512])
                    wb = wpush(dr["qw"][oc + 2, :, 512:1024])
                    qcols[oc + 2] = (wa, wb, P.count("#V"))
                wa, wb, vwq = qcols[oc]
                pb = PS[oc % 4]
                for mc in range(NCH):
                    colof = (wa if mc < 4 else wb) + (mc % 4) * 128
                    P.op("tensor",
                         lambda e, pb=pb, mc=mc, qb=qb, colof=colof: e.matmul(
                             pb[:], WB16[:, colof:colof + 128],
                             XLN[:, mc * S + qb * QB:mc * S + qb * QB + QB],
                             start=(mc == 0), stop=(mc == NCH - 1)),
                         wait=([("abs", "#V", vwq)] if mc == 0 else []) +
                              ([("abs", "#V", qev[oc - 4])] if mc == 0 and oc >= 4 else []) +
                              (["vector"] if mc == 0 and oc == 0 and qb > 0 else []),
                         sig=("vector" if mc == NCH - 1 else None))
                P.op("vector",
                     lambda e, pb=pb, oc=oc: e.tensor_scalar(
                         QSTG[:, oc * 512:(oc + 1) * 512], pb[:],
                         0.125, QB8S[:, oc:oc + 1], ALU.mult, ALU.add),
                     wait=["tensor"], sig=True)
                qev.append(P.count("#V"))
            for oc in range(NCH):
                P.op("vector",
                     lambda e, oc=oc, qb=qb: e.tensor_copy(
                         XLN[:, oc * S + qb * QB:oc * S + qb * QB + QB],
                         QSTG[:, oc * 512:(oc + 1) * 512]),
                     sig=(["tensor"] if oc == NCH - 1 else None))
        for wi, TS in ((0, XLN), (1, KF)):
            for qb in range(NQB):
                for c in range(NCH):
                    P.op("vector",
                         lambda e, TS=TS, c=c, qb=qb: e.tensor_mul(
                             PBUF[:, (c % 4) * 512:(c % 4) * 512 + 512],
                             TS[:, c * S + qb * QB:c * S + qb * QB + QB],
                             TS[:, c * S + qb * QB:c * S + qb * QB + QB]),
                         wait=([("tensor", 3)] if (c >= 4 or qb > 0 or wi > 0) else []) +
                              (["scalar"] if wi == 0 and qb == 0 and c == 0 else []),
                         sig="tensor")
                    P.op("tensor",
                         lambda e, c=c: e.matmul(
                             PS[4][0:16, :], BO16[:, c * 16:c * 16 + 16],
                             PBUF[:, (c % 4) * 512:(c % 4) * 512 + 512],
                             start=(c == 0), stop=(c == NCH - 1)),
                         wait=["vector"], sig=True)
                P.op("vector",
                     lambda e: e.reduce_max(STAT[0:16, 1:2], PS[4][0:16, :], mybir.AxisListType.X),
                     wait=["tensor"], sig="tensor")
                if qb == 0:
                    P.op("vector", lambda e, wi=wi: e.tensor_copy(
                        STAT[0:16, 4 + wi:5 + wi], STAT[0:16, 1:2]))
                else:
                    P.op("vector", lambda e, wi=wi: e.tensor_max(
                        STAT[0:16, 4 + wi:5 + wi], STAT[0:16, 4 + wi:5 + wi], STAT[0:16, 1:2]))
        P.op("vector", lambda e: e.tensor_mul(STAT[0:16, 6:7], STAT[0:16, 4:5], STAT[0:16, 5:6]),
             sig="scalar")
        P.op("scalar", lambda e: e.activation(STAT[0:16, 7:8], STAT[0:16, 6:7], AF.Ln),
             wait=["vector"])
        P.op("scalar", lambda e: e.activation(STAT[0:16, 8:9], STAT[0:16, 7:8], AF.Exp, scale=0.5),
             sig="vector")
        P.op("vector", lambda e: e.tensor_single_scalar(STAT[0:16, 9:10], STAT[0:16, 8:9],
                                                        -1.0, ALU.mult),
             wait=["scalar"])
        P.op("vector", lambda e: e.tensor_copy(BH16[:], STAT[0:16, 9:10]), sig="tensor")
        for h in range(NH):
            P.op("tensor",
                 lambda e, h=h: e.matmul(PS[5][:, h:h + 1],
                                         I128B[0:16, h:h + 1].broadcast_to([16, 128]),
                                         BH16[:], start=True, stop=True),
                 wait=(["vector"] if h == 0 else []),
                 sig=("scalar" if h == NH - 1 else None))
        P.op("scalar", lambda e: e.copy(BH[:], PS[5][:, 0:16]), wait=["tensor"],
             sig=["tensor", "vector"])

        P.wait("tensor", ["vector", "scalar", "#G"])
        sslot = [0]
        bank_rec = {0: None, 1: None, 2: None, 3: None}
        prslot = [0]
        pr_rec = {i: 0 for i in range(8)}
        PVSET = ((4, 6), (5, 7))
        mult_rec = {0: 0, 1: 0}
        pend = []

        def claim_banks(n):
            if n == 2 and sslot[0] % 2:
                sslot[0] += 1
            b0 = sslot[0] % 4
            sslot[0] += n
            return [b0, b0 + 1] if n == 2 else [b0]

        def claim_pring(n):
            if n == 2 and prslot[0] % 2:
                prslot[0] += 1
            s0 = prslot[0] % 8
            prslot[0] += n
            return [s0, s0 + 1] if n == 2 else [s0]

        def flush_norm():
            for h, pvb, rt, setf, vcnt in pend:
                hc, hp = h // 2, (h % 2) * 64
                b = claim_banks(1)[0]
                w = [("abs", "#V", vcnt)]
                if bank_rec[b] is not None:
                    w.append(("abs",) + bank_rec[b])
                P.op("tensor",
                     lambda e, b=b, rt=rt: e.matmul(
                         PS[b][0:64, :], ONES1[0:1, 0:64], rt[:],
                         start=True, stop=True),
                     wait=w, sig=True)
                P.op("scalar",
                     lambda e, pvb=pvb, hc=hc, hp=hp: e.copy(
                         ATQ[hp:hp + 64, hc * 512:(hc + 1) * 512], pvb[0:64, :]),
                     wait=["tensor"])
                P.op("vector",
                     lambda e, b=b, hc=hc, hp=hp: e.tensor_mul(
                         ATQ[hp:hp + 64, hc * 512:(hc + 1) * 512],
                         ATQ[hp:hp + 64, hc * 512:(hc + 1) * 512], PS[b][0:64, :]),
                     wait=["scalar"], sig="tensor")
                bank_rec[b] = ("#V", P.count("#V"))
                mult_rec[setf] = P.count("#V")
            del pend[:]

        for qb in range(NQB):
            for hp2 in range(NH // 2):
                pidx = qb * (NH // 2) + hp2
                myset = pidx % 2
                h0, h1 = 2 * hp2, 2 * hp2 + 1
                chunks = []
                for k2 in range(0, 4 * qb, 2):
                    chunks.append(([k2, k2 + 1], [0, 0]))
                for j in range(4):
                    chunks.append(([4 * qb + j], [128 * j]))
                nch_ = len(chunks)

                def stile(h, kts, c0s, qb=qb):
                    hc, hp = h // 2, (h % 2) * 64
                    banks = claim_banks(len(kts))
                    slots = claim_pring(len(kts))
                    for kt, c0, b in zip(kts, c0s, banks):
                        w = []
                        if bank_rec[b] is not None:
                            w.append(("abs",) + bank_rec[b])
                        P.op("tensor",
                             lambda e, b=b, hc=hc, hp=hp, kt=kt, qb=qb, c0=c0: e.matmul(
                                 PS[b][:, c0:512],
                                 KF[hp:hp + 64, hc * S + kt * 128:hc * S + kt * 128 + 128],
                                 XLN[hp:hp + 64, hc * S + qb * QB + c0:hc * S + qb * QB + QB],
                                 start=True, stop=True),
                             wait=w, sig=(True if kt == kts[-1] else None))
                    prmax = max(pr_rec[s] for s in slots)
                    prw = [("abs", "tensor", prmax)] if prmax > 0 else []
                    if len(kts) == 2:
                        b0, s0 = banks[0], slots[0]
                        P.op("scalar",
                             lambda e, b0=b0, s0=s0, h=h: e.activation(
                                 PRING[:, s0 * 512:s0 * 512 + 1024],
                                 PSALL[:, b0 * 512:b0 * 512 + 1024],
                                 AF.Exp, bias=BH[:, h:h + 1]),
                             wait=["tensor"] + prw)
                        bank_rec[banks[0]] = bank_rec[banks[1]] = ("#S", P.count("#S"))
                        tri = 0
                    else:
                        b, sl, c0 = banks[0], slots[0], c0s[0]
                        P.op("scalar",
                             lambda e, b=b, sl=sl, c0=c0, h=h: e.activation(
                                 PRING[:, sl * 512 + c0:sl * 512 + 512], PS[b][:, c0:512],
                                 AF.Exp, bias=BH[:, h:h + 1]),
                             wait=["tensor"] + prw)
                        bank_rec[b] = ("#S", P.count("#S"))
                        P.op("gpsimd",
                             lambda e, sl=sl, c0=c0: e.tensor_mul(
                                 PRING[:, sl * 512 + c0:sl * 512 + c0 + 128],
                                 PRING[:, sl * 512 + c0:sl * 512 + c0 + 128], TRI[:]),
                             wait=["scalar"])
                        tri = P.count("#G")
                    return slots, P.count("#S"), tri

                def pvgrp(h, kts, c0s, slots, exp_cnt, tri_cnt, first, last,
                          myset=myset):
                    pvb = PS[PVSET[myset][h % 2]]
                    n = len(kts)
                    for i, (kt, c0, sl) in enumerate(zip(kts, c0s, slots)):
                        w = []
                        if i == 0:
                            w.append(("abs", "#S", exp_cnt))
                            if tri_cnt:
                                w.append(("abs", "#G", tri_cnt))
                            if first and mult_rec[myset] > 0:
                                w.append(("abs", "#V", mult_rec[myset]))
                        P.op("tensor",
                             lambda e, pvb=pvb, h=h, kt=kt, c0=c0, sl=sl,
                                    st=(first and i == 0),
                                    sp=(last and i == n - 1): e.matmul(
                                 pvb[0:65, c0:512],
                                 VT[:, kt * 1040 + h * 65:kt * 1040 + h * 65 + 65],
                                 PRING[:, sl * 512 + c0:sl * 512 + 512],
                                 start=st, stop=sp),
                             wait=w, sig=(True if i == n - 1 else None))
                    cnt = P.count("tensor")
                    for sl in slots:
                        pr_rec[sl] = cnt

                prev = None
                for ci, (kts, c0s) in enumerate(chunks):
                    d0 = stile(h0, kts, c0s)
                    d1 = stile(h1, kts, c0s)
                    if ci == 1:
                        flush_norm()
                    if prev is not None:
                        j, pk, pc, p0, p1 = prev
                        pvgrp(h0, pk, pc, p0[0], p0[1], p0[2], j == 0, False)
                        pvgrp(h1, pk, pc, p1[0], p1[1], p1[2], j == 0, False)
                    prev = (ci, kts, c0s, d0, d1)
                j, pk, pc, p0, p1 = prev
                pvgrp(h0, pk, pc, p0[0], p0[1], p0[2], j == 0, True)
                pvgrp(h1, pk, pc, p1[0], p1[1], p1[2], j == 0, True)
                for h in (h0, h1):
                    pvb = PS[PVSET[myset][h % 2]]
                    rt = RROWM if h % 2 == 0 else RROWN
                    P.op("vector",
                         lambda e, pvb=pvb: e.reciprocal(
                             RR32F[:], pvb[64:65, :]),
                         wait=["tensor"], sig="tensor")
                    P.op("vector", lambda e, rt=rt: e.tensor_copy(rt[:], RR32F[:]),
                         sig="tensor")
                    pend.append((h, pvb, rt, myset, P.count("#V")))
            flush_norm()
            oev = []
            ocols = {}
            for j in range(2):
                wa = wpush(dr["ow"][j, :, 0:512])
                wb = wpush(dr["ow"][j, :, 512:1024])
                ocols[j] = (wa, wb, P.count("#V"))
            for mc in range(NCH):
                if mc + 2 < NCH:
                    wa = wpush(dr["ow"][mc + 2, :, 0:512])
                    wb = wpush(dr["ow"][mc + 2, :, 512:1024])
                    ocols[mc + 2] = (wa, wb, P.count("#V"))
                wa, wb, vwo = ocols[mc]
                pb = PS[mc % 4]
                for hcc in range(NCH):
                    colof = (wa if hcc < 4 else wb) + (hcc % 4) * 128
                    P.op("tensor",
                         lambda e, pb=pb, hcc=hcc, colof=colof: e.matmul(
                             pb[:], WB16[:, colof:colof + 128], ATQ[:, hcc * 512:(hcc + 1) * 512],
                             start=(hcc == 0), stop=(hcc == NCH - 1)),
                         wait=([("abs", "#V", vwo)] if hcc == 0 else []) +
                              ([("abs", "#V", oev[mc - 4])] if hcc == 0 and mc >= 4 else []),
                         sig=("vector" if hcc == NCH - 1 else None))
                o = mc * S + qb * QB
                P.op("vector",
                     lambda e, pb=pb, o=o, mc=mc: e.scalar_tensor_tensor(
                         X[:, o:o + 512], pb[:], OB8[:, mc:mc + 1], X[:, o:o + 512],
                         ALU.add, ALU.add),
                     wait=["tensor"], sig=["tensor"])
                oev.append(P.count("#V"))
            P.wait("tensor", ["vector"])

    emit_ln("ln1")
    if dbg_dump("ln1"):
        return nc, P, dbg_outs
    emit_routing("r1")
    if dbg_dump("route1"):
        return nc, P, dbg_outs
    emit_tree("k", "KF", "t1k")
    if dbg_dump("ktree1"):
        return nc, P, dbg_outs
    emit_tree("v", "VT", "t1v", scale=0.5)
    if dbg_dump("vtree1"):
        return nc, P, dbg_outs
    emit_tree("q", "XLN", "t1q", scale=1.0 / 32.0)
    if dbg_dump("qtree1"):
        return nc, P, dbg_outs
    emit_fff_attn("a1")
    if dbg_dump("attn1"):
        return nc, P, dbg_outs
    emit_ln("ln2")
    if dbg_dump("ln2"):
        return nc, P, dbg_outs
    emit_mha("mha")
    if dbg_dump("mha"):
        return nc, P, dbg_outs
    emit_ln("ln3")
    if dbg_dump("ln3"):
        return nc, P, dbg_outs
    emit_routing("r2")
    if dbg_dump("route2"):
        return nc, P, dbg_outs
    emit_tree("k", "KF", "t2k")
    emit_tree("v", "VT", "t2v", scale=0.5)
    emit_tree("q", "XLN", "t2q", scale=1.0 / 32.0)
    if dbg_dump("trees2"):
        return nc, P, dbg_outs
    emit_fff_attn("a2", emit_output=True)
    for cn in ("o0", "o1"):
        s = P._chan(cn)
        P.q["sync"].append(lambda e, h=s[0], v=s[1]: e.wait_ge(h, v))
    dbg_dump("out")
    return nc, P, dbg_outs


def finalize(nc, P):
    with nc.Block() as block:
        @block.sync
        def _(e):
            for fn in P.q["sync"]:
                fn(e)

        @block.tensor
        def _(e):
            for fn in P.q["tensor"]:
                fn(e)

        @block.scalar
        def _(e):
            for fn in P.q["scalar"]:
                fn(e)

        @block.vector
        def _(e):
            for fn in P.q["vector"]:
                fn(e)

        @block.gpsimd
        def _(e):
            for fn in P.q["gpsimd"]:
                fn(e)
    return nc


_BUILD_CACHE = {}


def _get_program(upto=None, dbg=()):
    key = (upto, tuple(dbg))
    if key not in _BUILD_CACHE:
        nc, P, dbg_outs = build(upto=upto, dbg=dbg)
        finalize(nc, P)
        _BUILD_CACHE[key] = (nc, dbg_outs)
    return _BUILD_CACHE[key]


def run(inputs, upto=None, dbg=()):
    per_core = host_prepare(inputs)
    nc, dbg_outs = _get_program(upto=upto, dbg=dbg)
    res = run_bass_kernel_spmd(nc, per_core, core_ids=list(range(8)))
    return res, dbg_outs


def kernel(**inputs):
    res, _ = run(inputs)
    return np.stack([np.asarray(r["y"], np.float32) for r in res.results], axis=0)
